# version 10
# baseline (speedup 1.0000x reference)
import contextlib
import numpy as np
import ml_dtypes
import concourse.bass as bass
import concourse.mybir as mybir
from concourse.bass_utils import run_bass_kernel_spmd

F32 = mybir.dt.float32
BF16 = mybir.dt.bfloat16
I32 = mybir.dt.int32
AF = mybir.ActivationFunctionType
ALU = mybir.AluOpType
AX = mybir.AxisListType

D = 2048
S = 2048
HID = 5632
NHC = 44
NDC = 16
T = 512
NT = S // T
NTC = T // 128
CD = 1024
NKC = CD // 128
NH = 16
EPS = 1e-5
ISQ = float(1.0 / np.sqrt(128.0))

ENGINES = ("tensor", "vector", "scalar", "gpsimd", "sync")
DMA_SEMS = ("DX", "DX2", "DW0", "DW1", "DW2", "DW3", "DWO", "DSG", "DSB", "DR", "DC", "DT", "DI", "DF", "DG0", "DG1", "DG2", "DG3", "DTA", "DTB")
ESEM = {"tensor": "T", "vector": "V", "scalar": "S", "gpsimd": "G", "sync": "Y"}
FSEM = {"tensor": "TF", "vector": "VF", "scalar": "SF", "gpsimd": "GF", "sync": "YF"}


class Em:

    def __init__(self, nc):
        self.nc = nc
        self.q = {e: [] for e in ENGINES}
        self.cnt = {}
        self.sems = {}

    def reg_sem(self, name, handle):
        self.sems[name] = handle
        self.cnt[name] = 0

    def wait(self, eng, ev):
        if ev is None:
            return
        name, val = ev
        self.q[eng].append(("w", self.sems[name], val))

    def waits(self, eng, evs):
        for ev in evs:
            self.wait(eng, ev)

    def add(self, eng, meth, *args, inc=None, incn=1, **kw):
        ev = None
        sem = None
        if inc is not None:
            self.cnt[inc] += incn
            ev = (inc, self.cnt[inc])
            sem = self.sems[inc]
        self.q[eng].append(("o", meth, args, kw, sem, incn))
        return ev

    def dma(self, eng, qname, out, in_, waits=(), **kw):
        for ev in waits:
            self.wait(eng, ev)
        return self.add(eng, "dma_start", inc=qname, incn=16, out=out, in_=in_,
                        **kw)

    def fence(self, engines=ENGINES, dmaq=DMA_SEMS):
        vals = [(ESEM[o], self.cnt[ESEM[o]]) for o in ("tensor", "vector",
                                                       "scalar", "gpsimd")]
        dws = [(qn, self.cnt[qn]) for qn in dmaq]
        for e in engines:
            for sv in vals:
                if sv[1] > 0:
                    self.wait(e, sv)
            for dw in dws:
                if dw[1] > 0:
                    self.wait(e, dw)

    def flush(self):
        qs = {e: self.q[e] for e in ENGINES}
        self.q = {e: [] for e in ENGINES}

        def replay(eng, items):
            for it in items:
                if it[0] == "w":
                    eng.wait_ge(it[1], it[2])
                else:
                    _, meth, args, kw, sem, incn = it
                    r = getattr(eng, meth)(*args, **kw)
                    if sem is not None:
                        r.then_inc(sem, incn)

        with self.nc.Block() as block:
            @block.tensor
            def _(e):
                replay(e, qs["tensor"])

            @block.vector
            def _(e):
                replay(e, qs["vector"])

            @block.scalar
            def _(e):
                replay(e, qs["scalar"])

            @block.gpsimd
            def _(e):
                replay(e, qs["gpsimd"])

            @block.sync
            def _(e):
                replay(e, qs["sync"])


class NS:
    pass


def rar(x, pat, **kw):
    if isinstance(x, bass.AP):
        return x.rearrange(pat, **kw)
    return x.ap().rearrange(pat, **kw)



def emit_xnprep(em, b, src, tile):
    st = b.sstat
    for tc in range(NTC):
        row0 = tile * T + tc * 128
        ev_x = em.dma("sync", "DX", out=b.sxt[:], in_=src[row0:row0 + 128, :],
                      waits=b.free_sxt)
        em.wait("vector", ev_x)
        em.wait("scalar", ev_x)
        sum_ = st[:, 0:1]; sumsq = st[:, 1:2]; negm = st[:, 2:3]; msq = st[:, 3:4]
        ve = st[:, 4:5]; s0 = st[:, 5:6]; r0 = st[:, 6:7]; t1 = st[:, 7:8]
        t3 = st[:, 8:9]; rstd = st[:, 9:10]; bnr = st[:, 10:11]
        ev1 = em.add("vector", "tensor_reduce", sum_, b.sxt[:], AX.X, ALU.add, inc="V")
        em.waits("scalar", b.free_sjunk)
        ev2 = em.add("scalar", "activation", b.sjunk[:], b.sxt[:], AF.Square,
                     accum_out=sumsq, inc="S")
        em.wait("vector", ev1)
        ev3 = em.add("vector", "tensor_scalar", negm, sum_, -1.0 / D, None,
                     op0=ALU.mult, inc="V")
        em.wait("vector", ev2)
        em.wait("vector", ev3)
        ev4 = em.add("vector", "tensor_tensor", msq, negm, negm, ALU.mult, inc="V")
        ev5 = em.add("vector", "tensor_scalar", ve, sumsq, 1.0 / D, EPS,
                     op0=ALU.mult, op1=ALU.add, inc="V")
        em.wait("vector", ev4)
        em.wait("vector", ev5)
        ev6 = em.add("vector", "tensor_tensor", ve, ve, msq, ALU.subtract, inc="V")
        em.wait("scalar", ev6)
        ev7 = em.add("scalar", "activation", s0, ve, AF.Sqrt, inc="S")
        em.wait("vector", ev7)
        ev8 = em.add("vector", "reciprocal", r0, s0, inc="V")
        em.wait("vector", ev8)
        ev9 = em.add("vector", "tensor_tensor", t1, r0, r0, ALU.mult, inc="V")
        em.wait("vector", ev9)
        ev10 = em.add("vector", "tensor_tensor", t1, t1, ve, ALU.mult, inc="V")
        em.wait("vector", ev10)
        ev11 = em.add("vector", "tensor_scalar", t3, t1, -0.5, 1.5,
                      op0=ALU.mult, op1=ALU.add, inc="V")
        em.wait("vector", ev11)
        ev12 = em.add("vector", "tensor_tensor", rstd, r0, t3, ALU.mult, inc="V")
        em.wait("vector", ev12)
        ev13 = em.add("vector", "tensor_tensor", bnr, negm, rstd, ALU.mult, inc="V")
        em.wait("scalar", ev13)
        em.waits("scalar", b.free_sxtm)
        evh = em.add("scalar", "activation", b.sxtmh[:], b.sxt[:], AF.Identity,
                     bias=bnr, scale=rstd, inc="S")
        em.wait("vector", ev2)
        evj = em.add("vector", "tensor_scalar", b.sjunk[:], b.sxt[:], negm, rstd,
                     op0=ALU.add, op1=ALU.mult, inc="V")
        em.wait("vector", evh)
        em.wait("vector", evj)
        evl = em.add("vector", "tensor_tensor", b.sxtml[:], b.sjunk[:], b.sxtmh[:],
                     ALU.subtract, inc="V")
        b.free_sjunk = [evl]
        b.free_sxt = [evj, evh]
        evta = em.dma("sync", "DTA",
                      out=rar(b.sxnh, "p (c t) -> p c t", t=512)[:, :, tc * 128:(tc + 1) * 128],
                      in_=rar(b.sxtmh, "p (c f) -> p c f", f=128),
                      waits=[evh] + list(b.xn_free), transpose=True)
        evtb = em.dma("sync", "DTB",
                      out=rar(b.sxnl, "p (c t) -> p c t", t=512)[:, :, tc * 128:(tc + 1) * 128],
                      in_=rar(b.sxtml, "p (c f) -> p c f", f=128),
                      waits=[evl] + list(b.xn_free), transpose=True)
        b.free_sxtm = [evta, evtb]
        b.xn_ready = [evta, evtb]
    b.xn_free = ()



def emit_wconv(em, b, in_ap, win, win_free, cache_slab, wsem, tsem):
    ev_i = em.dma("sync", "DI", out=b.sint[:], in_=in_ap, waits=b.free_sint)
    em.wait("vector", ev_i)
    em.waits("vector", b.free_sconv)
    evc = em.add("vector", "tensor_scalar", b.sconv[:], b.sint[:], -128.0, None,
                 op0=ALU.add, inc="V")
    b.free_sint = [evc]
    ev_t = em.dma("sync", tsem,
                  out=rar(win, "p (c f) -> p c f", f=128),
                  in_=rar(b.sconv, "p (c f) -> p c f", f=128),
                  waits=list(win_free or ()) + [evc], transpose=True)
    b.free_sconv = [ev_t]
    ev_st = em.dma("gpsimd", wsem, out=cache_slab, in_=win[:, :], waits=[ev_t])
    return ev_t, ev_st



PF = 2


def emit_ffn(em, b, src_, dst, convert):
    ev_sdb = em.dma("sync", "DT", out=b.ssdb[:], in_=b.sdb_in[:, :])
    for tile in range(NT):
        if tile == 0:
            emit_xnprep(em, b, src_, 0)
        win_free = b.winf
        prev_sil = None
        prev_h32 = None
        prev_lo = None
        for hc in range(NHC):
            wg = b.win[hc % 2]
            wu = b.win[2 + hc % 2]
            if convert and tile == 0:
                ev_wg_d, st_g = emit_wconv(
                    em, b, b.wq_gate[hc * 128:(hc + 1) * 128, :], wg,
                    win_free[hc % 2], b.wgT_c[hc * 128:(hc + 1) * 128, :],
                    f"DG{hc % 2}", f"DW{hc % 2}")
                ev_wu_d, st_u = emit_wconv(
                    em, b, b.wq_up[hc * 128:(hc + 1) * 128, :], wu,
                    win_free[2 + hc % 2], b.wuT_c[hc * 128:(hc + 1) * 128, :],
                    f"DG{2 + hc % 2}", f"DW{2 + hc % 2}")
            else:
                st_g = st_u = None
                if b.pref:
                    ev_wg_d, ev_wu_d = b.pref.pop(0)
                else:
                    ev_wg_d = em.dma("sync", f"DW{hc % 2}", out=wg[:, :],
                                     in_=b.wgT_c[hc * 128:(hc + 1) * 128, :],
                                     waits=win_free[hc % 2])
                    ev_wu_d = em.dma("sync", f"DW{2 + hc % 2}", out=wu[:, :],
                                     in_=b.wuT_c[hc * 128:(hc + 1) * 128, :],
                                     waits=win_free[2 + hc % 2])
            em.wait("tensor", ev_wg_d)
            if hc == 0:
                em.waits("tensor", b.xn_ready)
                em.waits("tensor", b.ps_free)
            if prev_sil is not None:
                em.wait("tensor", prev_sil)
            ev_g = None
            for pi, xn in enumerate((b.sxnh, b.sxnl)):
                for dc in range(NDC):
                    last = (pi == 1 and dc == NDC - 1)
                    ev_g = em.add("tensor", "matmul", b.pA[:, :],
                                  wg[:, dc * 128:(dc + 1) * 128],
                                  xn[:, dc * 512:(dc + 1) * 512],
                                  start=(pi == 0 and dc == 0), stop=last,
                                  inc=("T" if last else None))
            em.wait("tensor", ev_wu_d)
            if prev_h32 is not None:
                em.wait("tensor", prev_h32)
            ev_u = None
            for pi, xn in enumerate((b.sxnh, b.sxnl)):
                for dc in range(NDC):
                    last = (pi == 1 and dc == NDC - 1)
                    ev_u = em.add("tensor", "matmul", b.pB[:, :],
                                  wu[:, dc * 128:(dc + 1) * 128],
                                  xn[:, dc * 512:(dc + 1) * 512],
                                  start=(pi == 0 and dc == 0), stop=last,
                                  inc=("T" if last else None))
            em.wait("scalar", ev_g)
            if prev_h32 is not None:
                em.wait("scalar", prev_h32)
            evs = em.add("scalar", "activation", b.ssil[:], b.pA[:, :], AF.Silu,
                         scale=b.sscg[:, hc:hc + 1], inc="S")
            prev_sil = evs
            win_free[hc % 2] = [ev_g] + ([st_g] if st_g else [])
            if st_g is None and hc == NHC - 1:
                pass
            em.wait("vector", ev_u)
            em.wait("vector", evs)
            if prev_lo is not None:
                em.wait("vector", prev_lo)
            evm = em.add("vector", "scalar_tensor_tensor", b.shh32[:], b.pB[:, :],
                         b.sscu[:, hc:hc + 1], b.ssil[:],
                         op0=ALU.mult, op1=ALU.mult, inc="V")
            prev_h32 = evm
            win_free[2 + hc % 2] = [ev_u] + ([st_u] if st_u else [])
            em.wait("vector", evm)
            evhi = em.add("vector", "tensor_copy",
                          b.shhh[:, hc * 512:(hc + 1) * 512], b.shh32[:], inc="V")
            em.wait("vector", evhi)
            prev_lo = em.add("vector", "tensor_tensor",
                             b.shhl[:, hc * 512:(hc + 1) * 512], b.shh32[:],
                             b.shhh[:, hc * 512:(hc + 1) * 512], ALU.subtract,
                             inc="V")
        hh_ready = [prev_lo, prev_sil, prev_h32]
        b.xn_free = [("T", em.cnt["T"])]
        first_sweep_mm = True
        for sweep in ((0, 1), (2, 3)):
            wd_free = b.winf
            psets = {sweep[0]: [b.pdown[:, dg * 512:(dg + 1) * 512]
                                for dg in range(4)],
                     sweep[1]: [b.pA[:, :], b.pB[:, :], b.pC[:, :], b.pD[:, :]]}
            ev_d = None
            for hc in range(NHC):
                wd = b.win[hc % 4]
                if convert and tile == 0 and sweep[0] == 0:
                    in_ap = rar(b.wq_down[:, hc * 128:(hc + 1) * 128],
                                "(c p) f -> p c f", p=128)
                    ev_wd, st_d = emit_wconv(
                        em, b, in_ap, wd, wd_free[hc % 4],
                        b.wdT_c[hc * 128:(hc + 1) * 128, :],
                        f"DG{hc % 4}", f"DW{hc % 4}")
                else:
                    st_d = None
                    ev_wd = em.dma("sync", f"DW{hc % 4}", out=wd[:, :],
                                   in_=b.wdT_c[hc * 128:(hc + 1) * 128, :],
                                   waits=wd_free[hc % 4])
                em.wait("tensor", ev_wd)
                if first_sweep_mm:
                    em.waits("tensor", hh_ready)
                    em.waits("tensor", b.ps_free)
                    first_sweep_mm = False
                for tc in sweep:
                    for pi, hh in enumerate((b.shhh, b.shhl)):
                        lhsT = hh[:, hc * 512 + tc * 128:hc * 512 + (tc + 1) * 128]
                        for dg in range(4):
                            em.add("tensor", "matmul", psets[tc][dg], lhsT,
                                   wd[:, dg * 512:(dg + 1) * 512],
                                   start=(hc == 0 and pi == 0),
                                   stop=(hc == NHC - 1 and pi == 1),
                                   inc=("T" if (tc == sweep[-1] and pi == 1
                                                and dg == 3) else None))
                ev_d = ("T", em.cnt["T"])
                wd_free[hc % 4] = [ev_d] + ([st_d] if st_d else [])
            for tc in sweep:
                row0 = tile * T + tc * 128
                ev_x2 = em.dma("sync", "DX2", out=b.sxt2[:],
                               in_=src_[row0:row0 + 128, :], waits=b.free_sxt2)
                em.wait("vector", ev_d)
                em.wait("vector", ev_x2)
                em.wait("vector", ev_sdb)
                em.waits("vector", b.free_sjunk)
                if tc == sweep[0]:
                    evt_ = em.add("vector", "tensor_tensor", b.sjunk[:],
                                  b.pdown[:, :], b.ssdb[:], ALU.mult, inc="V")
                else:
                    for dg in range(4):
                        evt_ = em.add("vector", "tensor_tensor",
                                      b.sjunk[:, dg * 512:(dg + 1) * 512],
                                      psets[tc][dg],
                                      b.ssdb[:, dg * 512:(dg + 1) * 512],
                                      ALU.mult, inc="V")
                em.wait("vector", evt_)
                em.waits("vector", b.free_sres)
                evr = em.add("vector", "scalar_tensor_tensor", b.sres[:],
                             b.sjunk[:], 0.5, b.sxt2[:],
                             op0=ALU.mult, op1=ALU.add, inc="V")
                b.free_sjunk = [evr]
                b.free_sxt2 = [evr]
                em.wait("tensor", evt_)
                b.ps_free = [evt_]
                em.dma("gpsimd", "DR", out=dst[row0:row0 + 128, :], in_=b.sres[:],
                       waits=[evr])
                b.free_sres = [("DR", em.cnt["DR"])]
            if convert and tile == 0 and sweep[0] == 0:
                em.fence()
            if sweep[0] == 0 and tile + 1 < NT:
                emit_xnprep(em, b, src_, tile + 1)
        if tile + 1 < NT:
            for hc in range(PF):
                ev_wg_d = em.dma("sync", f"DW{hc % 2}", out=b.win[hc % 2][:, :],
                                 in_=b.wgT_c[hc * 128:(hc + 1) * 128, :],
                                 waits=b.winf[hc % 2])
                ev_wu_d = em.dma("sync", f"DW{2 + hc % 2}",
                                 out=b.win[2 + hc % 2][:, :],
                                 in_=b.wuT_c[hc * 128:(hc + 1) * 128, :],
                                 waits=b.winf[2 + hc % 2])
                b.winf[hc % 2] = [ev_wg_d]
                b.winf[2 + hc % 2] = [ev_wu_d]
                b.pref.append((ev_wg_d, ev_wu_d))
    em.fence()



def emit_wsplit(em, b, w_dram, cache_h, cache_l, nrow, ncol):
    for rc in range(nrow // 128):
        par = rc % 2
        ldb = b.sres if par == 0 else b.sjunk
        ld_free = b.free_sres if par == 0 else b.free_sjunk
        stg = b.sstga if par == 0 else b.sstgb
        stg_free = b.free_stga if par == 0 else b.free_stgb
        ev = em.dma("sync", "DF" if par == 0 else "DI", out=ldb[:, :ncol],
                    in_=w_dram[rc * 128:(rc + 1) * 128, :], waits=ld_free)
        em.wait("scalar" if par == 0 else "vector", ev)
        if par == 0:
            em.waits("scalar", stg_free)
            evh = em.add("scalar", "activation", stg[:, 0:1024], ldb[:, 0:1024],
                         AF.Identity, inc="S")
            if ncol > 1024:
                evh = em.add("scalar", "activation", stg[:, 1024:ncol],
                             ldb[:, 1024:ncol], AF.Identity, inc="S")
            em.wait("vector", evh)
            em.wait("vector", ev)
        else:
            em.waits("vector", stg_free)
            evh = em.add("vector", "tensor_copy", stg[:, 0:ncol], ldb[:, :ncol],
                         inc="V")
            em.wait("vector", evh)
        evl = em.add("vector", "tensor_tensor", stg[:, 2048:2048 + ncol],
                     ldb[:, :ncol], stg[:, 0:ncol], ALU.subtract, inc="V")
        if par == 0:
            b.free_sres = [evl]
        else:
            b.free_sjunk = [evl]
        em.dma("gpsimd", "DSG" if par == 0 else "DSB",
               out=cache_h[rc * 128:(rc + 1) * 128, :], in_=stg[:, 0:ncol],
               waits=[evh])
        em.dma("gpsimd", "DSG" if par == 0 else "DSB",
               out=cache_l[rc * 128:(rc + 1) * 128, :],
               in_=stg[:, 2048:2048 + ncol], waits=[evl])
        if par == 0:
            b.free_stga = [("DSG", em.cnt["DSG"])]
        else:
            b.free_stgb = [("DSB", em.cnt["DSB"])]


def emit_wsplit_perm(em, b, w_dram, cache_h, cache_l, nrow, ncol, blk):
    nkc = nrow // 128
    wcols = nkc * blk
    for oc in range(ncol // blk):
        par = oc % 2
        ldb = b.sres if par == 0 else b.sjunk
        ld_free = b.free_sres if par == 0 else b.free_sjunk
        stg = b.sstga if par == 0 else b.sstgb
        stg_free = b.free_stga if par == 0 else b.free_stgb
        in_ap = rar(w_dram[:, oc * blk:(oc + 1) * blk], "(c p) f -> p c f", p=128)
        out_ap = rar(ldb[:, :wcols], "p (c f) -> p c f", f=blk)
        ev = em.dma("sync", "DF" if par == 0 else "DI", out=out_ap, in_=in_ap,
                    waits=ld_free)
        em.wait("scalar" if par == 0 else "vector", ev)
        if par == 0:
            em.waits("scalar", stg_free)
            evh = em.add("scalar", "activation", stg[:, 0:1024], ldb[:, 0:1024],
                         AF.Identity, inc="S")
            if wcols > 1024:
                evh = em.add("scalar", "activation", stg[:, 1024:wcols],
                             ldb[:, 1024:wcols], AF.Identity, inc="S")
            em.wait("vector", evh)
            em.wait("vector", ev)
        else:
            em.waits("vector", stg_free)
            evh = em.add("vector", "tensor_copy", stg[:, 0:wcols], ldb[:, :wcols],
                         inc="V")
            em.wait("vector", evh)
        evl = em.add("vector", "tensor_tensor", stg[:, 2048:2048 + wcols],
                     ldb[:, :wcols], stg[:, 0:wcols], ALU.subtract, inc="V")
        if par == 0:
            b.free_sres = [evl]
        else:
            b.free_sjunk = [evl]
        em.dma("gpsimd", "DSG" if par == 0 else "DSB",
               out=cache_h[oc * 128:(oc + 1) * 128, :], in_=stg[:, 0:wcols],
               waits=[evh])
        em.dma("gpsimd", "DSG" if par == 0 else "DSB",
               out=cache_l[oc * 128:(oc + 1) * 128, :],
               in_=stg[:, 2048:2048 + wcols], waits=[evl])
        if par == 0:
            b.free_stga = [("DSG", em.cnt["DSG"])]
        else:
            b.free_stgb = [("DSB", em.cnt["DSB"])]



def emit_rope(em, b, qr_psum, out_hi, out_lo, col_base, col_stride, g, h0=0,
              nh=16):
    qr = qr_psum.rearrange("p (h j) -> p h j", j=64)
    TW = 16 * 64
    W32 = nh * 32
    sinv = bass.AP(b.stab, g * 64, [[TW, 128], [0, nh], [1, 32]])
    cosv = bass.AP(b.stab, g * 64 + 32, [[TW, 128], [0, nh], [1, 32]])
    sr = rar(b.srop, "p (h j) -> p h j", j=32)[:, :nh]
    sr2 = rar(b.srop2, "p (h j) -> p h j", j=16)
    sro = rar(b.sropo, "p (h j) -> p h j", j=32)[:, :nh]
    ev = em.add("vector", "tensor_tensor", b.srop[:, :W32], qr[:, :, 0:32], cosv,
                ALU.mult, inc="V")
    sinv_a = bass.AP(b.stab, g * 64, [[TW, 128], [0, nh], [1, 16]])
    sinv_b = bass.AP(b.stab, g * 64 + 16, [[TW, 128], [0, nh], [1, 16]])
    ev2a = em.add("vector", "tensor_tensor", sr2[:, 0:nh, :], qr[:, :, 16:32],
                  sinv_a, ALU.mult, inc="V")
    ev2b = em.add("vector", "tensor_tensor", sr2[:, 16:16 + nh, :], qr[:, :, 0:16],
                  sinv_b, ALU.mult, inc="V")
    em.wait("vector", ev)
    em.wait("vector", ev2a)
    em.add("vector", "tensor_tensor", sro[:, :, 0:16], sr[:, :, 0:16],
           sr2[:, 0:nh, :], ALU.subtract, inc="V")
    em.wait("vector", ev2b)
    ev4 = em.add("vector", "tensor_tensor", sro[:, :, 16:32], sr[:, :, 16:32],
                 sr2[:, 16:16 + nh, :], ALU.add, inc="V")
    em.wait("vector", ev4)
    oh = rar(out_hi, "p (h j) -> p h j", j=col_stride)[:, h0:h0 + nh]
    ol = rar(out_lo, "p (h j) -> p h j", j=col_stride)[:, h0:h0 + nh]
    rot_h = oh[:, :, col_base:col_base + 32]
    rot_l = ol[:, :, col_base:col_base + 32]
    ev5 = em.add("vector", "tensor_copy", rot_h, b.sropo[:, :W32], inc="V")
    em.wait("vector", ev5)
    em.add("vector", "tensor_tensor", rot_l, b.sropo[:, :W32], rot_h, ALU.subtract,
           inc="V")
    pas_h = oh[:, :, col_base + 32:col_base + 64]
    pas_l = ol[:, :, col_base + 32:col_base + 64]
    ev7 = em.add("vector", "tensor_copy", pas_h, qr[:, :, 32:64], inc="V")
    em.wait("vector", ev7)
    em.add("vector", "tensor_tensor", pas_l, qr[:, :, 32:64], pas_h, ALU.subtract,
           inc="V")
    return ("V", em.cnt["V"])



def emit_b1a(em, b):
    for tile in range(NT):
        emit_xnprep(em, b, b.resid, tile)
        em.fence()
        win_free = [(), (), (), ()]
        pfree = {0: None, 1: None}
        for (cache_h, cache_l, dst_h, dst_l) in (
                (b.cdkv_h, b.cdkv_l, b.ckv_h, b.ckv_l),
                (b.cdq_h, b.cdq_l, b.cq_h, b.cq_l)):
            for oc in range(NKC):
                par = oc % 2
                ps_ = b.pA if par == 0 else b.pB
                so = par * 512
                wh = b.win[par]
                wl = b.win[2 + par]
                evh_d = em.dma("sync", f"DW{par}", out=wh[:, :],
                               in_=cache_h[oc * 128:(oc + 1) * 128, :],
                               waits=win_free[par])
                evl_d = em.dma("sync", f"DW{2 + par}", out=wl[:, :],
                               in_=cache_l[oc * 128:(oc + 1) * 128, :],
                               waits=win_free[2 + par])
                em.wait("tensor", evh_d)
                em.wait("tensor", evl_d)
                em.wait("tensor", pfree[par])
                ev_mm = None
                for (wv, xn) in ((wh, b.sxnh), (wh, b.sxnl), (wl, b.sxnh)):
                    for dc in range(NDC):
                        st = (wv is wh and xn is b.sxnh and dc == 0)
                        last = (wv is wl and dc == NDC - 1)
                        ev_mm = em.add("tensor", "matmul", ps_[:, :],
                                       wv[:, dc * 128:(dc + 1) * 128],
                                       xn[:, dc * 512:(dc + 1) * 512],
                                       start=st, stop=last,
                                       inc=("T" if last else None))
                em.wait("vector", ev_mm)
                em.waits("vector", b.free_stga if par == 0 else b.free_stgb)
                stg = b.sstga if par == 0 else b.sstgb
                evh = em.add("vector", "tensor_copy", stg[:, 0:512],
                             ps_[:, :], inc="V")
                em.wait("vector", evh)
                evl = em.add("vector", "tensor_tensor", stg[:, 512:1024],
                             ps_[:, :], stg[:, 0:512], ALU.subtract, inc="V")
                pfree[par] = evl
                win_free[par] = [ev_mm]
                win_free[2 + par] = [ev_mm]
                em.dma("gpsimd", "DSG" if par == 0 else "DSB",
                       out=dst_h[oc * 128:(oc + 1) * 128, tile * T:(tile + 1) * T],
                       in_=stg[:, 0:512], waits=[evh])
                em.dma("gpsimd", "DSG" if par == 0 else "DSB",
                       out=dst_l[oc * 128:(oc + 1) * 128, tile * T:(tile + 1) * T],
                       in_=stg[:, 512:1024], waits=[evl])
                if par == 0:
                    b.free_stga = [("DSG", em.cnt["DSG"])]
                else:
                    b.free_stgb = [("DSB", em.cnt["DSB"])]
        em.fence()
        em.fence()



def emit_b1b(em, b):
    first_m = True
    for tile in range(NT):
        for (dram_h, dram_l, sbh, sbl) in (
                (b.cq_h, b.cq_l, b.scqh, b.scql),
                (b.ckv_h, b.ckv_l, b.sckvh, b.sckvl)):
            em.dma("sync", "DT", out=rar(sbh, "p (c t) -> p c t", c=NKC),
                   in_=rar(dram_h[:, tile * T:(tile + 1) * T], "(c p) f -> p c f",
                           p=128))
            em.dma("sync", "DT", out=rar(sbl, "p (c t) -> p c t", c=NKC),
                   in_=rar(dram_l[:, tile * T:(tile + 1) * T], "(c p) f -> p c f",
                           p=128))
        em.fence()
        b.win_free = [(), (), (), ()]
        win_free = b.win_free
        qfree = {0: None, 1: None}
        for oc in range(NKC):
            par = oc % 2
            ps_ = b.pA if par == 0 else b.pB
            wh = b.win[par]
            wl = b.win[2 + par]
            evh_d = em.dma("sync", f"DW{par}", out=wh[:, 0:1024],
                           in_=b.cuq_h[oc * 128:(oc + 1) * 128, :],
                           waits=win_free[par])
            evl_d = em.dma("sync", f"DW{2 + par}", out=wl[:, 0:1024],
                           in_=b.cuq_l[oc * 128:(oc + 1) * 128, :],
                           waits=win_free[2 + par])
            em.wait("tensor", evh_d)
            em.wait("tensor", evl_d)
            em.wait("tensor", qfree[par])
            ev_mm = None
            for (wv, cq) in ((wh, b.scqh), (wh, b.scql), (wl, b.scqh)):
                for kc in range(NKC):
                    st = (wv is wh and cq is b.scqh and kc == 0)
                    last = (wv is wl and kc == NKC - 1)
                    ev_mm = em.add("tensor", "matmul", ps_[:, :],
                                   wv[:, kc * 128:(kc + 1) * 128],
                                   cq[:, kc * 512:(kc + 1) * 512],
                                   start=st, stop=last,
                                   inc=("T" if last else None))
            em.wait("vector", ev_mm)
            out_h = bass.AP(b.sqcfh, oc * 128, [[4096, 128], [1024, 4], [1, 128]])
            out_l = bass.AP(b.sqcfl, oc * 128, [[4096, 128], [1024, 4], [1, 128]])
            evh = em.add("vector", "tensor_copy", out_h, ps_[:, :], inc="V")
            em.wait("vector", evh)
            evl = em.add("vector", "tensor_tensor", out_l, ps_[:, :], out_h,
                         ALU.subtract, inc="V")
            qfree[par] = evl
            win_free[par] = [ev_mm]
            win_free[2 + par] = [ev_mm]
        em.fence()
        prev_mmm = None
        for tc in range(NTC):
            row0 = tile * T + tc * 128
            for half, (qcf, stg, Qd, tsem) in enumerate(
                    ((b.sqcfh, b.sQstgh, b.sQh, "DTA"),
                     (b.sqcfl, b.sQstgl, b.sQl, "DTB"))):
                in_ap = rar(qcf[:, tc * 1024:(tc + 1) * 1024],
                            "p (c t) -> p c t", t=128)
                evt = em.dma("sync", tsem,
                             out=rar(stg, "p (c d) -> p c d", d=128),
                             in_=in_ap, waits=b.qstg_free[half], transpose=True)
                em.wait("vector", evt)
                em.wait("vector", prev_mmm)
                out_ap = bass.AP(Qd, 0, [[2048, 128], [256, NKC], [128, 2], [1, 64]])
                in_ap2 = bass.AP(stg, 0, [[1024, 128], [128, NKC], [64, 2], [1, 64]])
                eve = em.add("vector", "tensor_copy", out_ap, in_ap2, inc="V")
                b.qstg_free[half] = [eve]
            win_free = b.win_free
            em.waits("tensor", b.qr_free)
            for oc in range(NKC):
                wh = b.win[oc % 2]
                wl = b.win[2 + oc % 2]
                evh_d = em.dma("sync", f"DW{oc % 2}", out=wh[:, 0:1024],
                               in_=b.cqr_h[oc * 128:(oc + 1) * 128, :],
                               waits=win_free[oc % 2])
                evl_d = em.dma("sync", f"DW{2 + oc % 2}", out=wl[:, 0:1024],
                               in_=b.cqr_l[oc * 128:(oc + 1) * 128, :],
                               waits=win_free[2 + oc % 2])
                em.wait("tensor", evh_d)
                em.wait("tensor", evl_d)
                for (wv, qcf) in ((wh, b.sqcfh), (wh, b.sqcfl), (wl, b.sqcfh)):
                    lhsT = qcf[:, tc * 1024 + oc * 128:tc * 1024 + (oc + 1) * 128]
                    st = (wv is wh and qcf is b.sqcfh and oc == 0)
                    for ng, png in ((0, b.pC), (1, b.pD)):
                        em.add("tensor", "matmul", png[:, :],
                               lhsT, wv[:, ng * 512:(ng + 1) * 512],
                               start=st, stop=(wv is wl and oc == NKC - 1),
                               inc=("T" if (wv is wl and ng == 1) else None))
                ev_qr = ("T", em.cnt["T"])
                win_free[oc % 2] = [ev_qr]
                win_free[2 + oc % 2] = [ev_qr]
            em.wait("vector", ev_qr)
            for half, png in ((0, b.pC), (1, b.pD)):
                ev_rope = emit_rope(em, b, png[:, :], b.sQh, b.sQl, 64, 128,
                                    tile * NTC + tc, h0=half * 8, nh=8)
            b.qr_free = [ev_rope]
            win_free = b.win_free
            for kc in range(NKC):
                wh = b.win[kc % 2]
                wl = b.win[2 + kc % 2]
                evh_d = em.dma("sync", f"DW{kc % 2}", out=wh[:, :],
                               in_=b.cuv_h[kc * 128:(kc + 1) * 128, :],
                               waits=win_free[kc % 2])
                evl_d = em.dma("sync", f"DW{2 + kc % 2}", out=wl[:, :],
                               in_=b.cuv_l[kc * 128:(kc + 1) * 128, :],
                               waits=win_free[2 + kc % 2])
                em.wait("tensor", evh_d)
                em.wait("tensor", evl_d)
                for (wv, ckv) in ((wh, b.sckvh), (wh, b.sckvl), (wl, b.sckvh)):
                    lhsT = ckv[:, kc * 512 + tc * 128:kc * 512 + (tc + 1) * 128]
                    st = (wv is wh and ckv is b.sckvh and kc == 0)
                    for vg in range(4):
                        em.add("tensor", "matmul",
                               b.pdown[:, vg * 512:(vg + 1) * 512],
                               lhsT, wv[:, vg * 512:(vg + 1) * 512],
                               start=st, stop=(wv is wl and kc == NKC - 1),
                               inc=("T" if (wv is wl and vg == 3) else None))
                ev_v = ("T", em.cnt["T"])
                win_free[kc % 2] = [ev_v]
                win_free[2 + kc % 2] = [ev_v]
            em.wait("vector", ev_v)
            em.wait("vector", prev_mmm)
            evh = em.add("vector", "tensor_copy", b.sVh[:], b.pdown[:, :], inc="V")
            em.wait("vector", evh)
            evl = em.add("vector", "tensor_tensor", b.sVl[:], b.pdown[:, :],
                         b.sVh[:], ALU.subtract, inc="V")
            em.wait("tensor", evl)
            em.wait("tensor", ev_rope)
            for h in range(NH):
                for (Qx, Vx) in ((b.sQh, b.sVh), (b.sQl, b.sVh), (b.sQh, b.sVl)):
                    st = (h % 4 == 0 and Qx is b.sQh and Vx is b.sVh)
                    sp = (h % 4 == 3 and Qx is b.sQh and Vx is b.sVl)
                    em.add("tensor", "matmul",
                           b.pdown[:, h * 128:(h + 1) * 128],
                           Qx[:, h * 128:(h + 1) * 128],
                           Vx[:, h * 128:(h + 1) * 128],
                           start=st, stop=sp,
                           inc=("T" if sp else None))
            ev_mmm = ("T", em.cnt["T"])
            prev_mmm = ev_mmm
            em.wait("vector", ev_mmm)
            if first_m:
                evm = em.add("vector", "tensor_copy", b.sMf[:], b.pdown[:, :],
                             inc="V")
                first_m = False
            else:
                evm = em.add("vector", "tensor_tensor", b.sMf[:], b.pdown[:, :],
                             b.sMf[:], ALU.add, inc="V")
            em.wait("tensor", evm)
        em.fence()



def emit_b2(em, b):
    prev = None
    for part, dst in ((0, b.ssinf), (1, b.scosf)):
        evh = em.dma("sync", "DC", out=b.sjunk[:],
                     in_=b.angf_hi[:, part * S:(part + 1) * S],
                     waits=([prev] if prev else []) + list(b.free_sjunk or ()))
        evl = em.dma("sync", "DC", out=b.sres[:],
                     in_=b.angf_lo[:, part * S:(part + 1) * S],
                     waits=b.free_sres)
        em.wait("vector", evl)
        eva = em.add("vector", "tensor_tensor", dst[:], b.sjunk[:], b.sres[:],
                     ALU.add, inc="V")
        b.free_sjunk = [eva]
        b.free_sres = [eva]
        em.wait("scalar", eva)
        prev = em.add("scalar", "activation", dst[:], dst[:], AF.Sin, inc="S")
    ev1 = em.add("vector", "tensor_scalar", b.sMf[:], b.sMf[:], ISQ, None,
                 op0=ALU.mult, inc="V")
    em.wait("vector", ev1)
    ev2 = em.add("vector", "tensor_copy", b.sMh[:], b.sMf[:], inc="V")
    em.wait("vector", ev2)
    ev3 = em.add("vector", "tensor_tensor", b.sMl[:], b.sMf[:], b.sMh[:],
                 ALU.subtract, inc="V")
    for (srcM, dstM, w) in ((b.sMh, b.sMsh, ev2), (b.sMl, b.sMsl, ev3)):
        em.dma("sync", "DT", out=dstM[0:64, :], in_=srcM[64:128, :], waits=[w])
        em.dma("sync", "DT", out=dstM[64:128, :], in_=srcM[0:64, :])
    em.fence()
    b.ps_free = []
    b.xn_free = ()
    b.ckv_free = ()
    for tile in range(NT):
        if tile == 0:
            emit_xnprep(em, b, b.resid, 0)
        ev_ckv = em.dma("sync", "DT", out=rar(b.sckvh, "p (c t) -> p c t", c=NKC),
                        in_=rar(b.ckv_h[:, tile * T:(tile + 1) * T],
                                "(c p) f -> p c f", p=128),
                        waits=b.ckv_free)
        ev_ckv = em.dma("sync", "DT", out=rar(b.sckvl, "p (c t) -> p c t", c=NKC),
                        in_=rar(b.ckv_l[:, tile * T:(tile + 1) * T],
                                "(c p) f -> p c f", p=128))
        cosli = b.scosf[:, tile * T:(tile + 1) * T]
        sinli = b.ssinf[:, tile * T:(tile + 1) * T]
        win_free = b.winf
        bank = {0: (b.pA[:, :], b.pB[:, :], b.pC[:, :], b.pD[:, :]),
                1: (b.pdown[:, 0:512], b.pdown[:, 512:1024],
                    b.pdown[:, 1024:1536], b.pdown[:, 1536:2048])}
        bfree = {0: {"kc": None, "kr": None, "sh": None, "o": None},
                 1: {"kc": None, "kr": None, "sh": None, "o": None}}
        sfree = {0: {"rp": (), "ro": (), "kc": ()}, 1: {"rp": (), "ro": (), "kc": ()}}
        for c in range(NKC):
            par = c % 2
            pKC, pKR, pSH, pO = bank[par]
            srph = b.srph[:, par * 512:(par + 1) * 512]
            srpl = b.srpl[:, par * 512:(par + 1) * 512]
            sroh = b.sroh[:, par * 512:(par + 1) * 512]
            srol = b.srol[:, par * 512:(par + 1) * 512]
            skch = b.skch[:, par * 512:(par + 1) * 512]
            skcl = b.skcl[:, par * 512:(par + 1) * 512]
            w0 = b.win[par]
            w1 = b.win[2 + par]
            em.dma("sync", f"DW{par}", out=w0[:, 0:1024],
                   in_=b.cuk_h[c * 128:(c + 1) * 128, :], waits=win_free[par])
            e_uk = em.dma("sync", f"DW{par}", out=w0[:, 1024:2048],
                          in_=b.cuk_l[c * 128:(c + 1) * 128, :])
            e_kh = em.dma("sync", f"DW{2 + par}", out=w1[:, :],
                          in_=b.ckr_h[c * 128:(c + 1) * 128, :],
                          waits=win_free[2 + par])
            w2 = b.win[2 + (1 - par)]
            e_kl = em.dma("sync", f"DW{2 + (1 - par)}", out=w2[:, :],
                          in_=b.ckr_l[c * 128:(c + 1) * 128, :],
                          waits=win_free[2 + (1 - par)])
            em.wait("tensor", e_kh)
            em.wait("tensor", e_kl)
            if c == 0:
                em.waits("tensor", b.xn_ready)
                em.waits("tensor", b.ps_free)
                em.wait("tensor", ev_ckv)
            em.wait("tensor", bfree[par]["kr"])
            ev_kr = None
            for (wv, xn) in ((w1, b.sxnh), (w1, b.sxnl), (w2, b.sxnh)):
                for dc in range(NDC):
                    st = (wv is w1 and xn is b.sxnh and dc == 0)
                    last = (wv is w2 and dc == NDC - 1)
                    ev_kr = em.add("tensor", "matmul", pKR,
                                   wv[:, dc * 128:(dc + 1) * 128],
                                   xn[:, dc * 512:(dc + 1) * 512],
                                   start=st, stop=last,
                                   inc=("T" if last else None))
            win_free[2 + par] = [ev_kr]
            win_free[2 + (1 - par)] = [ev_kr]
            em.wait("tensor", e_uk)
            em.wait("tensor", bfree[par]["kc"])
            ev_kc = None
            for (woff, ckv) in ((0, b.sckvh), (0, b.sckvl), (1024, b.sckvh)):
                for kc in range(NKC):
                    st = (woff == 0 and ckv is b.sckvh and kc == 0)
                    last = (woff == 1024 and kc == NKC - 1)
                    ev_kc = em.add("tensor", "matmul", pKC,
                                   w0[:, woff + kc * 128:woff + (kc + 1) * 128],
                                   ckv[:, kc * 512:(kc + 1) * 512],
                                   start=st, stop=last,
                                   inc=("T" if last else None))
            win_free[par] = [ev_kc]
            em.wait("vector", ev_kr)
            em.waits("vector", sfree[par]["rp"])
            e_h = em.add("vector", "tensor_copy", srph, pKR, inc="V")
            em.wait("vector", e_h)
            e_l = em.add("vector", "tensor_tensor", srpl, pKR, srph,
                         ALU.subtract, inc="V")
            em.wait("tensor", e_l)
            em.wait("tensor", bfree[par]["sh"])
            em.add("tensor", "matmul", pSH, b.sperm[:], srph,
                   start=True, stop=False)
            ev_sh = em.add("tensor", "matmul", pSH, b.sperm[:], srpl,
                           start=False, stop=True, inc="T")
            sfree[par]["rp"] = [ev_sh]
            em.wait("vector", ev_sh)
            r1 = em.add("vector", "tensor_tensor", b.sropa[:], pKR, cosli,
                        ALU.mult, inc="V")
            bfree[par]["kr"] = r1
            em.wait("vector", r1)
            r2 = em.add("vector", "tensor_tensor", b.sropb[:], pSH, sinli,
                        ALU.mult, inc="V")
            bfree[par]["sh"] = r2
            em.wait("vector", r2)
            r3 = em.add("vector", "tensor_tensor", b.sropa[:], b.sropa[:],
                        b.sropb[:], ALU.add, inc="V")
            em.wait("vector", r3)
            em.waits("vector", sfree[par]["ro"])
            g1 = em.add("vector", "tensor_copy", sroh, b.sropa[:], inc="V")
            em.wait("vector", g1)
            g2 = em.add("vector", "tensor_tensor", srol, b.sropa[:], sroh,
                        ALU.subtract, inc="V")
            em.wait("vector", ev_kc)
            em.waits("vector", sfree[par]["kc"])
            c1 = em.add("vector", "tensor_copy", skch, pKC, inc="V")
            em.wait("vector", c1)
            c2 = em.add("vector", "tensor_tensor", skcl, pKC, skch,
                        ALU.subtract, inc="V")
            bfree[par]["kc"] = c2
            for hp in range(2):
                h = 2 * c + hp
                base = hp * 64
                if hp == 0:
                    mtop = (b.sMh, b.sMl)
                    mbot = (b.sMsh, b.sMsl)
                else:
                    mtop = (b.sMsh, b.sMsl)
                    mbot = (b.sMh, b.sMl)
                em.wait("tensor", c2)
                em.wait("tensor", g2)
                em.wait("tensor", bfree[par]["o"])
                ev_o = None
                first = True
                for part, (Mt, rh, rl) in (
                        (0, ((mtop), skch, skcl)),
                        (1, ((mbot), sroh, srol))):
                    Mh_, Ml_ = Mt
                    for (Mx, rx) in ((Mh_, rh), (Ml_, rh), (Mh_, rl)):
                        ev_o = em.add(
                            "tensor", "matmul", pO,
                            Mx[base:base + 64, h * 128:(h + 1) * 128],
                            rx[base:base + 64, :],
                            start=first, stop=(part == 1 and Mx is Mh_ and rx is rl),
                            inc=("T" if (part == 1 and Mx is Mh_ and rx is rl)
                                 else None))
                        first = False
                em.wait("vector", ev_o)
                eh = em.add("vector", "tensor_copy",
                            b.sOh[:, h * 512:(h + 1) * 512], pO, inc="V")
                em.wait("vector", eh)
                el = em.add("vector", "tensor_tensor",
                            b.sOl[:, h * 512:(h + 1) * 512], pO,
                            b.sOh[:, h * 512:(h + 1) * 512], ALU.subtract,
                            inc="V")
                bfree[par]["o"] = el
            sfree[par]["ro"] = [("T", em.cnt["T"])]
            sfree[par]["kc"] = [("T", em.cnt["T"])]
        b.xn_free = [("T", em.cnt["T"])]
        b.ckv_free = [("T", em.cnt["T"])]
        if tile + 1 < NT:
            emit_xnprep(em, b, b.resid, tile + 1)
        o_ready = [bfree[0]["o"], bfree[1]["o"], bfree[0]["kc"], bfree[1]["kc"],
                   bfree[0]["kr"], bfree[1]["kr"], bfree[0]["sh"], bfree[1]["sh"]]
        first_op = True
        for sweep in ((0, 1), (2, 3)):
            psets = {sweep[0]: [b.pdown[:, dg * 512:(dg + 1) * 512]
                                for dg in range(4)],
                     sweep[1]: [b.pA[:, :], b.pB[:, :], b.pC[:, :], b.pD[:, :]]}
            ev_mm = None
            for cc in range(NDC):
                par = cc % 2
                woh = b.win[par]
                wol = b.win[2 + par]
                evh_d = em.dma("sync", f"DW{par}", out=woh[:, :],
                               in_=b.cout_h[cc * 128:(cc + 1) * 128, :],
                               waits=win_free[par])
                evl_d = em.dma("sync", f"DW{2 + par}", out=wol[:, :],
                               in_=b.cout_l[cc * 128:(cc + 1) * 128, :],
                               waits=win_free[2 + par])
                em.wait("tensor", evh_d)
                em.wait("tensor", evl_d)
                if first_op:
                    em.waits("tensor", o_ready)
                    em.waits("tensor", b.ps_free)
                    first_op = False
                for tc in sweep:
                    for (Ox, wv) in ((b.sOh, woh), (b.sOl, woh), (b.sOh, wol)):
                        lhsT = Ox[:, cc * 512 + tc * 128:cc * 512 + (tc + 1) * 128]
                        st = (Ox is b.sOh and wv is woh and cc == 0)
                        sp = (Ox is b.sOh and wv is wol and cc == NDC - 1)
                        for dg in range(4):
                            em.add("tensor", "matmul", psets[tc][dg], lhsT,
                                   wv[:, dg * 512:(dg + 1) * 512],
                                   start=st, stop=sp,
                                   inc=("T" if (tc == sweep[-1] and Ox is b.sOh
                                                and wv is wol and dg == 3)
                                        else None))
                ev_mm = ("T", em.cnt["T"])
                win_free[par] = [ev_mm]
                win_free[2 + par] = [ev_mm]
            for tc in sweep:
                row0 = tile * T + tc * 128
                ev_x2 = em.dma("sync", "DX2", out=b.sxt2[:],
                               in_=b.resid[row0:row0 + 128, :], waits=b.free_sxt2)
                em.wait("vector", ev_mm)
                em.wait("vector", ev_x2)
                em.waits("vector", b.free_sres)
                if tc == sweep[0]:
                    evr = em.add("vector", "tensor_tensor", b.sres[:],
                                 b.pdown[:, :], b.sxt2[:], ALU.add, inc="V")
                else:
                    for dg in range(4):
                        evr = em.add("vector", "tensor_tensor",
                                     b.sres[:, dg * 512:(dg + 1) * 512],
                                     psets[tc][dg],
                                     b.sxt2[:, dg * 512:(dg + 1) * 512],
                                     ALU.add, inc="V")
                b.free_sxt2 = [evr]
                em.wait("tensor", evr)
                b.ps_free = [evr]
                em.dma("gpsimd", "DR", out=b.resid[row0:row0 + 128, :],
                       in_=b.sres[:], waits=[evr])
                b.free_sres = [("DR", em.cnt["DR"])]
    em.fence()



ALL_PHASES = ("P0", "A", "B1a", "B1b", "B2", "C")


def build_program(debug=False, phases=ALL_PHASES):
    nc = bass.Bass("TRN2", target_bir_lowering=False)
    b = NS()

    def din(name, shape, dt=F32):
        return nc.dram_tensor(name, shape, dt, kind="ExternalInput").ap()

    b.x_in = din("x", [S, D])
    b.w_dkv = din("w_dkv", [D, CD]); b.w_dq = din("w_dq", [D, CD])
    b.w_uk = din("w_uk", [CD, CD]); b.w_uv = din("w_uv", [CD, D])
    b.w_uq = din("w_uq", [CD, CD]); b.w_kr = din("w_kr", [D, CD])
    b.w_qr = din("w_qr", [CD, CD]); b.w_out = din("w_out", [D, D])
    b.wq_gate = din("wq_gate", [HID, D], I32)
    b.wq_up = din("wq_up", [HID, D], I32)
    b.wq_down = din("wq_down", [D, HID], I32)
    b.s_gate = din("s_gate", [HID]); b.s_up = din("s_up", [HID])
    b.sdb_in = din("sdb", [128, D])
    b.ang_hi = din("ang_hi", [S, 64]); b.ang_lo = din("ang_lo", [S, 64])
    b.angf_hi = din("angf_hi", [128, 2 * S]); b.angf_lo = din("angf_lo", [128, 2 * S])
    b.perm_in = din("perm", [128, 128], BF16)
    b.ident_in = din("ident", [128, 128], BF16)
    b.y = nc.dram_tensor("y", [S, D], F32, kind="ExternalOutput").ap()

    def scr(name, shape, dt, dbgk=False):
        kind = "ExternalOutput" if (dbgk and debug) else "Internal"
        return nc.dram_tensor(name, shape, dt, kind=kind).ap()

    b.resid = scr("resid", [S, D], F32, dbgk=True)
    b.wgT_c = scr("wgT_c", [NHC * 128, D], BF16)
    b.wuT_c = scr("wuT_c", [NHC * 128, D], BF16)
    b.wdT_c = scr("wdT_c", [NHC * 128, D], BF16)
    b.cdkv_h = scr("cdkv_h", [CD, D], BF16); b.cdkv_l = scr("cdkv_l", [CD, D], BF16)
    b.cdq_h = scr("cdq_h", [CD, D], BF16); b.cdq_l = scr("cdq_l", [CD, D], BF16)
    b.cuq_h = scr("cuq_h", [CD, CD], BF16); b.cuq_l = scr("cuq_l", [CD, CD], BF16)
    b.cuk_h = scr("cuk_h", [CD, CD], BF16)
    b.cuk_l = scr("cuk_l", [CD, CD], BF16)
    b.ckr_h = scr("ckr_h", [CD, D], BF16); b.ckr_l = scr("ckr_l", [CD, D], BF16)
    b.cqr_h = scr("cqr_h", [CD, CD], BF16); b.cqr_l = scr("cqr_l", [CD, CD], BF16)
    b.cuv_h = scr("cuv_h", [CD, D], BF16); b.cuv_l = scr("cuv_l", [CD, D], BF16)
    b.cout_h = scr("cout_h", [D, D], BF16); b.cout_l = scr("cout_l", [D, D], BF16)
    b.ckv_h = scr("ckv_h", [CD, S], BF16, dbgk=True)
    b.ckv_l = scr("ckv_l", [CD, S], BF16)
    b.cq_h = scr("cq_h", [CD, S], BF16); b.cq_l = scr("cq_l", [CD, S], BF16)

    em = Em(nc)
    with contextlib.ExitStack() as stack:
        def sb(name, shape, dt):
            return stack.enter_context(nc.sbuf_tensor(name, shape, dt))

        def ps(name, shape, dt):
            return stack.enter_context(nc.psum_tensor(name, shape, dt))

        for sname in ("T", "V", "S", "G", "Y", "TF", "VF", "SF", "GF", "YF") + DMA_SEMS:
            em.reg_sem(sname, stack.enter_context(nc.semaphore(sname)))
        b.sxt = sb("sxt", [128, D], F32)
        b.sxt2 = sb("sxt2", [128, D], F32)
        b.sxnh = sb("sxnh", [128, NDC * 512], BF16)
        b.sxnl = sb("sxnl", [128, NDC * 512], BF16)
        b.sxtmh = sb("sxtmh", [128, D], BF16)
        b.sxtml = sb("sxtml", [128, D], BF16)
        b.sjunk = sb("sjunk", [128, D], F32)
        b.sres = sb("sres", [128, D], F32)
        b.sstat = sb("sstat", [128, 16], F32)
        b.sscg = sb("sscg", [128, NHC], F32)
        b.sscu = sb("sscu", [128, NHC], F32)
        b.sident = sb("sident", [128, 128], BF16)
        b.ssil = sb("ssil", [128, 512], F32)
        b.shh32 = sb("shh32", [128, 512], F32)
        b.win = [sb(f"win{i}", [128, 2048], BF16) for i in range(4)]
        b.stab = sb("stab", [128, 16 * 64], F32)
        b.pA = ps("pA", [128, 512], F32)
        b.pB = ps("pB", [128, 512], F32)
        b.pC = ps("pC", [128, 512], F32)
        b.pD = ps("pD", [128, 512], F32)
        b.pdown = ps("pdown", [128, 2048], F32)
        b.sperm = sb("sperm", [128, 128], BF16)
        for attr in ("free_sxt", "free_sxt2", "free_sjunk", "free_sres",
                     "free_sxtm", "free_sint", "free_sconv", "free_stga",
                     "free_stgb"):
            setattr(b, attr, ())


        b.winf = [(), (), (), ()]
        b.ps_free = []
        b.xn_free = ()
        b.xn_ready = []
        b.ckv_free = ()
        b.pref = []
        em.dma("sync", "DT", out=b.sident[:], in_=b.ident_in[:, :])
        em.dma("sync", "DT", out=b.sperm[:], in_=b.perm_in[:, :])
        em.dma("sync", "DT", out=b.sscg[:],
               in_=rar(b.s_gate, "(c p) -> p c", p=128),
               allow_slow_non_contiguous=True)
        em.dma("sync", "DT", out=b.sscu[:],
               in_=rar(b.s_up, "(c p) -> p c", p=128),
               allow_slow_non_contiguous=True)
        prev_sin = None
        for g in range(16):
            evh = em.dma("sync", "DC", out=b.sjunk[:, 0:64],
                         in_=b.ang_hi[g * 128:(g + 1) * 128, :],
                         waits=([prev_sin] if prev_sin else ()))
            evl = em.dma("sync", "DC", out=b.sjunk[:, 64:128],
                         in_=b.ang_lo[g * 128:(g + 1) * 128, :])
            em.wait("vector", evl)
            if prev_sin is not None:
                em.wait("vector", prev_sin)
            eva = em.add("vector", "tensor_tensor", b.sjunk[:, 128:192],
                         b.sjunk[:, 0:64], b.sjunk[:, 64:128], ALU.add, inc="V")
            em.wait("scalar", eva)
            prev_sin = em.add("scalar", "activation",
                              b.stab[:, g * 64:(g + 1) * 64],
                              b.sjunk[:, 128:192], AF.Sin, inc="S")
        b.free_sjunk = [prev_sin]
        if "P0" in phases:
            with nc.sbuf_tensor("sstga", [128, 4096], BF16) as sstga, \
                 nc.sbuf_tensor("sstgb", [128, 4096], BF16) as sstgb:
                b.sstga, b.sstgb = sstga, sstgb
                emit_wsplit_perm(em, b, b.w_dkv, b.cdkv_h, b.cdkv_l, D, CD, 128)
                emit_wsplit_perm(em, b, b.w_dq, b.cdq_h, b.cdq_l, D, CD, 128)
                emit_wsplit_perm(em, b, b.w_uq, b.cuq_h, b.cuq_l, CD, CD, 128)
                emit_wsplit_perm(em, b, b.w_uk, b.cuk_h, b.cuk_l, CD, CD, 128)
                emit_wsplit_perm(em, b, b.w_kr, b.ckr_h, b.ckr_l, D, CD, 128)
                emit_wsplit(em, b, b.w_qr, b.cqr_h, b.cqr_l, CD, CD)
                emit_wsplit(em, b, b.w_uv, b.cuv_h, b.cuv_l, CD, D)
                emit_wsplit(em, b, b.w_out, b.cout_h, b.cout_l, D, D)
                em.fence()
                em.flush()
        if "A" in phases:
            with nc.sbuf_tensor("shhh", [128, NHC * 512], BF16) as shhh, \
                 nc.sbuf_tensor("shhl", [128, NHC * 512], BF16) as shhl, \
                 nc.sbuf_tensor("sint", [128, 2048], I32) as sint, \
                 nc.sbuf_tensor("ssdbA", [128, D], F32) as ssdb, \
                 nc.sbuf_tensor("sconvA", [128, 2048], BF16) as sconv:
                b.shhh, b.shhl, b.sint, b.sconv = shhh, shhl, sint, sconv
                b.ssdb = ssdb
                emit_ffn(em, b, b.x_in, b.resid, convert=True)
                em.fence()
                em.flush()
        if "B1a" in phases:
            with nc.sbuf_tensor("sstga2", [128, 2048], BF16) as sstga, \
                 nc.sbuf_tensor("sstgb2", [128, 2048], BF16) as sstgb, \
                 nc.sbuf_tensor("skrs_h", [128, 1024], BF16) as skrs_h, \
                 nc.sbuf_tensor("skrs_l", [128, 1024], BF16) as skrs_l, \
                 nc.sbuf_tensor("scosA", [128, 512], F32) as scos, \
                 nc.sbuf_tensor("ssinA", [128, 512], F32) as ssin, \
                 nc.sbuf_tensor("sropA", [128, 512], F32) as srop, \
                 nc.sbuf_tensor("srop2A", [128, 512], F32) as srop2, \
                 nc.sbuf_tensor("sropoA", [128, 512], F32) as sropo:
                b.sstga, b.sstgb = sstga, sstgb
                b.skrs_h, b.skrs_l = skrs_h, skrs_l
                b.scos, b.ssin, b.srop, b.srop2, b.sropo = scos, ssin, srop, srop2, sropo
                b.free_stga = ()
                b.free_stgb = ()
                emit_b1a(em, b)
                em.fence()
                em.flush()
        if "B1b" not in phases:
            return nc
        with nc.sbuf_tensor("sckvh", [128, NKC * 512], BF16) as sckvh, \
             nc.sbuf_tensor("sckvl", [128, NKC * 512], BF16) as sckvl, \
             nc.sbuf_tensor("sMf", [128, NH * 128], F32) as sMf:
            b.sckvh, b.sckvl, b.sMf = sckvh, sckvl, sMf
            with nc.sbuf_tensor("scqh", [128, NKC * 512], BF16) as scqh, \
                 nc.sbuf_tensor("scql", [128, NKC * 512], BF16) as scql, \
                 nc.sbuf_tensor("sqcfh", [128, NKC * 512], BF16) as sqcfh, \
                 nc.sbuf_tensor("sqcfl", [128, NKC * 512], BF16) as sqcfl, \
                 nc.sbuf_tensor("sQh", [128, 2048], BF16) as sQh, \
                 nc.sbuf_tensor("sQl", [128, 2048], BF16) as sQl, \
                 nc.sbuf_tensor("sVh", [128, 2048], BF16) as sVh, \
                 nc.sbuf_tensor("sVl", [128, 2048], BF16) as sVl, \
                 nc.sbuf_tensor("sQstgh", [128, 1024], BF16) as sQstgh, \
                 nc.sbuf_tensor("sQstgl", [128, 1024], BF16) as sQstgl, \
                 nc.sbuf_tensor("sropB", [128, 512], F32) as srop, \
                 nc.sbuf_tensor("srop2B", [128, 512], F32) as srop2, \
                 nc.sbuf_tensor("sropoB", [128, 512], F32) as sropo:
                b.scqh, b.scql, b.sqcfh, b.sqcfl = scqh, scql, sqcfh, sqcfl
                b.sQh, b.sQl, b.sVh, b.sVl = sQh, sQl, sVh, sVl
                b.sQstgh, b.sQstgl = sQstgh, sQstgl
                b.qstg_free = {0: (), 1: ()}
                b.qr_free = ()
                b.srop, b.srop2, b.sropo = srop, srop2, sropo
                emit_b1b(em, b)
                em.fence()
                em.flush()
            if "B2" not in phases:
                return nc
            with nc.sbuf_tensor("sMh", [128, NH * 128], BF16) as sMh, \
                 nc.sbuf_tensor("sMl", [128, NH * 128], BF16) as sMl, \
                 nc.sbuf_tensor("sMsh", [128, NH * 128], BF16) as sMsh, \
                 nc.sbuf_tensor("sMsl", [128, NH * 128], BF16) as sMsl, \
                 nc.sbuf_tensor("scosf", [128, S], F32) as scosf, \
                 nc.sbuf_tensor("ssinf", [128, S], F32) as ssinf, \
                 nc.sbuf_tensor("sropa", [128, 512], F32) as sropa, \
                 nc.sbuf_tensor("sropb", [128, 512], F32) as sropb, \
                 nc.sbuf_tensor("srph", [128, 1024], BF16) as srph, \
                 nc.sbuf_tensor("srpl", [128, 1024], BF16) as srpl, \
                 nc.sbuf_tensor("sroh", [128, 1024], BF16) as sroh, \
                 nc.sbuf_tensor("srol", [128, 1024], BF16) as srol, \
                 nc.sbuf_tensor("skch", [128, 1024], BF16) as skch, \
                 nc.sbuf_tensor("skcl", [128, 1024], BF16) as skcl, \
                 nc.sbuf_tensor("sOh", [128, NH * 512], BF16) as sOh, \
                 nc.sbuf_tensor("sOl", [128, NH * 512], BF16) as sOl:
                b.sMh, b.sMl, b.sMsh, b.sMsl = sMh, sMl, sMsh, sMsl
                b.scosf, b.ssinf = scosf, ssinf
                b.sropa, b.sropb = sropa, sropb
                b.srph, b.srpl, b.sroh, b.srol = srph, srpl, sroh, srol
                b.skch, b.skcl, b.sOh, b.sOl = skch, skcl, sOh, sOl
                b.free_srp = ()
                b.free_sro = ()
                b.free_skc = ()
                emit_b2(em, b)
                em.fence()
                em.flush()
        if "C" in phases:
            with nc.sbuf_tensor("shhh2", [128, NHC * 512], BF16) as shhh, \
                 nc.sbuf_tensor("shhl2", [128, NHC * 512], BF16) as shhl, \
                 nc.sbuf_tensor("ssdbC", [128, D], F32) as ssdb:
                b.shhh, b.shhl = shhh, shhl
                b.ssdb = ssdb
                emit_ffn(em, b, b.resid, b.y, convert=False)
                em.fence()
                em.flush()
    return nc



def make_consts():
    ident = np.eye(128, dtype=ml_dtypes.bfloat16)
    half = 32
    inv_freq = 1.0 / 10000.0 ** (np.arange(0, half, 2, dtype=np.float64) / half)
    t = np.arange(S, dtype=np.float64) / 40.0
    rot = np.concatenate([np.outer(t, inv_freq)] * 2, axis=1)
    twopi = 2.0 * np.pi
    angS = rot - twopi * np.round(rot / twopi)
    angC = (rot + np.pi / 2)
    angC = angC - twopi * np.round(angC / twopi)
    ang = np.concatenate([angS, angC], axis=1)
    ang_hi = ang.astype(np.float32)
    ang_lo = (ang - ang_hi.astype(np.float64)).astype(np.float32)
    angf = np.zeros((2, 64, S), np.float64)
    angf[0, :32, :] = angS.T
    angf[1, :32, :] = angC.T
    angf[1, 32:, :] = angC_pass = np.pi / 2
    angfS = np.concatenate([angf[0]] * 2, axis=0)
    angfC = np.concatenate([angf[1]] * 2, axis=0)
    angf_all = np.concatenate([angfS, angfC], axis=1)
    angf_hi = angf_all.astype(np.float32)
    angf_lo = (angf_all - angf_hi.astype(np.float64)).astype(np.float32)
    perm = np.zeros((128, 128), np.float32)
    for blk in range(2):
        o = blk * 64
        for p in range(16):
            perm[o + p + 16, o + p] = -1.0
        for p in range(16, 32):
            perm[o + p - 16, o + p] = 1.0
        for p in range(32, 64):
            perm[o + p, o + p] = 1.0
    perm = perm.astype(ml_dtypes.bfloat16)
    return ident, ang_hi, ang_lo, angf_hi, angf_lo, perm


_nc_cache = {}


def get_nc(debug=False):
    if debug not in _nc_cache:
        _nc_cache[debug] = build_program(debug=debug)
    return _nc_cache[debug]


def make_in_maps(inputs):
    ident, ang_hi, ang_lo, angf_hi, angf_lo, perm = make_consts()
    sdb = np.ascontiguousarray(np.broadcast_to(
        np.asarray(inputs["L0_ffn_s_down"], np.float32)[None, :], (128, D)))
    common = dict(
        w_dkv=np.ascontiguousarray(inputs["L0_attn_W_dkv"], np.float32),
        w_dq=np.ascontiguousarray(inputs["L0_attn_W_dq"], np.float32),
        w_uk=np.ascontiguousarray(inputs["L0_attn_W_uk"], np.float32),
        w_uv=np.ascontiguousarray(inputs["L0_attn_W_uv"], np.float32),
        w_uq=np.ascontiguousarray(inputs["L0_attn_W_uq"], np.float32),
        w_kr=np.ascontiguousarray(inputs["L0_attn_W_kr"], np.float32),
        w_qr=np.ascontiguousarray(inputs["L0_attn_W_qr"], np.float32),
        w_out=np.ascontiguousarray(inputs["L0_attn_W_out"], np.float32),
        wq_gate=np.ascontiguousarray(inputs["L0_ffn_wq_gate"], np.int32),
        wq_up=np.ascontiguousarray(inputs["L0_ffn_wq_up"], np.int32),
        wq_down=np.ascontiguousarray(inputs["L0_ffn_wq_down"], np.int32),
        s_gate=np.ascontiguousarray(inputs["L0_ffn_s_gate"], np.float32),
        s_up=np.ascontiguousarray(inputs["L0_ffn_s_up"], np.float32),
        sdb=sdb, ang_hi=ang_hi, ang_lo=ang_lo, angf_hi=angf_hi,
        angf_lo=angf_lo, perm=perm, ident=ident,
    )
    x = np.asarray(inputs["L0_attn_x"], np.float32)
    return [dict(common, x=np.ascontiguousarray(x[i])) for i in range(x.shape[0])]


def kernel(**inputs):
    nc = get_nc(debug=False)
    in_maps = make_in_maps(inputs)
    res = run_bass_kernel_spmd(nc, in_maps, core_ids=list(range(8)))
    return np.stack([res.results[i]["y"] for i in range(8)], axis=0)


# version 11 (final)
# speedup vs baseline: 1.1547x; 1.1547x over previous
import contextlib
import numpy as np
import ml_dtypes
import concourse.bass as bass
import concourse.mybir as mybir
from concourse.bass_utils import run_bass_kernel_spmd

F32 = mybir.dt.float32
BF16 = mybir.dt.bfloat16
I32 = mybir.dt.int32
AF = mybir.ActivationFunctionType
ALU = mybir.AluOpType
AX = mybir.AxisListType

D = 2048
S = 2048
HID = 5632
NHC = 44
NDC = 16
T = 512
NT = S // T
NTC = T // 128
CD = 1024
NKC = CD // 128
NH = 16
EPS = 1e-5
ISQ = float(1.0 / np.sqrt(128.0))

ENGINES = ("tensor", "vector", "scalar", "gpsimd", "sync")
DMA_SEMS = ("DX", "DX2", "DW0", "DW1", "DW2", "DW3", "DWO", "DSG", "DSB", "DR", "DC", "DT", "DI", "DF", "DG0", "DG1", "DG2", "DG3", "DTA", "DTB")
ESEM = {"tensor": "T", "vector": "V", "scalar": "S", "gpsimd": "G", "sync": "Y"}
FSEM = {"tensor": "TF", "vector": "VF", "scalar": "SF", "gpsimd": "GF", "sync": "YF"}


class Em:

    def __init__(self, nc):
        self.nc = nc
        self.q = {e: [] for e in ENGINES}
        self.cnt = {}
        self.sems = {}

    def reg_sem(self, name, handle):
        self.sems[name] = handle
        self.cnt[name] = 0

    def wait(self, eng, ev):
        if ev is None:
            return
        name, val = ev
        self.q[eng].append(("w", self.sems[name], val))

    def waits(self, eng, evs):
        for ev in evs:
            self.wait(eng, ev)

    def add(self, eng, meth, *args, inc=None, incn=1, **kw):
        ev = None
        sem = None
        if inc is not None:
            self.cnt[inc] += incn
            ev = (inc, self.cnt[inc])
            sem = self.sems[inc]
        self.q[eng].append(("o", meth, args, kw, sem, incn))
        return ev

    def dma(self, eng, qname, out, in_, waits=(), **kw):
        for ev in waits:
            self.wait(eng, ev)
        return self.add(eng, "dma_start", inc=qname, incn=16, out=out, in_=in_,
                        **kw)

    def fence(self, engines=ENGINES, dmaq=DMA_SEMS):
        vals = [(ESEM[o], self.cnt[ESEM[o]]) for o in ("tensor", "vector",
                                                       "scalar", "gpsimd")]
        dws = [(qn, self.cnt[qn]) for qn in dmaq]
        for e in engines:
            for sv in vals:
                if sv[1] > 0:
                    self.wait(e, sv)
            for dw in dws:
                if dw[1] > 0:
                    self.wait(e, dw)

    def flush(self):
        qs = {e: self.q[e] for e in ENGINES}
        self.q = {e: [] for e in ENGINES}

        def replay(eng, items):
            for it in items:
                if it[0] == "w":
                    eng.wait_ge(it[1], it[2])
                else:
                    _, meth, args, kw, sem, incn = it
                    r = getattr(eng, meth)(*args, **kw)
                    if sem is not None:
                        r.then_inc(sem, incn)

        with self.nc.Block() as block:
            @block.tensor
            def _(e):
                replay(e, qs["tensor"])

            @block.vector
            def _(e):
                replay(e, qs["vector"])

            @block.scalar
            def _(e):
                replay(e, qs["scalar"])

            @block.gpsimd
            def _(e):
                replay(e, qs["gpsimd"])

            @block.sync
            def _(e):
                replay(e, qs["sync"])


class NS:
    pass


def rar(x, pat, **kw):
    if isinstance(x, bass.AP):
        return x.rearrange(pat, **kw)
    return x.ap().rearrange(pat, **kw)



def emit_xnprep_tc(em, b, src, tile, tc, xnh=None, xnl=None):
    xnh = xnh if xnh is not None else b.sxnh
    xnl = xnl if xnl is not None else b.sxnl
    st = b.sstat
    row0 = tile * T + tc * 128
    ev_x = em.dma("scalar", "DX", out=b.sxt[:], in_=src[row0:row0 + 128, :],
                  waits=b.free_sxt)
    em.wait("vector", ev_x)
    em.wait("scalar", ev_x)
    sum_ = st[:, 0:1]; sumsq = st[:, 1:2]; negm = st[:, 2:3]; msq = st[:, 3:4]
    ve = st[:, 4:5]; s0 = st[:, 5:6]; r0 = st[:, 6:7]; t1 = st[:, 7:8]
    t3 = st[:, 8:9]; rstd = st[:, 9:10]; bnr = st[:, 10:11]
    ev1 = em.add("vector", "tensor_reduce", sum_, b.sxt[:], AX.X, ALU.add, inc="V")
    em.waits("scalar", b.free_sjunk)
    ev2 = em.add("scalar", "activation", b.sjunk[:], b.sxt[:], AF.Square,
                 accum_out=sumsq, inc="S")
    em.wait("vector", ev1)
    ev3 = em.add("vector", "tensor_scalar", negm, sum_, -1.0 / D, None,
                 op0=ALU.mult, inc="V")
    em.wait("vector", ev2)
    em.wait("vector", ev3)
    ev4 = em.add("vector", "tensor_tensor", msq, negm, negm, ALU.mult, inc="V")
    ev5 = em.add("vector", "tensor_scalar", ve, sumsq, 1.0 / D, EPS,
                 op0=ALU.mult, op1=ALU.add, inc="V")
    em.wait("vector", ev4)
    em.wait("vector", ev5)
    ev6 = em.add("vector", "tensor_tensor", ve, ve, msq, ALU.subtract, inc="V")
    em.wait("scalar", ev6)
    ev7 = em.add("scalar", "activation", s0, ve, AF.Sqrt, inc="S")
    em.wait("vector", ev7)
    ev8 = em.add("vector", "reciprocal", r0, s0, inc="V")
    em.wait("vector", ev8)
    ev9 = em.add("vector", "tensor_tensor", t1, r0, r0, ALU.mult, inc="V")
    em.wait("vector", ev9)
    ev10 = em.add("vector", "tensor_tensor", t1, t1, ve, ALU.mult, inc="V")
    em.wait("vector", ev10)
    ev11 = em.add("vector", "tensor_scalar", t3, t1, -0.5, 1.5,
                  op0=ALU.mult, op1=ALU.add, inc="V")
    em.wait("vector", ev11)
    ev12 = em.add("vector", "tensor_tensor", rstd, r0, t3, ALU.mult, inc="V")
    em.wait("vector", ev12)
    ev13 = em.add("vector", "tensor_tensor", bnr, negm, rstd, ALU.mult, inc="V")
    em.wait("scalar", ev13)
    em.waits("scalar", b.free_sxtm)
    evh = em.add("scalar", "activation", b.sxtmh[:], b.sxt[:], AF.Identity,
                 bias=bnr, scale=rstd, inc="S")
    em.wait("vector", ev2)
    evj = em.add("vector", "tensor_scalar", b.sjunk[:], b.sxt[:], negm, rstd,
                 op0=ALU.add, op1=ALU.mult, inc="V")
    em.wait("vector", evh)
    em.wait("vector", evj)
    evl = em.add("vector", "tensor_tensor", b.sxtml[:], b.sjunk[:], b.sxtmh[:],
                 ALU.subtract, inc="V")
    b.free_sjunk = [evl]
    b.free_sxt = [evj, evh]
    evta = em.dma("scalar", "DTA",
                  out=rar(xnh, "p (c t) -> p c t", t=512)[:, :, tc * 128:(tc + 1) * 128],
                  in_=rar(b.sxtmh, "p (c f) -> p c f", f=128),
                  waits=[evh] + list(b.xn_free), transpose=True)
    evtb = em.dma("scalar", "DTB",
                  out=rar(xnl, "p (c t) -> p c t", t=512)[:, :, tc * 128:(tc + 1) * 128],
                  in_=rar(b.sxtml, "p (c f) -> p c f", f=128),
                  waits=[evl] + list(b.xn_free), transpose=True)
    b.free_sxtm = [evta, evtb]
    return [evta, evtb]


def emit_xnprep(em, b, src, tile):
    for tc in range(NTC):
        b.xn_ready = emit_xnprep_tc(em, b, src, tile, tc)
    b.xn_free = ()



def emit_wconv(em, b, in_ap, win, win_free, cache_slab, wsem, tsem):
    ev_i = em.dma("sync", "DI", out=b.sint[:], in_=in_ap, waits=b.free_sint)
    em.wait("vector", ev_i)
    em.waits("vector", b.free_sconv)
    evc = em.add("vector", "tensor_scalar", b.sconv[:], b.sint[:], -128.0, None,
                 op0=ALU.add, inc="V")
    b.free_sint = [evc]
    ev_t = em.dma("sync", tsem,
                  out=rar(win, "p (c f) -> p c f", f=128),
                  in_=rar(b.sconv, "p (c f) -> p c f", f=128),
                  waits=list(win_free or ()) + [evc], transpose=True)
    b.free_sconv = [ev_t]
    ev_st = em.dma("gpsimd", wsem, out=cache_slab, in_=win[:, :], waits=[ev_t])
    return ev_t, ev_st



PF = 2


def emit_ffn(em, b, src_, dst, convert):
    ev_sdb = em.dma("sync", "DT", out=b.ssdb[:], in_=b.sdb_in[:, :])
    for tile in range(NT):
        if tile == 0:
            emit_xnprep(em, b, src_, 0)
        win_free = b.winf
        prev_sil = None
        prev_h32 = None
        prev_lo = None
        for hc in range(NHC):
            wg = b.win[hc % 2]
            wu = b.win[2 + hc % 2]
            if convert and tile == 0:
                ev_wg_d, st_g = emit_wconv(
                    em, b, b.wq_gate[hc * 128:(hc + 1) * 128, :], wg,
                    win_free[hc % 2], b.wgT_c[hc * 128:(hc + 1) * 128, :],
                    f"DG{hc % 2}", f"DW{hc % 2}")
                ev_wu_d, st_u = emit_wconv(
                    em, b, b.wq_up[hc * 128:(hc + 1) * 128, :], wu,
                    win_free[2 + hc % 2], b.wuT_c[hc * 128:(hc + 1) * 128, :],
                    f"DG{2 + hc % 2}", f"DW{2 + hc % 2}")
            else:
                st_g = st_u = None
                if b.pref:
                    ev_wg_d, ev_wu_d = b.pref.pop(0)
                else:
                    ev_wg_d = em.dma("sync", f"DW{hc % 2}", out=wg[:, :],
                                     in_=b.wgT_c[hc * 128:(hc + 1) * 128, :],
                                     waits=win_free[hc % 2])
                    ev_wu_d = em.dma("sync", f"DW{2 + hc % 2}", out=wu[:, :],
                                     in_=b.wuT_c[hc * 128:(hc + 1) * 128, :],
                                     waits=win_free[2 + hc % 2])
            em.wait("tensor", ev_wg_d)
            if hc == 0:
                em.waits("tensor", b.xn_ready)
                em.waits("tensor", b.ps_free)
            if prev_sil is not None:
                em.wait("tensor", prev_sil)
            ev_g = None
            for pi, xn in enumerate((b.sxnh, b.sxnl)):
                for dc in range(NDC):
                    last = (pi == 1 and dc == NDC - 1)
                    ev_g = em.add("tensor", "matmul", b.pA[:, :],
                                  wg[:, dc * 128:(dc + 1) * 128],
                                  xn[:, dc * 512:(dc + 1) * 512],
                                  start=(pi == 0 and dc == 0), stop=last,
                                  inc=("T" if last else None))
            em.wait("tensor", ev_wu_d)
            if prev_h32 is not None:
                em.wait("tensor", prev_h32)
            ev_u = None
            for pi, xn in enumerate((b.sxnh, b.sxnl)):
                for dc in range(NDC):
                    last = (pi == 1 and dc == NDC - 1)
                    ev_u = em.add("tensor", "matmul", b.pB[:, :],
                                  wu[:, dc * 128:(dc + 1) * 128],
                                  xn[:, dc * 512:(dc + 1) * 512],
                                  start=(pi == 0 and dc == 0), stop=last,
                                  inc=("T" if last else None))
            em.wait("scalar", ev_g)
            if prev_h32 is not None:
                em.wait("scalar", prev_h32)
            evs = em.add("scalar", "activation", b.ssil[:], b.pA[:, :], AF.Silu,
                         scale=b.sscg[:, hc:hc + 1], inc="S")
            prev_sil = evs
            win_free[hc % 2] = [ev_g] + ([st_g] if st_g else [])
            if st_g is None and hc == NHC - 1:
                pass
            em.wait("vector", ev_u)
            em.wait("vector", evs)
            if prev_lo is not None:
                em.wait("vector", prev_lo)
            evm = em.add("vector", "scalar_tensor_tensor", b.shh32[:], b.pB[:, :],
                         b.sscu[:, hc:hc + 1], b.ssil[:],
                         op0=ALU.mult, op1=ALU.mult, inc="V")
            prev_h32 = evm
            win_free[2 + hc % 2] = [ev_u] + ([st_u] if st_u else [])
            em.wait("vector", evm)
            evhi = em.add("vector", "tensor_copy",
                          b.shhh[:, hc * 512:(hc + 1) * 512], b.shh32[:], inc="V")
            em.wait("vector", evhi)
            prev_lo = em.add("vector", "tensor_tensor",
                             b.shhl[:, hc * 512:(hc + 1) * 512], b.shh32[:],
                             b.shhh[:, hc * 512:(hc + 1) * 512], ALU.subtract,
                             inc="V")
        hh_ready = [prev_lo, prev_sil, prev_h32]
        b.xn_free = [("T", em.cnt["T"])]
        first_sweep_mm = True
        for sweep in ((0, 1), (2, 3)):
            wd_free = b.winf
            psets = {sweep[0]: [b.pdown[:, dg * 512:(dg + 1) * 512]
                                for dg in range(4)],
                     sweep[1]: [b.pA[:, :], b.pB[:, :], b.pC[:, :], b.pD[:, :]]}
            ev_d = None
            for hc in range(NHC):
                wd = b.win[hc % 4]
                if convert and tile == 0 and sweep[0] == 0:
                    in_ap = rar(b.wq_down[:, hc * 128:(hc + 1) * 128],
                                "(c p) f -> p c f", p=128)
                    ev_wd, st_d = emit_wconv(
                        em, b, in_ap, wd, wd_free[hc % 4],
                        b.wdT_c[hc * 128:(hc + 1) * 128, :],
                        f"DG{hc % 4}", f"DW{hc % 4}")
                else:
                    st_d = None
                    ev_wd = em.dma("sync", f"DW{hc % 4}", out=wd[:, :],
                                   in_=b.wdT_c[hc * 128:(hc + 1) * 128, :],
                                   waits=wd_free[hc % 4])
                em.wait("tensor", ev_wd)
                if first_sweep_mm:
                    em.waits("tensor", hh_ready)
                    em.waits("tensor", b.ps_free)
                    first_sweep_mm = False
                for tc in sweep:
                    for pi, hh in enumerate((b.shhh, b.shhl)):
                        lhsT = hh[:, hc * 512 + tc * 128:hc * 512 + (tc + 1) * 128]
                        for dg in range(4):
                            em.add("tensor", "matmul", psets[tc][dg], lhsT,
                                   wd[:, dg * 512:(dg + 1) * 512],
                                   start=(hc == 0 and pi == 0),
                                   stop=(hc == NHC - 1 and pi == 1),
                                   inc=("T" if (tc == sweep[-1] and pi == 1
                                                and dg == 3) else None))
                ev_d = ("T", em.cnt["T"])
                wd_free[hc % 4] = [ev_d] + ([st_d] if st_d else [])
            for tc in sweep:
                row0 = tile * T + tc * 128
                ev_x2 = em.dma("scalar", "DX2", out=b.sxt2[:],
                               in_=src_[row0:row0 + 128, :], waits=b.free_sxt2)
                em.wait("vector", ev_d)
                em.wait("vector", ev_x2)
                em.wait("vector", ev_sdb)
                em.waits("vector", b.free_sjunk)
                if tc == sweep[0]:
                    evt_ = em.add("vector", "tensor_tensor", b.sjunk[:],
                                  b.pdown[:, :], b.ssdb[:], ALU.mult, inc="V")
                else:
                    for dg in range(4):
                        evt_ = em.add("vector", "tensor_tensor",
                                      b.sjunk[:, dg * 512:(dg + 1) * 512],
                                      psets[tc][dg],
                                      b.ssdb[:, dg * 512:(dg + 1) * 512],
                                      ALU.mult, inc="V")
                em.wait("vector", evt_)
                em.waits("vector", b.free_sres)
                evr = em.add("vector", "scalar_tensor_tensor", b.sres[:],
                             b.sjunk[:], 0.5, b.sxt2[:],
                             op0=ALU.mult, op1=ALU.add, inc="V")
                b.free_sjunk = [evr]
                b.free_sxt2 = [evr]
                em.wait("tensor", evt_)
                b.ps_free = [evt_]
                em.dma("gpsimd", "DR", out=dst[row0:row0 + 128, :], in_=b.sres[:],
                       waits=[evr])
                b.free_sres = [("DR", em.cnt["DR"])]
            if convert and tile == 0 and sweep[0] == 0:
                em.fence()
            if sweep[0] == 0 and tile + 1 < NT:
                emit_xnprep(em, b, src_, tile + 1)
        if tile + 1 < NT:
            for hc in range(PF):
                ev_wg_d = em.dma("sync", f"DW{hc % 2}", out=b.win[hc % 2][:, :],
                                 in_=b.wgT_c[hc * 128:(hc + 1) * 128, :],
                                 waits=b.winf[hc % 2])
                ev_wu_d = em.dma("sync", f"DW{2 + hc % 2}",
                                 out=b.win[2 + hc % 2][:, :],
                                 in_=b.wuT_c[hc * 128:(hc + 1) * 128, :],
                                 waits=b.winf[2 + hc % 2])
                b.winf[hc % 2] = [ev_wg_d]
                b.winf[2 + hc % 2] = [ev_wu_d]
                b.pref.append((ev_wg_d, ev_wu_d))
    em.fence()



def emit_wsplit(em, b, w_dram, cache_h, cache_l, nrow, ncol):
    for rc in range(nrow // 128):
        par = rc % 2
        ldb = b.sres if par == 0 else b.sjunk
        ld_free = b.free_sres if par == 0 else b.free_sjunk
        stg = b.sstga if par == 0 else b.sstgb
        stg_free = b.free_stga if par == 0 else b.free_stgb
        ev = em.dma("sync", "DF" if par == 0 else "DI", out=ldb[:, :ncol],
                    in_=w_dram[rc * 128:(rc + 1) * 128, :], waits=ld_free)
        em.wait("scalar" if par == 0 else "vector", ev)
        if par == 0:
            em.waits("scalar", stg_free)
            evh = em.add("scalar", "activation", stg[:, 0:1024], ldb[:, 0:1024],
                         AF.Identity, inc="S")
            if ncol > 1024:
                evh = em.add("scalar", "activation", stg[:, 1024:ncol],
                             ldb[:, 1024:ncol], AF.Identity, inc="S")
            em.wait("vector", evh)
            em.wait("vector", ev)
        else:
            em.waits("vector", stg_free)
            evh = em.add("vector", "tensor_copy", stg[:, 0:ncol], ldb[:, :ncol],
                         inc="V")
            em.wait("vector", evh)
        evl = em.add("vector", "tensor_tensor", stg[:, 2048:2048 + ncol],
                     ldb[:, :ncol], stg[:, 0:ncol], ALU.subtract, inc="V")
        if par == 0:
            b.free_sres = [evl]
        else:
            b.free_sjunk = [evl]
        em.dma("gpsimd", "DSG" if par == 0 else "DSB",
               out=cache_h[rc * 128:(rc + 1) * 128, :], in_=stg[:, 0:ncol],
               waits=[evh])
        em.dma("gpsimd", "DSG" if par == 0 else "DSB",
               out=cache_l[rc * 128:(rc + 1) * 128, :],
               in_=stg[:, 2048:2048 + ncol], waits=[evl])
        if par == 0:
            b.free_stga = [("DSG", em.cnt["DSG"])]
        else:
            b.free_stgb = [("DSB", em.cnt["DSB"])]


def emit_wsplit_perm(em, b, w_dram, cache_h, cache_l, nrow, ncol, blk):
    nkc = nrow // 128
    wcols = nkc * blk
    for oc in range(ncol // blk):
        par = oc % 2
        ldb = b.sres if par == 0 else b.sjunk
        ld_free = b.free_sres if par == 0 else b.free_sjunk
        stg = b.sstga if par == 0 else b.sstgb
        stg_free = b.free_stga if par == 0 else b.free_stgb
        in_ap = rar(w_dram[:, oc * blk:(oc + 1) * blk], "(c p) f -> p c f", p=128)
        out_ap = rar(ldb[:, :wcols], "p (c f) -> p c f", f=blk)
        ev = em.dma("sync", "DF" if par == 0 else "DI", out=out_ap, in_=in_ap,
                    waits=ld_free)
        em.wait("scalar" if par == 0 else "vector", ev)
        if par == 0:
            em.waits("scalar", stg_free)
            evh = em.add("scalar", "activation", stg[:, 0:1024], ldb[:, 0:1024],
                         AF.Identity, inc="S")
            if wcols > 1024:
                evh = em.add("scalar", "activation", stg[:, 1024:wcols],
                             ldb[:, 1024:wcols], AF.Identity, inc="S")
            em.wait("vector", evh)
            em.wait("vector", ev)
        else:
            em.waits("vector", stg_free)
            evh = em.add("vector", "tensor_copy", stg[:, 0:wcols], ldb[:, :wcols],
                         inc="V")
            em.wait("vector", evh)
        evl = em.add("vector", "tensor_tensor", stg[:, 2048:2048 + wcols],
                     ldb[:, :wcols], stg[:, 0:wcols], ALU.subtract, inc="V")
        if par == 0:
            b.free_sres = [evl]
        else:
            b.free_sjunk = [evl]
        em.dma("gpsimd", "DSG" if par == 0 else "DSB",
               out=cache_h[oc * 128:(oc + 1) * 128, :], in_=stg[:, 0:wcols],
               waits=[evh])
        em.dma("gpsimd", "DSG" if par == 0 else "DSB",
               out=cache_l[oc * 128:(oc + 1) * 128, :],
               in_=stg[:, 2048:2048 + wcols], waits=[evl])
        if par == 0:
            b.free_stga = [("DSG", em.cnt["DSG"])]
        else:
            b.free_stgb = [("DSB", em.cnt["DSB"])]



def emit_rope(em, b, qr_psum, out_hi, out_lo, col_base, col_stride, g, h0=0,
              nh=16):
    qr = qr_psum.rearrange("p (h j) -> p h j", j=64)
    TW = 16 * 64
    W32 = nh * 32
    sinv = bass.AP(b.stab, g * 64, [[TW, 128], [0, nh], [1, 32]])
    cosv = bass.AP(b.stab, g * 64 + 32, [[TW, 128], [0, nh], [1, 32]])
    sr = rar(b.srop, "p (h j) -> p h j", j=32)[:, :nh]
    sr2 = rar(b.srop2, "p (h j) -> p h j", j=16)
    sro = rar(b.sropo, "p (h j) -> p h j", j=32)[:, :nh]
    ev = em.add("vector", "tensor_tensor", b.srop[:, :W32], qr[:, :, 0:32], cosv,
                ALU.mult, inc="V")
    sinv_a = bass.AP(b.stab, g * 64, [[TW, 128], [0, nh], [1, 16]])
    sinv_b = bass.AP(b.stab, g * 64 + 16, [[TW, 128], [0, nh], [1, 16]])
    ev2a = em.add("vector", "tensor_tensor", sr2[:, 0:nh, :], qr[:, :, 16:32],
                  sinv_a, ALU.mult, inc="V")
    ev2b = em.add("vector", "tensor_tensor", sr2[:, 16:16 + nh, :], qr[:, :, 0:16],
                  sinv_b, ALU.mult, inc="V")
    em.wait("vector", ev)
    em.wait("vector", ev2a)
    em.add("vector", "tensor_tensor", sro[:, :, 0:16], sr[:, :, 0:16],
           sr2[:, 0:nh, :], ALU.subtract, inc="V")
    em.wait("vector", ev2b)
    ev4 = em.add("vector", "tensor_tensor", sro[:, :, 16:32], sr[:, :, 16:32],
                 sr2[:, 16:16 + nh, :], ALU.add, inc="V")
    em.wait("vector", ev4)
    oh = rar(out_hi, "p (h j) -> p h j", j=col_stride)[:, h0:h0 + nh]
    ol = rar(out_lo, "p (h j) -> p h j", j=col_stride)[:, h0:h0 + nh]
    rot_h = oh[:, :, col_base:col_base + 32]
    rot_l = ol[:, :, col_base:col_base + 32]
    ev5 = em.add("vector", "tensor_copy", rot_h, b.sropo[:, :W32], inc="V")
    em.wait("vector", ev5)
    em.add("vector", "tensor_tensor", rot_l, b.sropo[:, :W32], rot_h, ALU.subtract,
           inc="V")
    pas_h = oh[:, :, col_base + 32:col_base + 64]
    pas_l = ol[:, :, col_base + 32:col_base + 64]
    ev7 = em.add("vector", "tensor_copy", pas_h, qr[:, :, 32:64], inc="V")
    em.wait("vector", ev7)
    em.add("vector", "tensor_tensor", pas_l, qr[:, :, 32:64], pas_h, ALU.subtract,
           inc="V")
    return ("V", em.cnt["V"])



def emit_b1a(em, b):
    pfree = {0: None, 1: None}
    xnbuf = [(b.sxnh, b.sxnl), (b.sxnh2, b.sxnl2)]
    xnready = {0: None, 1: None}
    xnfree = {0: list(b.xn_free), 1: []}
    for tc in range(NTC):
        b.xn_free = xnfree[0]
        xnready[0] = emit_xnprep_tc(em, b, b.resid, 0, tc)
    for tile in range(NT):
        par_t = tile % 2
        xh, xl = xnbuf[par_t]
        win_free = b.winf
        first_round_mm = True
        for ri, (cache_h, cache_l, dst_h, dst_l) in enumerate((
                (b.cdkv_h, b.cdkv_l, b.ckv_h, b.ckv_l),
                (b.cdq_h, b.cdq_l, b.cq_h, b.cq_l))):
            for oc in range(NKC):
                if ri == 1 and tile + 1 < NT and oc % 2 == 0:
                    np_par = (tile + 1) % 2
                    b.xn_free = xnfree[np_par]
                    xnready[np_par] = emit_xnprep_tc(
                        em, b, b.resid, tile + 1, oc // 2,
                        xnbuf[np_par][0], xnbuf[np_par][1])
                par = oc % 2
                ps_ = b.pA if par == 0 else b.pB
                so = par * 512
                wh = b.win[par]
                wl = b.win[2 + par]
                evh_d = em.dma("sync", f"DW{par}", out=wh[:, :],
                               in_=cache_h[oc * 128:(oc + 1) * 128, :],
                               waits=win_free[par])
                evl_d = em.dma("sync", f"DW{2 + par}", out=wl[:, :],
                               in_=cache_l[oc * 128:(oc + 1) * 128, :],
                               waits=win_free[2 + par])
                em.wait("tensor", evh_d)
                em.wait("tensor", evl_d)
                em.wait("tensor", pfree[par])
                if first_round_mm:
                    em.waits("tensor", xnready[par_t])
                    first_round_mm = False
                ev_mm = None
                for (wv, xn) in ((wh, xh), (wh, xl), (wl, xh)):
                    for dc in range(NDC):
                        st = (wv is wh and xn is xh and dc == 0)
                        last = (wv is wl and dc == NDC - 1)
                        ev_mm = em.add("tensor", "matmul", ps_[:, :],
                                       wv[:, dc * 128:(dc + 1) * 128],
                                       xn[:, dc * 512:(dc + 1) * 512],
                                       start=st, stop=last,
                                       inc=("T" if last else None))
                em.wait("vector", ev_mm)
                em.waits("vector", b.free_stga if par == 0 else b.free_stgb)
                stg = b.sstga if par == 0 else b.sstgb
                evh = em.add("vector", "tensor_copy", stg[:, 0:512],
                             ps_[:, :], inc="V")
                em.wait("vector", evh)
                evl = em.add("vector", "tensor_tensor", stg[:, 512:1024],
                             ps_[:, :], stg[:, 0:512], ALU.subtract, inc="V")
                pfree[par] = evl
                win_free[par] = [ev_mm]
                win_free[2 + par] = [ev_mm]
                em.dma("gpsimd", "DSG" if par == 0 else "DSB",
                       out=dst_h[oc * 128:(oc + 1) * 128, tile * T:(tile + 1) * T],
                       in_=stg[:, 0:512], waits=[evh])
                em.dma("gpsimd", "DSG" if par == 0 else "DSB",
                       out=dst_l[oc * 128:(oc + 1) * 128, tile * T:(tile + 1) * T],
                       in_=stg[:, 512:1024], waits=[evl])
                if par == 0:
                    b.free_stga = [("DSG", em.cnt["DSG"])]
                else:
                    b.free_stgb = [("DSB", em.cnt["DSB"])]
        xnfree[par_t] = [("T", em.cnt["T"])]
    b.xn_free = ()
    b.xn_ready = []



def emit_b1b(em, b):
    first_m = True
    for tile in range(NT):
        for (dram_h, dram_l, sbh, sbl) in (
                (b.cq_h, b.cq_l, b.scqh, b.scql),
                (b.ckv_h, b.ckv_l, b.sckvh, b.sckvl)):
            em.dma("sync", "DT", out=rar(sbh, "p (c t) -> p c t", c=NKC),
                   in_=rar(dram_h[:, tile * T:(tile + 1) * T], "(c p) f -> p c f",
                           p=128))
            em.dma("sync", "DT", out=rar(sbl, "p (c t) -> p c t", c=NKC),
                   in_=rar(dram_l[:, tile * T:(tile + 1) * T], "(c p) f -> p c f",
                           p=128))
        em.fence()
        b.win_free = [(), (), (), ()]
        win_free = b.win_free
        qfree = {0: None, 1: None}
        for oc in range(NKC):
            par = oc % 2
            ps_ = b.pA if par == 0 else b.pB
            wh = b.win[par]
            wl = b.win[2 + par]
            evh_d = em.dma("sync", f"DW{par}", out=wh[:, 0:1024],
                           in_=b.cuq_h[oc * 128:(oc + 1) * 128, :],
                           waits=win_free[par])
            evl_d = em.dma("sync", f"DW{2 + par}", out=wl[:, 0:1024],
                           in_=b.cuq_l[oc * 128:(oc + 1) * 128, :],
                           waits=win_free[2 + par])
            em.wait("tensor", evh_d)
            em.wait("tensor", evl_d)
            em.wait("tensor", qfree[par])
            ev_mm = None
            for (wv, cq) in ((wh, b.scqh), (wh, b.scql), (wl, b.scqh)):
                for kc in range(NKC):
                    st = (wv is wh and cq is b.scqh and kc == 0)
                    last = (wv is wl and kc == NKC - 1)
                    ev_mm = em.add("tensor", "matmul", ps_[:, :],
                                   wv[:, kc * 128:(kc + 1) * 128],
                                   cq[:, kc * 512:(kc + 1) * 512],
                                   start=st, stop=last,
                                   inc=("T" if last else None))
            em.wait("vector", ev_mm)
            out_h = bass.AP(b.sqcfh, oc * 128, [[4096, 128], [1024, 4], [1, 128]])
            out_l = bass.AP(b.sqcfl, oc * 128, [[4096, 128], [1024, 4], [1, 128]])
            evh = em.add("vector", "tensor_copy", out_h, ps_[:, :], inc="V")
            em.wait("vector", evh)
            evl = em.add("vector", "tensor_tensor", out_l, ps_[:, :], out_h,
                         ALU.subtract, inc="V")
            qfree[par] = evl
            win_free[par] = [ev_mm]
            win_free[2 + par] = [ev_mm]
        em.fence()
        prev_mmm = None
        for tc in range(NTC):
            row0 = tile * T + tc * 128
            for half, (qcf, stg, Qd, tsem) in enumerate(
                    ((b.sqcfh, b.sQstgh, b.sQh, "DTA"),
                     (b.sqcfl, b.sQstgl, b.sQl, "DTB"))):
                in_ap = rar(qcf[:, tc * 1024:(tc + 1) * 1024],
                            "p (c t) -> p c t", t=128)
                evt = em.dma("sync", tsem,
                             out=rar(stg, "p (c d) -> p c d", d=128),
                             in_=in_ap, waits=b.qstg_free[half], transpose=True)
                em.wait("vector", evt)
                em.wait("vector", prev_mmm)
                out_ap = bass.AP(Qd, 0, [[2048, 128], [256, NKC], [128, 2], [1, 64]])
                in_ap2 = bass.AP(stg, 0, [[1024, 128], [128, NKC], [64, 2], [1, 64]])
                eve = em.add("vector", "tensor_copy", out_ap, in_ap2, inc="V")
                b.qstg_free[half] = [eve]
            win_free = b.win_free
            em.waits("tensor", b.qr_free)
            for oc in range(NKC):
                wh = b.win[oc % 2]
                wl = b.win[2 + oc % 2]
                evh_d = em.dma("sync", f"DW{oc % 2}", out=wh[:, 0:1024],
                               in_=b.cqr_h[oc * 128:(oc + 1) * 128, :],
                               waits=win_free[oc % 2])
                evl_d = em.dma("sync", f"DW{2 + oc % 2}", out=wl[:, 0:1024],
                               in_=b.cqr_l[oc * 128:(oc + 1) * 128, :],
                               waits=win_free[2 + oc % 2])
                em.wait("tensor", evh_d)
                em.wait("tensor", evl_d)
                for (wv, qcf) in ((wh, b.sqcfh), (wh, b.sqcfl), (wl, b.sqcfh)):
                    lhsT = qcf[:, tc * 1024 + oc * 128:tc * 1024 + (oc + 1) * 128]
                    st = (wv is wh and qcf is b.sqcfh and oc == 0)
                    for ng, png in ((0, b.pC), (1, b.pD)):
                        em.add("tensor", "matmul", png[:, :],
                               lhsT, wv[:, ng * 512:(ng + 1) * 512],
                               start=st, stop=(wv is wl and oc == NKC - 1),
                               inc=("T" if (wv is wl and ng == 1) else None))
                ev_qr = ("T", em.cnt["T"])
                win_free[oc % 2] = [ev_qr]
                win_free[2 + oc % 2] = [ev_qr]
            em.wait("vector", ev_qr)
            for half, png in ((0, b.pC), (1, b.pD)):
                ev_rope = emit_rope(em, b, png[:, :], b.sQh, b.sQl, 64, 128,
                                    tile * NTC + tc, h0=half * 8, nh=8)
            b.qr_free = [ev_rope]
            win_free = b.win_free
            for kc in range(NKC):
                wh = b.win[kc % 2]
                wl = b.win[2 + kc % 2]
                evh_d = em.dma("sync", f"DW{kc % 2}", out=wh[:, :],
                               in_=b.cuv_h[kc * 128:(kc + 1) * 128, :],
                               waits=win_free[kc % 2])
                evl_d = em.dma("sync", f"DW{2 + kc % 2}", out=wl[:, :],
                               in_=b.cuv_l[kc * 128:(kc + 1) * 128, :],
                               waits=win_free[2 + kc % 2])
                em.wait("tensor", evh_d)
                em.wait("tensor", evl_d)
                for (wv, ckv) in ((wh, b.sckvh), (wh, b.sckvl), (wl, b.sckvh)):
                    lhsT = ckv[:, kc * 512 + tc * 128:kc * 512 + (tc + 1) * 128]
                    st = (wv is wh and ckv is b.sckvh and kc == 0)
                    for vg in range(4):
                        em.add("tensor", "matmul",
                               b.pdown[:, vg * 512:(vg + 1) * 512],
                               lhsT, wv[:, vg * 512:(vg + 1) * 512],
                               start=st, stop=(wv is wl and kc == NKC - 1),
                               inc=("T" if (wv is wl and vg == 3) else None))
                ev_v = ("T", em.cnt["T"])
                win_free[kc % 2] = [ev_v]
                win_free[2 + kc % 2] = [ev_v]
            em.wait("vector", ev_v)
            em.wait("vector", prev_mmm)
            evh = em.add("vector", "tensor_copy", b.sVh[:], b.pdown[:, :], inc="V")
            em.wait("vector", evh)
            evl = em.add("vector", "tensor_tensor", b.sVl[:], b.pdown[:, :],
                         b.sVh[:], ALU.subtract, inc="V")
            em.wait("tensor", evl)
            em.wait("tensor", ev_rope)
            for h in range(NH):
                for (Qx, Vx) in ((b.sQh, b.sVh), (b.sQl, b.sVh), (b.sQh, b.sVl)):
                    st = (h % 4 == 0 and Qx is b.sQh and Vx is b.sVh)
                    sp = (h % 4 == 3 and Qx is b.sQh and Vx is b.sVl)
                    em.add("tensor", "matmul",
                           b.pdown[:, h * 128:(h + 1) * 128],
                           Qx[:, h * 128:(h + 1) * 128],
                           Vx[:, h * 128:(h + 1) * 128],
                           start=st, stop=sp,
                           inc=("T" if sp else None))
            ev_mmm = ("T", em.cnt["T"])
            prev_mmm = ev_mmm
            em.wait("vector", ev_mmm)
            if first_m:
                evm = em.add("vector", "tensor_copy", b.sMf[:], b.pdown[:, :],
                             inc="V")
                first_m = False
            else:
                evm = em.add("vector", "tensor_tensor", b.sMf[:], b.pdown[:, :],
                             b.sMf[:], ALU.add, inc="V")
            em.wait("tensor", evm)
        em.fence()



def emit_b2(em, b):
    prev = None
    for part, dst in ((0, b.ssinf), (1, b.scosf)):
        evh = em.dma("sync", "DC", out=b.sjunk[:],
                     in_=b.angf_hi[:, part * S:(part + 1) * S],
                     waits=([prev] if prev else []) + list(b.free_sjunk or ()))
        evl = em.dma("sync", "DC", out=b.sres[:],
                     in_=b.angf_lo[:, part * S:(part + 1) * S],
                     waits=b.free_sres)
        em.wait("vector", evl)
        eva = em.add("vector", "tensor_tensor", dst[:], b.sjunk[:], b.sres[:],
                     ALU.add, inc="V")
        b.free_sjunk = [eva]
        b.free_sres = [eva]
        em.wait("scalar", eva)
        prev = em.add("scalar", "activation", dst[:], dst[:], AF.Sin, inc="S")
    ev1 = em.add("vector", "tensor_scalar", b.sMf[:], b.sMf[:], ISQ, None,
                 op0=ALU.mult, inc="V")
    em.wait("vector", ev1)
    ev2 = em.add("vector", "tensor_copy", b.sMh[:], b.sMf[:], inc="V")
    em.wait("vector", ev2)
    ev3 = em.add("vector", "tensor_tensor", b.sMl[:], b.sMf[:], b.sMh[:],
                 ALU.subtract, inc="V")
    for (srcM, dstM, w) in ((b.sMh, b.sMsh, ev2), (b.sMl, b.sMsl, ev3)):
        em.dma("sync", "DT", out=dstM[0:64, :], in_=srcM[64:128, :], waits=[w])
        em.dma("sync", "DT", out=dstM[64:128, :], in_=srcM[0:64, :])
    em.fence()
    b.ps_free = []
    b.xn_free = ()
    b.ckv_free = ()
    for tile in range(NT):
        if tile == 0:
            emit_xnprep(em, b, b.resid, 0)
        ev_ckv = em.dma("sync", "DT", out=rar(b.sckvh, "p (c t) -> p c t", c=NKC),
                        in_=rar(b.ckv_h[:, tile * T:(tile + 1) * T],
                                "(c p) f -> p c f", p=128),
                        waits=b.ckv_free)
        ev_ckv = em.dma("sync", "DT", out=rar(b.sckvl, "p (c t) -> p c t", c=NKC),
                        in_=rar(b.ckv_l[:, tile * T:(tile + 1) * T],
                                "(c p) f -> p c f", p=128))
        cosli = b.scosf[:, tile * T:(tile + 1) * T]
        sinli = b.ssinf[:, tile * T:(tile + 1) * T]
        win_free = b.winf
        bank = {0: (b.pA[:, :], b.pB[:, :], b.pC[:, :], b.pD[:, :]),
                1: (b.pdown[:, 0:512], b.pdown[:, 512:1024],
                    b.pdown[:, 1024:1536], b.pdown[:, 1536:2048])}
        bfree = {0: {"kc": None, "kr": None, "sh": None, "o": None},
                 1: {"kc": None, "kr": None, "sh": None, "o": None}}
        sfree = {0: {"rp": (), "ro": (), "kc": ()}, 1: {"rp": (), "ro": (), "kc": ()}}
        for c in range(NKC):
            par = c % 2
            pKC, pKR, pSH, pO = bank[par]
            srph = b.srph[:, par * 512:(par + 1) * 512]
            srpl = b.srpl[:, par * 512:(par + 1) * 512]
            sroh = b.sroh[:, par * 512:(par + 1) * 512]
            srol = b.srol[:, par * 512:(par + 1) * 512]
            skch = b.skch[:, par * 512:(par + 1) * 512]
            skcl = b.skcl[:, par * 512:(par + 1) * 512]
            w0 = b.win[par]
            w1 = b.win[2 + par]
            em.dma("sync", f"DW{par}", out=w0[:, 0:1024],
                   in_=b.cuk_h[c * 128:(c + 1) * 128, :], waits=win_free[par])
            e_uk = em.dma("sync", f"DW{par}", out=w0[:, 1024:2048],
                          in_=b.cuk_l[c * 128:(c + 1) * 128, :])
            e_kh = em.dma("sync", f"DW{2 + par}", out=w1[:, :],
                          in_=b.ckr_h[c * 128:(c + 1) * 128, :],
                          waits=win_free[2 + par])
            w2 = b.win[2 + (1 - par)]
            e_kl = em.dma("sync", f"DW{2 + (1 - par)}", out=w2[:, :],
                          in_=b.ckr_l[c * 128:(c + 1) * 128, :],
                          waits=win_free[2 + (1 - par)])
            em.wait("tensor", e_kh)
            em.wait("tensor", e_kl)
            if c == 0:
                em.waits("tensor", b.xn_ready)
                em.waits("tensor", b.ps_free)
                em.wait("tensor", ev_ckv)
            em.wait("tensor", bfree[par]["kr"])
            ev_kr = None
            for (wv, xn) in ((w1, b.sxnh), (w1, b.sxnl), (w2, b.sxnh)):
                for dc in range(NDC):
                    st = (wv is w1 and xn is b.sxnh and dc == 0)
                    last = (wv is w2 and dc == NDC - 1)
                    ev_kr = em.add("tensor", "matmul", pKR,
                                   wv[:, dc * 128:(dc + 1) * 128],
                                   xn[:, dc * 512:(dc + 1) * 512],
                                   start=st, stop=last,
                                   inc=("T" if last else None))
            win_free[2 + par] = [ev_kr]
            win_free[2 + (1 - par)] = [ev_kr]
            em.wait("tensor", e_uk)
            em.wait("tensor", bfree[par]["kc"])
            ev_kc = None
            for (woff, ckv) in ((0, b.sckvh), (0, b.sckvl), (1024, b.sckvh)):
                for kc in range(NKC):
                    st = (woff == 0 and ckv is b.sckvh and kc == 0)
                    last = (woff == 1024 and kc == NKC - 1)
                    ev_kc = em.add("tensor", "matmul", pKC,
                                   w0[:, woff + kc * 128:woff + (kc + 1) * 128],
                                   ckv[:, kc * 512:(kc + 1) * 512],
                                   start=st, stop=last,
                                   inc=("T" if last else None))
            win_free[par] = [ev_kc]
            em.wait("vector", ev_kr)
            em.waits("vector", sfree[par]["rp"])
            e_h = em.add("vector", "tensor_copy", srph, pKR, inc="V")
            em.wait("vector", e_h)
            e_l = em.add("vector", "tensor_tensor", srpl, pKR, srph,
                         ALU.subtract, inc="V")
            em.wait("tensor", e_l)
            em.wait("tensor", bfree[par]["sh"])
            em.add("tensor", "matmul", pSH, b.sperm[:], srph,
                   start=True, stop=False)
            ev_sh = em.add("tensor", "matmul", pSH, b.sperm[:], srpl,
                           start=False, stop=True, inc="T")
            sfree[par]["rp"] = [ev_sh]
            em.wait("vector", ev_sh)
            r1 = em.add("vector", "tensor_tensor", b.sropa[:], pKR, cosli,
                        ALU.mult, inc="V")
            bfree[par]["kr"] = r1
            em.wait("vector", r1)
            r2 = em.add("vector", "tensor_tensor", b.sropb[:], pSH, sinli,
                        ALU.mult, inc="V")
            bfree[par]["sh"] = r2
            em.wait("vector", r2)
            r3 = em.add("vector", "tensor_tensor", b.sropa[:], b.sropa[:],
                        b.sropb[:], ALU.add, inc="V")
            em.wait("vector", r3)
            em.waits("vector", sfree[par]["ro"])
            g1 = em.add("vector", "tensor_copy", sroh, b.sropa[:], inc="V")
            em.wait("vector", g1)
            g2 = em.add("vector", "tensor_tensor", srol, b.sropa[:], sroh,
                        ALU.subtract, inc="V")
            em.wait("vector", ev_kc)
            em.waits("vector", sfree[par]["kc"])
            c1 = em.add("vector", "tensor_copy", skch, pKC, inc="V")
            em.wait("vector", c1)
            c2 = em.add("vector", "tensor_tensor", skcl, pKC, skch,
                        ALU.subtract, inc="V")
            bfree[par]["kc"] = c2
            for hp in range(2):
                h = 2 * c + hp
                base = hp * 64
                if hp == 0:
                    mtop = (b.sMh, b.sMl)
                    mbot = (b.sMsh, b.sMsl)
                else:
                    mtop = (b.sMsh, b.sMsl)
                    mbot = (b.sMh, b.sMl)
                em.wait("tensor", c2)
                em.wait("tensor", g2)
                em.wait("tensor", bfree[par]["o"])
                ev_o = None
                first = True
                for part, (Mt, rh, rl) in (
                        (0, ((mtop), skch, skcl)),
                        (1, ((mbot), sroh, srol))):
                    Mh_, Ml_ = Mt
                    for (Mx, rx) in ((Mh_, rh), (Ml_, rh), (Mh_, rl)):
                        ev_o = em.add(
                            "tensor", "matmul", pO,
                            Mx[base:base + 64, h * 128:(h + 1) * 128],
                            rx[base:base + 64, :],
                            start=first, stop=(part == 1 and Mx is Mh_ and rx is rl),
                            inc=("T" if (part == 1 and Mx is Mh_ and rx is rl)
                                 else None))
                        first = False
                em.wait("vector", ev_o)
                eh = em.add("vector", "tensor_copy",
                            b.sOh[:, h * 512:(h + 1) * 512], pO, inc="V")
                em.wait("vector", eh)
                el = em.add("vector", "tensor_tensor",
                            b.sOl[:, h * 512:(h + 1) * 512], pO,
                            b.sOh[:, h * 512:(h + 1) * 512], ALU.subtract,
                            inc="V")
                bfree[par]["o"] = el
            sfree[par]["ro"] = [("T", em.cnt["T"])]
            sfree[par]["kc"] = [("T", em.cnt["T"])]
        b.xn_free = [("T", em.cnt["T"])]
        b.ckv_free = [("T", em.cnt["T"])]
        if tile + 1 < NT:
            emit_xnprep(em, b, b.resid, tile + 1)
        o_ready = [bfree[0]["o"], bfree[1]["o"], bfree[0]["kc"], bfree[1]["kc"],
                   bfree[0]["kr"], bfree[1]["kr"], bfree[0]["sh"], bfree[1]["sh"]]
        first_op = True
        for sweep in ((0, 1), (2, 3)):
            psets = {sweep[0]: [b.pdown[:, dg * 512:(dg + 1) * 512]
                                for dg in range(4)],
                     sweep[1]: [b.pA[:, :], b.pB[:, :], b.pC[:, :], b.pD[:, :]]}
            ev_mm = None
            for cc in range(NDC):
                par = cc % 2
                woh = b.win[par]
                wol = b.win[2 + par]
                evh_d = em.dma("sync", f"DW{par}", out=woh[:, :],
                               in_=b.cout_h[cc * 128:(cc + 1) * 128, :],
                               waits=win_free[par])
                evl_d = em.dma("sync", f"DW{2 + par}", out=wol[:, :],
                               in_=b.cout_l[cc * 128:(cc + 1) * 128, :],
                               waits=win_free[2 + par])
                em.wait("tensor", evh_d)
                em.wait("tensor", evl_d)
                if first_op:
                    em.waits("tensor", o_ready)
                    em.waits("tensor", b.ps_free)
                    first_op = False
                for tc in sweep:
                    for (Ox, wv) in ((b.sOh, woh), (b.sOl, woh), (b.sOh, wol)):
                        lhsT = Ox[:, cc * 512 + tc * 128:cc * 512 + (tc + 1) * 128]
                        st = (Ox is b.sOh and wv is woh and cc == 0)
                        sp = (Ox is b.sOh and wv is wol and cc == NDC - 1)
                        for dg in range(4):
                            em.add("tensor", "matmul", psets[tc][dg], lhsT,
                                   wv[:, dg * 512:(dg + 1) * 512],
                                   start=st, stop=sp,
                                   inc=("T" if (tc == sweep[-1] and Ox is b.sOh
                                                and wv is wol and dg == 3)
                                        else None))
                ev_mm = ("T", em.cnt["T"])
                win_free[par] = [ev_mm]
                win_free[2 + par] = [ev_mm]
            for tc in sweep:
                row0 = tile * T + tc * 128
                ev_x2 = em.dma("scalar", "DX2", out=b.sxt2[:],
                               in_=b.resid[row0:row0 + 128, :], waits=b.free_sxt2)
                em.wait("vector", ev_mm)
                em.wait("vector", ev_x2)
                em.waits("vector", b.free_sres)
                if tc == sweep[0]:
                    evr = em.add("vector", "tensor_tensor", b.sres[:],
                                 b.pdown[:, :], b.sxt2[:], ALU.add, inc="V")
                else:
                    for dg in range(4):
                        evr = em.add("vector", "tensor_tensor",
                                     b.sres[:, dg * 512:(dg + 1) * 512],
                                     psets[tc][dg],
                                     b.sxt2[:, dg * 512:(dg + 1) * 512],
                                     ALU.add, inc="V")
                b.free_sxt2 = [evr]
                em.wait("tensor", evr)
                b.ps_free = [evr]
                em.dma("gpsimd", "DR", out=b.resid[row0:row0 + 128, :],
                       in_=b.sres[:], waits=[evr])
                b.free_sres = [("DR", em.cnt["DR"])]
    em.fence()



ALL_PHASES = ("P0", "A", "B1a", "B1b", "B2", "C")


def build_program(debug=False, phases=ALL_PHASES):
    nc = bass.Bass("TRN2", target_bir_lowering=False)
    b = NS()

    def din(name, shape, dt=F32):
        return nc.dram_tensor(name, shape, dt, kind="ExternalInput").ap()

    b.x_in = din("x", [S, D])
    b.w_dkv = din("w_dkv", [D, CD]); b.w_dq = din("w_dq", [D, CD])
    b.w_uk = din("w_uk", [CD, CD]); b.w_uv = din("w_uv", [CD, D])
    b.w_uq = din("w_uq", [CD, CD]); b.w_kr = din("w_kr", [D, CD])
    b.w_qr = din("w_qr", [CD, CD]); b.w_out = din("w_out", [D, D])
    b.wq_gate = din("wq_gate", [HID, D], I32)
    b.wq_up = din("wq_up", [HID, D], I32)
    b.wq_down = din("wq_down", [D, HID], I32)
    b.s_gate = din("s_gate", [HID]); b.s_up = din("s_up", [HID])
    b.sdb_in = din("sdb", [128, D])
    b.ang_hi = din("ang_hi", [S, 64]); b.ang_lo = din("ang_lo", [S, 64])
    b.angf_hi = din("angf_hi", [128, 2 * S]); b.angf_lo = din("angf_lo", [128, 2 * S])
    b.perm_in = din("perm", [128, 128], BF16)
    b.ident_in = din("ident", [128, 128], BF16)
    b.y = nc.dram_tensor("y", [S, D], F32, kind="ExternalOutput").ap()

    def scr(name, shape, dt, dbgk=False):
        kind = "ExternalOutput" if (dbgk and debug) else "Internal"
        return nc.dram_tensor(name, shape, dt, kind=kind).ap()

    b.resid = scr("resid", [S, D], F32, dbgk=True)
    b.wgT_c = scr("wgT_c", [NHC * 128, D], BF16)
    b.wuT_c = scr("wuT_c", [NHC * 128, D], BF16)
    b.wdT_c = scr("wdT_c", [NHC * 128, D], BF16)
    b.cdkv_h = scr("cdkv_h", [CD, D], BF16); b.cdkv_l = scr("cdkv_l", [CD, D], BF16)
    b.cdq_h = scr("cdq_h", [CD, D], BF16); b.cdq_l = scr("cdq_l", [CD, D], BF16)
    b.cuq_h = scr("cuq_h", [CD, CD], BF16); b.cuq_l = scr("cuq_l", [CD, CD], BF16)
    b.cuk_h = scr("cuk_h", [CD, CD], BF16)
    b.cuk_l = scr("cuk_l", [CD, CD], BF16)
    b.ckr_h = scr("ckr_h", [CD, D], BF16); b.ckr_l = scr("ckr_l", [CD, D], BF16)
    b.cqr_h = scr("cqr_h", [CD, CD], BF16); b.cqr_l = scr("cqr_l", [CD, CD], BF16)
    b.cuv_h = scr("cuv_h", [CD, D], BF16); b.cuv_l = scr("cuv_l", [CD, D], BF16)
    b.cout_h = scr("cout_h", [D, D], BF16); b.cout_l = scr("cout_l", [D, D], BF16)
    b.ckv_h = scr("ckv_h", [CD, S], BF16, dbgk=True)
    b.ckv_l = scr("ckv_l", [CD, S], BF16)
    b.cq_h = scr("cq_h", [CD, S], BF16); b.cq_l = scr("cq_l", [CD, S], BF16)

    em = Em(nc)
    with contextlib.ExitStack() as stack:
        def sb(name, shape, dt):
            return stack.enter_context(nc.sbuf_tensor(name, shape, dt))

        def ps(name, shape, dt):
            return stack.enter_context(nc.psum_tensor(name, shape, dt))

        for sname in ("T", "V", "S", "G", "Y", "TF", "VF", "SF", "GF", "YF") + DMA_SEMS:
            em.reg_sem(sname, stack.enter_context(nc.semaphore(sname)))
        b.sxt = sb("sxt", [128, D], F32)
        b.sxt2 = sb("sxt2", [128, D], F32)
        b.sxnh = sb("sxnh", [128, NDC * 512], BF16)
        b.sxnl = sb("sxnl", [128, NDC * 512], BF16)
        b.sxtmh = sb("sxtmh", [128, D], BF16)
        b.sxtml = sb("sxtml", [128, D], BF16)
        b.sjunk = sb("sjunk", [128, D], F32)
        b.sres = sb("sres", [128, D], F32)
        b.sstat = sb("sstat", [128, 16], F32)
        b.sscg = sb("sscg", [128, NHC], F32)
        b.sscu = sb("sscu", [128, NHC], F32)
        b.sident = sb("sident", [128, 128], BF16)
        b.ssil = sb("ssil", [128, 512], F32)
        b.shh32 = sb("shh32", [128, 512], F32)
        b.win = [sb(f"win{i}", [128, 2048], BF16) for i in range(4)]
        b.stab = sb("stab", [128, 16 * 64], F32)
        b.pA = ps("pA", [128, 512], F32)
        b.pB = ps("pB", [128, 512], F32)
        b.pC = ps("pC", [128, 512], F32)
        b.pD = ps("pD", [128, 512], F32)
        b.pdown = ps("pdown", [128, 2048], F32)
        b.sperm = sb("sperm", [128, 128], BF16)
        for attr in ("free_sxt", "free_sxt2", "free_sjunk", "free_sres",
                     "free_sxtm", "free_sint", "free_sconv", "free_stga",
                     "free_stgb"):
            setattr(b, attr, ())


        b.winf = [(), (), (), ()]
        b.ps_free = []
        b.xn_free = ()
        b.xn_ready = []
        b.ckv_free = ()
        b.pref = []
        em.dma("sync", "DT", out=b.sident[:], in_=b.ident_in[:, :])
        em.dma("sync", "DT", out=b.sperm[:], in_=b.perm_in[:, :])
        em.dma("sync", "DT", out=b.sscg[:],
               in_=rar(b.s_gate, "(c p) -> p c", p=128),
               allow_slow_non_contiguous=True)
        em.dma("sync", "DT", out=b.sscu[:],
               in_=rar(b.s_up, "(c p) -> p c", p=128),
               allow_slow_non_contiguous=True)
        prev_sin = None
        for g in range(16):
            evh = em.dma("sync", "DC", out=b.sjunk[:, 0:64],
                         in_=b.ang_hi[g * 128:(g + 1) * 128, :],
                         waits=([prev_sin] if prev_sin else ()))
            evl = em.dma("sync", "DC", out=b.sjunk[:, 64:128],
                         in_=b.ang_lo[g * 128:(g + 1) * 128, :])
            em.wait("vector", evl)
            if prev_sin is not None:
                em.wait("vector", prev_sin)
            eva = em.add("vector", "tensor_tensor", b.sjunk[:, 128:192],
                         b.sjunk[:, 0:64], b.sjunk[:, 64:128], ALU.add, inc="V")
            em.wait("scalar", eva)
            prev_sin = em.add("scalar", "activation",
                              b.stab[:, g * 64:(g + 1) * 64],
                              b.sjunk[:, 128:192], AF.Sin, inc="S")
        b.free_sjunk = [prev_sin]
        if "P0" in phases:
            with nc.sbuf_tensor("sstga", [128, 4096], BF16) as sstga, \
                 nc.sbuf_tensor("sstgb", [128, 4096], BF16) as sstgb:
                b.sstga, b.sstgb = sstga, sstgb
                emit_wsplit_perm(em, b, b.w_dkv, b.cdkv_h, b.cdkv_l, D, CD, 128)
                emit_wsplit_perm(em, b, b.w_dq, b.cdq_h, b.cdq_l, D, CD, 128)
                emit_wsplit_perm(em, b, b.w_uq, b.cuq_h, b.cuq_l, CD, CD, 128)
                emit_wsplit_perm(em, b, b.w_uk, b.cuk_h, b.cuk_l, CD, CD, 128)
                emit_wsplit_perm(em, b, b.w_kr, b.ckr_h, b.ckr_l, D, CD, 128)
                emit_wsplit(em, b, b.w_qr, b.cqr_h, b.cqr_l, CD, CD)
                emit_wsplit(em, b, b.w_uv, b.cuv_h, b.cuv_l, CD, D)
                emit_wsplit(em, b, b.w_out, b.cout_h, b.cout_l, D, D)
                em.fence()
                em.flush()
        if "A" in phases:
            with nc.sbuf_tensor("shhh", [128, NHC * 512], BF16) as shhh, \
                 nc.sbuf_tensor("shhl", [128, NHC * 512], BF16) as shhl, \
                 nc.sbuf_tensor("sint", [128, 2048], I32) as sint, \
                 nc.sbuf_tensor("ssdbA", [128, D], F32) as ssdb, \
                 nc.sbuf_tensor("sconvA", [128, 2048], BF16) as sconv:
                b.shhh, b.shhl, b.sint, b.sconv = shhh, shhl, sint, sconv
                b.ssdb = ssdb
                emit_ffn(em, b, b.x_in, b.resid, convert=True)
                em.fence()
                em.flush()
        if "B1a" in phases:
            with nc.sbuf_tensor("sstga2", [128, 2048], BF16) as sstga, \
                 nc.sbuf_tensor("sstgb2", [128, 2048], BF16) as sstgb, \
                 nc.sbuf_tensor("sxnh2", [128, NDC * 512], BF16) as sxnh2, \
                 nc.sbuf_tensor("sxnl2", [128, NDC * 512], BF16) as sxnl2, \
                 nc.sbuf_tensor("skrs_h", [128, 1024], BF16) as skrs_h, \
                 nc.sbuf_tensor("skrs_l", [128, 1024], BF16) as skrs_l, \
                 nc.sbuf_tensor("scosA", [128, 512], F32) as scos, \
                 nc.sbuf_tensor("ssinA", [128, 512], F32) as ssin, \
                 nc.sbuf_tensor("sropA", [128, 512], F32) as srop, \
                 nc.sbuf_tensor("srop2A", [128, 512], F32) as srop2, \
                 nc.sbuf_tensor("sropoA", [128, 512], F32) as sropo:
                b.sstga, b.sstgb = sstga, sstgb
                b.sxnh2, b.sxnl2 = sxnh2, sxnl2
                b.skrs_h, b.skrs_l = skrs_h, skrs_l
                b.scos, b.ssin, b.srop, b.srop2, b.sropo = scos, ssin, srop, srop2, sropo
                b.free_stga = ()
                b.free_stgb = ()
                emit_b1a(em, b)
                em.fence()
                em.flush()
        if "B1b" not in phases:
            return nc
        with nc.sbuf_tensor("sckvh", [128, NKC * 512], BF16) as sckvh, \
             nc.sbuf_tensor("sckvl", [128, NKC * 512], BF16) as sckvl, \
             nc.sbuf_tensor("sMf", [128, NH * 128], F32) as sMf:
            b.sckvh, b.sckvl, b.sMf = sckvh, sckvl, sMf
            with nc.sbuf_tensor("scqh", [128, NKC * 512], BF16) as scqh, \
                 nc.sbuf_tensor("scql", [128, NKC * 512], BF16) as scql, \
                 nc.sbuf_tensor("sqcfh", [128, NKC * 512], BF16) as sqcfh, \
                 nc.sbuf_tensor("sqcfl", [128, NKC * 512], BF16) as sqcfl, \
                 nc.sbuf_tensor("sQh", [128, 2048], BF16) as sQh, \
                 nc.sbuf_tensor("sQl", [128, 2048], BF16) as sQl, \
                 nc.sbuf_tensor("sVh", [128, 2048], BF16) as sVh, \
                 nc.sbuf_tensor("sVl", [128, 2048], BF16) as sVl, \
                 nc.sbuf_tensor("sQstgh", [128, 1024], BF16) as sQstgh, \
                 nc.sbuf_tensor("sQstgl", [128, 1024], BF16) as sQstgl, \
                 nc.sbuf_tensor("sropB", [128, 512], F32) as srop, \
                 nc.sbuf_tensor("srop2B", [128, 512], F32) as srop2, \
                 nc.sbuf_tensor("sropoB", [128, 512], F32) as sropo:
                b.scqh, b.scql, b.sqcfh, b.sqcfl = scqh, scql, sqcfh, sqcfl
                b.sQh, b.sQl, b.sVh, b.sVl = sQh, sQl, sVh, sVl
                b.sQstgh, b.sQstgl = sQstgh, sQstgl
                b.qstg_free = {0: (), 1: ()}
                b.qr_free = ()
                b.srop, b.srop2, b.sropo = srop, srop2, sropo
                emit_b1b(em, b)
                em.fence()
                em.flush()
            if "B2" not in phases:
                return nc
            with nc.sbuf_tensor("sMh", [128, NH * 128], BF16) as sMh, \
                 nc.sbuf_tensor("sMl", [128, NH * 128], BF16) as sMl, \
                 nc.sbuf_tensor("sMsh", [128, NH * 128], BF16) as sMsh, \
                 nc.sbuf_tensor("sMsl", [128, NH * 128], BF16) as sMsl, \
                 nc.sbuf_tensor("scosf", [128, S], F32) as scosf, \
                 nc.sbuf_tensor("ssinf", [128, S], F32) as ssinf, \
                 nc.sbuf_tensor("sropa", [128, 512], F32) as sropa, \
                 nc.sbuf_tensor("sropb", [128, 512], F32) as sropb, \
                 nc.sbuf_tensor("srph", [128, 1024], BF16) as srph, \
                 nc.sbuf_tensor("srpl", [128, 1024], BF16) as srpl, \
                 nc.sbuf_tensor("sroh", [128, 1024], BF16) as sroh, \
                 nc.sbuf_tensor("srol", [128, 1024], BF16) as srol, \
                 nc.sbuf_tensor("skch", [128, 1024], BF16) as skch, \
                 nc.sbuf_tensor("skcl", [128, 1024], BF16) as skcl, \
                 nc.sbuf_tensor("sOh", [128, NH * 512], BF16) as sOh, \
                 nc.sbuf_tensor("sOl", [128, NH * 512], BF16) as sOl:
                b.sMh, b.sMl, b.sMsh, b.sMsl = sMh, sMl, sMsh, sMsl
                b.scosf, b.ssinf = scosf, ssinf
                b.sropa, b.sropb = sropa, sropb
                b.srph, b.srpl, b.sroh, b.srol = srph, srpl, sroh, srol
                b.skch, b.skcl, b.sOh, b.sOl = skch, skcl, sOh, sOl
                b.free_srp = ()
                b.free_sro = ()
                b.free_skc = ()
                emit_b2(em, b)
                em.fence()
                em.flush()
        if "C" in phases:
            with nc.sbuf_tensor("shhh2", [128, NHC * 512], BF16) as shhh, \
                 nc.sbuf_tensor("shhl2", [128, NHC * 512], BF16) as shhl, \
                 nc.sbuf_tensor("ssdbC", [128, D], F32) as ssdb:
                b.shhh, b.shhl = shhh, shhl
                b.ssdb = ssdb
                emit_ffn(em, b, b.resid, b.y, convert=False)
                em.fence()
                em.flush()
    return nc



def make_consts():
    ident = np.eye(128, dtype=ml_dtypes.bfloat16)
    half = 32
    inv_freq = 1.0 / 10000.0 ** (np.arange(0, half, 2, dtype=np.float64) / half)
    t = np.arange(S, dtype=np.float64) / 40.0
    rot = np.concatenate([np.outer(t, inv_freq)] * 2, axis=1)
    twopi = 2.0 * np.pi
    angS = rot - twopi * np.round(rot / twopi)
    angC = (rot + np.pi / 2)
    angC = angC - twopi * np.round(angC / twopi)
    ang = np.concatenate([angS, angC], axis=1)
    ang_hi = ang.astype(np.float32)
    ang_lo = (ang - ang_hi.astype(np.float64)).astype(np.float32)
    angf = np.zeros((2, 64, S), np.float64)
    angf[0, :32, :] = angS.T
    angf[1, :32, :] = angC.T
    angf[1, 32:, :] = angC_pass = np.pi / 2
    angfS = np.concatenate([angf[0]] * 2, axis=0)
    angfC = np.concatenate([angf[1]] * 2, axis=0)
    angf_all = np.concatenate([angfS, angfC], axis=1)
    angf_hi = angf_all.astype(np.float32)
    angf_lo = (angf_all - angf_hi.astype(np.float64)).astype(np.float32)
    perm = np.zeros((128, 128), np.float32)
    for blk in range(2):
        o = blk * 64
        for p in range(16):
            perm[o + p + 16, o + p] = -1.0
        for p in range(16, 32):
            perm[o + p - 16, o + p] = 1.0
        for p in range(32, 64):
            perm[o + p, o + p] = 1.0
    perm = perm.astype(ml_dtypes.bfloat16)
    return ident, ang_hi, ang_lo, angf_hi, angf_lo, perm


_nc_cache = {}


def get_nc(debug=False):
    if debug not in _nc_cache:
        _nc_cache[debug] = build_program(debug=debug)
    return _nc_cache[debug]


def make_in_maps(inputs):
    ident, ang_hi, ang_lo, angf_hi, angf_lo, perm = make_consts()
    sdb = np.ascontiguousarray(np.broadcast_to(
        np.asarray(inputs["L0_ffn_s_down"], np.float32)[None, :], (128, D)))
    common = dict(
        w_dkv=np.ascontiguousarray(inputs["L0_attn_W_dkv"], np.float32),
        w_dq=np.ascontiguousarray(inputs["L0_attn_W_dq"], np.float32),
        w_uk=np.ascontiguousarray(inputs["L0_attn_W_uk"], np.float32),
        w_uv=np.ascontiguousarray(inputs["L0_attn_W_uv"], np.float32),
        w_uq=np.ascontiguousarray(inputs["L0_attn_W_uq"], np.float32),
        w_kr=np.ascontiguousarray(inputs["L0_attn_W_kr"], np.float32),
        w_qr=np.ascontiguousarray(inputs["L0_attn_W_qr"], np.float32),
        w_out=np.ascontiguousarray(inputs["L0_attn_W_out"], np.float32),
        wq_gate=np.ascontiguousarray(inputs["L0_ffn_wq_gate"], np.int32),
        wq_up=np.ascontiguousarray(inputs["L0_ffn_wq_up"], np.int32),
        wq_down=np.ascontiguousarray(inputs["L0_ffn_wq_down"], np.int32),
        s_gate=np.ascontiguousarray(inputs["L0_ffn_s_gate"], np.float32),
        s_up=np.ascontiguousarray(inputs["L0_ffn_s_up"], np.float32),
        sdb=sdb, ang_hi=ang_hi, ang_lo=ang_lo, angf_hi=angf_hi,
        angf_lo=angf_lo, perm=perm, ident=ident,
    )
    x = np.asarray(inputs["L0_attn_x"], np.float32)
    return [dict(common, x=np.ascontiguousarray(x[i])) for i in range(x.shape[0])]


def kernel(**inputs):
    nc = get_nc(debug=False)
    in_maps = make_in_maps(inputs)
    res = run_bass_kernel_spmd(nc, in_maps, core_ids=list(range(8)))
    return np.stack([res.results[i]["y"] for i in range(8)], axis=0)


# version 12
# speedup vs baseline: 1.1565x; 1.0016x over previous
import contextlib
import numpy as np
import ml_dtypes
import concourse.bass as bass
import concourse.mybir as mybir
from concourse.bass_utils import run_bass_kernel_spmd

F32 = mybir.dt.float32
BF16 = mybir.dt.bfloat16
I32 = mybir.dt.int32
AF = mybir.ActivationFunctionType
ALU = mybir.AluOpType
AX = mybir.AxisListType

D = 2048
S = 2048
HID = 5632
NHC = 44
NDC = 16
T = 512
NT = S // T
NTC = T // 128
CD = 1024
NKC = CD // 128
NH = 16
EPS = 1e-5
ISQ = float(1.0 / np.sqrt(128.0))

ENGINES = ("tensor", "vector", "scalar", "gpsimd", "sync")
DMA_SEMS = ("DX", "DX2", "DW0", "DW1", "DW2", "DW3", "DWO", "DSG", "DSB", "DR", "DC", "DT", "DI", "DF", "DG0", "DG1", "DG2", "DG3", "DTA", "DTB", "DP0", "DP1")
ESEM = {"tensor": "T", "vector": "V", "scalar": "S", "gpsimd": "G", "sync": "Y"}
FSEM = {"tensor": "TF", "vector": "VF", "scalar": "SF", "gpsimd": "GF", "sync": "YF"}


class Em:

    def __init__(self, nc):
        self.nc = nc
        self.q = {e: [] for e in ENGINES}
        self.cnt = {}
        self.sems = {}

    def reg_sem(self, name, handle):
        self.sems[name] = handle
        self.cnt[name] = 0

    def wait(self, eng, ev):
        if ev is None:
            return
        name, val = ev
        self.q[eng].append(("w", self.sems[name], val))

    def waits(self, eng, evs):
        for ev in evs:
            self.wait(eng, ev)

    def add(self, eng, meth, *args, inc=None, incn=1, **kw):
        ev = None
        sem = None
        if inc is not None:
            self.cnt[inc] += incn
            ev = (inc, self.cnt[inc])
            sem = self.sems[inc]
        self.q[eng].append(("o", meth, args, kw, sem, incn))
        return ev

    def dma(self, eng, qname, out, in_, waits=(), **kw):
        for ev in waits:
            self.wait(eng, ev)
        return self.add(eng, "dma_start", inc=qname, incn=16, out=out, in_=in_,
                        **kw)

    def fence(self, engines=ENGINES, dmaq=DMA_SEMS):
        vals = [(ESEM[o], self.cnt[ESEM[o]]) for o in ("tensor", "vector",
                                                       "scalar", "gpsimd")]
        dws = [(qn, self.cnt[qn]) for qn in dmaq]
        for e in engines:
            for sv in vals:
                if sv[1] > 0:
                    self.wait(e, sv)
            for dw in dws:
                if dw[1] > 0:
                    self.wait(e, dw)

    def flush(self):
        qs = {e: self.q[e] for e in ENGINES}
        self.q = {e: [] for e in ENGINES}

        def replay(eng, items):
            for it in items:
                if it[0] == "w":
                    eng.wait_ge(it[1], it[2])
                else:
                    _, meth, args, kw, sem, incn = it
                    r = getattr(eng, meth)(*args, **kw)
                    if sem is not None:
                        r.then_inc(sem, incn)

        with self.nc.Block() as block:
            @block.tensor
            def _(e):
                replay(e, qs["tensor"])

            @block.vector
            def _(e):
                replay(e, qs["vector"])

            @block.scalar
            def _(e):
                replay(e, qs["scalar"])

            @block.gpsimd
            def _(e):
                replay(e, qs["gpsimd"])

            @block.sync
            def _(e):
                replay(e, qs["sync"])


class NS:
    pass


def rar(x, pat, **kw):
    if isinstance(x, bass.AP):
        return x.rearrange(pat, **kw)
    return x.ap().rearrange(pat, **kw)



def emit_xnprep_tc(em, b, src, tile, tc, xnh=None, xnl=None):
    xnh = xnh if xnh is not None else b.sxnh
    xnl = xnl if xnl is not None else b.sxnl
    st = b.sstat
    row0 = tile * T + tc * 128
    ev_x = em.dma("scalar", "DX", out=b.sxt[:], in_=src[row0:row0 + 128, :],
                  waits=b.free_sxt)
    em.wait("vector", ev_x)
    em.wait("scalar", ev_x)
    sum_ = st[:, 0:1]; sumsq = st[:, 1:2]; negm = st[:, 2:3]; msq = st[:, 3:4]
    ve = st[:, 4:5]; s0 = st[:, 5:6]; r0 = st[:, 6:7]; t1 = st[:, 7:8]
    t3 = st[:, 8:9]; rstd = st[:, 9:10]; bnr = st[:, 10:11]
    ev1 = em.add("vector", "tensor_reduce", sum_, b.sxt[:], AX.X, ALU.add, inc="V")
    em.waits("scalar", b.free_sjunk)
    ev2 = em.add("scalar", "activation", b.sjunk[:], b.sxt[:], AF.Square,
                 accum_out=sumsq, inc="S")
    em.wait("vector", ev1)
    ev3 = em.add("vector", "tensor_scalar", negm, sum_, -1.0 / D, None,
                 op0=ALU.mult, inc="V")
    em.wait("vector", ev2)
    em.wait("vector", ev3)
    ev4 = em.add("vector", "tensor_tensor", msq, negm, negm, ALU.mult, inc="V")
    ev5 = em.add("vector", "tensor_scalar", ve, sumsq, 1.0 / D, EPS,
                 op0=ALU.mult, op1=ALU.add, inc="V")
    em.wait("vector", ev4)
    em.wait("vector", ev5)
    ev6 = em.add("vector", "tensor_tensor", ve, ve, msq, ALU.subtract, inc="V")
    em.wait("scalar", ev6)
    ev7 = em.add("scalar", "activation", s0, ve, AF.Sqrt, inc="S")
    em.wait("vector", ev7)
    ev8 = em.add("vector", "reciprocal", r0, s0, inc="V")
    em.wait("vector", ev8)
    ev9 = em.add("vector", "tensor_tensor", t1, r0, r0, ALU.mult, inc="V")
    em.wait("vector", ev9)
    ev10 = em.add("vector", "tensor_tensor", t1, t1, ve, ALU.mult, inc="V")
    em.wait("vector", ev10)
    ev11 = em.add("vector", "tensor_scalar", t3, t1, -0.5, 1.5,
                  op0=ALU.mult, op1=ALU.add, inc="V")
    em.wait("vector", ev11)
    ev12 = em.add("vector", "tensor_tensor", rstd, r0, t3, ALU.mult, inc="V")
    em.wait("vector", ev12)
    ev13 = em.add("vector", "tensor_tensor", bnr, negm, rstd, ALU.mult, inc="V")
    em.wait("scalar", ev13)
    em.waits("scalar", b.free_sxtm)
    evh = em.add("scalar", "activation", b.sxtmh[:], b.sxt[:], AF.Identity,
                 bias=bnr, scale=rstd, inc="S")
    em.wait("vector", ev2)
    evj = em.add("vector", "tensor_scalar", b.sjunk[:], b.sxt[:], negm, rstd,
                 op0=ALU.add, op1=ALU.mult, inc="V")
    em.wait("vector", evh)
    em.wait("vector", evj)
    evl = em.add("vector", "tensor_tensor", b.sxtml[:], b.sjunk[:], b.sxtmh[:],
                 ALU.subtract, inc="V")
    b.free_sjunk = [evl]
    b.free_sxt = [evj, evh]
    evta = em.dma("scalar", "DTA",
                  out=rar(xnh, "p (c t) -> p c t", t=512)[:, :, tc * 128:(tc + 1) * 128],
                  in_=rar(b.sxtmh, "p (c f) -> p c f", f=128),
                  waits=[evh] + list(b.xn_free), transpose=True)
    evtb = em.dma("scalar", "DTB",
                  out=rar(xnl, "p (c t) -> p c t", t=512)[:, :, tc * 128:(tc + 1) * 128],
                  in_=rar(b.sxtml, "p (c f) -> p c f", f=128),
                  waits=[evl] + list(b.xn_free), transpose=True)
    b.free_sxtm = [evta, evtb]
    return [evta, evtb]


def emit_xnprep(em, b, src, tile):
    for tc in range(NTC):
        b.xn_ready = emit_xnprep_tc(em, b, src, tile, tc)
    b.xn_free = ()



def emit_wconv(em, b, in_ap, win, win_free, cache_slab, wsem, tsem):
    ev_i = em.dma("sync", "DI", out=b.sint[:], in_=in_ap, waits=b.free_sint)
    em.wait("vector", ev_i)
    em.waits("vector", b.free_sconv)
    evc = em.add("vector", "tensor_scalar", b.sconv[:], b.sint[:], -128.0, None,
                 op0=ALU.add, inc="V")
    b.free_sint = [evc]
    ev_t = em.dma("sync", tsem,
                  out=rar(win, "p (c f) -> p c f", f=128),
                  in_=rar(b.sconv, "p (c f) -> p c f", f=128),
                  waits=list(win_free or ()) + [evc], transpose=True)
    b.free_sconv = [ev_t]
    ev_st = em.dma("gpsimd", wsem, out=cache_slab, in_=win[:, :], waits=[ev_t])
    return ev_t, ev_st



PF = 2


def emit_ffn(em, b, src_, dst, convert):
    ev_sdb = em.dma("sync", "DT", out=b.ssdb[:], in_=b.sdb_in[:, :])
    for tile in range(NT):
        if tile == 0:
            emit_xnprep(em, b, src_, 0)
        win_free = b.winf
        prev_sil = None
        prev_h32 = None
        prev_lo = None
        for hc in range(NHC):
            wg = b.win[hc % 2]
            wu = b.win[2 + hc % 2]
            if convert and tile == 0:
                ev_wg_d, st_g = emit_wconv(
                    em, b, b.wq_gate[hc * 128:(hc + 1) * 128, :], wg,
                    win_free[hc % 2], b.wgT_c[hc * 128:(hc + 1) * 128, :],
                    f"DG{hc % 2}", f"DW{hc % 2}")
                ev_wu_d, st_u = emit_wconv(
                    em, b, b.wq_up[hc * 128:(hc + 1) * 128, :], wu,
                    win_free[2 + hc % 2], b.wuT_c[hc * 128:(hc + 1) * 128, :],
                    f"DG{2 + hc % 2}", f"DW{2 + hc % 2}")
            else:
                st_g = st_u = None
                if b.pref:
                    ev_wg_d, ev_wu_d = b.pref.pop(0)
                else:
                    ev_wg_d = em.dma("sync", f"DW{hc % 2}", out=wg[:, :],
                                     in_=b.wgT_c[hc * 128:(hc + 1) * 128, :],
                                     waits=win_free[hc % 2])
                    ev_wu_d = em.dma("sync", f"DW{2 + hc % 2}", out=wu[:, :],
                                     in_=b.wuT_c[hc * 128:(hc + 1) * 128, :],
                                     waits=win_free[2 + hc % 2])
            em.wait("tensor", ev_wg_d)
            if hc == 0:
                em.waits("tensor", b.xn_ready)
                em.waits("tensor", b.ps_free)
            if prev_sil is not None:
                em.wait("tensor", prev_sil)
            ev_g = None
            for pi, xn in enumerate((b.sxnh, b.sxnl)):
                for dc in range(NDC):
                    last = (pi == 1 and dc == NDC - 1)
                    ev_g = em.add("tensor", "matmul", b.pA[:, :],
                                  wg[:, dc * 128:(dc + 1) * 128],
                                  xn[:, dc * 512:(dc + 1) * 512],
                                  start=(pi == 0 and dc == 0), stop=last,
                                  inc=("T" if last else None))
            em.wait("tensor", ev_wu_d)
            if prev_h32 is not None:
                em.wait("tensor", prev_h32)
            ev_u = None
            for pi, xn in enumerate((b.sxnh, b.sxnl)):
                for dc in range(NDC):
                    last = (pi == 1 and dc == NDC - 1)
                    ev_u = em.add("tensor", "matmul", b.pB[:, :],
                                  wu[:, dc * 128:(dc + 1) * 128],
                                  xn[:, dc * 512:(dc + 1) * 512],
                                  start=(pi == 0 and dc == 0), stop=last,
                                  inc=("T" if last else None))
            em.wait("scalar", ev_g)
            if prev_h32 is not None:
                em.wait("scalar", prev_h32)
            evs = em.add("scalar", "activation", b.ssil[:], b.pA[:, :], AF.Silu,
                         scale=b.sscg[:, hc:hc + 1], inc="S")
            prev_sil = evs
            win_free[hc % 2] = [ev_g] + ([st_g] if st_g else [])
            if st_g is None and hc == NHC - 1:
                pass
            em.wait("vector", ev_u)
            em.wait("vector", evs)
            if prev_lo is not None:
                em.wait("vector", prev_lo)
            evm = em.add("vector", "scalar_tensor_tensor", b.shh32[:], b.pB[:, :],
                         b.sscu[:, hc:hc + 1], b.ssil[:],
                         op0=ALU.mult, op1=ALU.mult, inc="V")
            prev_h32 = evm
            win_free[2 + hc % 2] = [ev_u] + ([st_u] if st_u else [])
            em.wait("vector", evm)
            evhi = em.add("vector", "tensor_copy",
                          b.shhh[:, hc * 512:(hc + 1) * 512], b.shh32[:], inc="V")
            em.wait("vector", evhi)
            prev_lo = em.add("vector", "tensor_tensor",
                             b.shhl[:, hc * 512:(hc + 1) * 512], b.shh32[:],
                             b.shhh[:, hc * 512:(hc + 1) * 512], ALU.subtract,
                             inc="V")
        hh_ready = [prev_lo, prev_sil, prev_h32]
        b.xn_free = [("T", em.cnt["T"])]
        first_sweep_mm = True
        for sweep in ((0, 1), (2, 3)):
            wd_free = b.winf
            psets = {sweep[0]: [b.pdown[:, dg * 512:(dg + 1) * 512]
                                for dg in range(4)],
                     sweep[1]: [b.pA[:, :], b.pB[:, :], b.pC[:, :], b.pD[:, :]]}
            ev_d = None
            for hc in range(NHC):
                wd = b.win[hc % 4]
                if convert and tile == 0 and sweep[0] == 0:
                    in_ap = rar(b.wq_down[:, hc * 128:(hc + 1) * 128],
                                "(c p) f -> p c f", p=128)
                    ev_wd, st_d = emit_wconv(
                        em, b, in_ap, wd, wd_free[hc % 4],
                        b.wdT_c[hc * 128:(hc + 1) * 128, :],
                        f"DG{hc % 4}", f"DW{hc % 4}")
                else:
                    st_d = None
                    ev_wd = em.dma("sync", f"DW{hc % 4}", out=wd[:, :],
                                   in_=b.wdT_c[hc * 128:(hc + 1) * 128, :],
                                   waits=wd_free[hc % 4])
                em.wait("tensor", ev_wd)
                if first_sweep_mm:
                    em.waits("tensor", hh_ready)
                    em.waits("tensor", b.ps_free)
                    first_sweep_mm = False
                for tc in sweep:
                    for pi, hh in enumerate((b.shhh, b.shhl)):
                        lhsT = hh[:, hc * 512 + tc * 128:hc * 512 + (tc + 1) * 128]
                        for dg in range(4):
                            em.add("tensor", "matmul", psets[tc][dg], lhsT,
                                   wd[:, dg * 512:(dg + 1) * 512],
                                   start=(hc == 0 and pi == 0),
                                   stop=(hc == NHC - 1 and pi == 1),
                                   inc=("T" if (tc == sweep[-1] and pi == 1
                                                and dg == 3) else None))
                ev_d = ("T", em.cnt["T"])
                wd_free[hc % 4] = [ev_d] + ([st_d] if st_d else [])
            for tc in sweep:
                row0 = tile * T + tc * 128
                ev_x2 = em.dma("scalar", "DX2", out=b.sxt2[:],
                               in_=src_[row0:row0 + 128, :], waits=b.free_sxt2)
                em.wait("vector", ev_d)
                em.wait("vector", ev_x2)
                em.wait("vector", ev_sdb)
                em.waits("vector", b.free_sjunk)
                if tc == sweep[0]:
                    evt_ = em.add("vector", "tensor_tensor", b.sjunk[:],
                                  b.pdown[:, :], b.ssdb[:], ALU.mult, inc="V")
                else:
                    for dg in range(4):
                        evt_ = em.add("vector", "tensor_tensor",
                                      b.sjunk[:, dg * 512:(dg + 1) * 512],
                                      psets[tc][dg],
                                      b.ssdb[:, dg * 512:(dg + 1) * 512],
                                      ALU.mult, inc="V")
                em.wait("vector", evt_)
                em.waits("vector", b.free_sres)
                evr = em.add("vector", "scalar_tensor_tensor", b.sres[:],
                             b.sjunk[:], 0.5, b.sxt2[:],
                             op0=ALU.mult, op1=ALU.add, inc="V")
                b.free_sjunk = [evr]
                b.free_sxt2 = [evr]
                em.wait("tensor", evt_)
                b.ps_free = [evt_]
                em.dma("gpsimd", "DR", out=dst[row0:row0 + 128, :], in_=b.sres[:],
                       waits=[evr])
                b.free_sres = [("DR", em.cnt["DR"])]
            if convert and tile == 0 and sweep[0] == 0:
                em.fence()
            if sweep[0] == 0 and tile + 1 < NT:
                emit_xnprep(em, b, src_, tile + 1)
        if tile + 1 < NT:
            for hc in range(PF):
                ev_wg_d = em.dma("sync", f"DW{hc % 2}", out=b.win[hc % 2][:, :],
                                 in_=b.wgT_c[hc * 128:(hc + 1) * 128, :],
                                 waits=b.winf[hc % 2])
                ev_wu_d = em.dma("sync", f"DW{2 + hc % 2}",
                                 out=b.win[2 + hc % 2][:, :],
                                 in_=b.wuT_c[hc * 128:(hc + 1) * 128, :],
                                 waits=b.winf[2 + hc % 2])
                b.winf[hc % 2] = [ev_wg_d]
                b.winf[2 + hc % 2] = [ev_wu_d]
                b.pref.append((ev_wg_d, ev_wu_d))
    em.fence()



def emit_wsplit(em, b, w_dram, cache_h, cache_l, nrow, ncol):
    for rc in range(nrow // 128):
        par = rc % 2
        ldb = b.sres if par == 0 else b.sjunk
        ld_free = b.free_sres if par == 0 else b.free_sjunk
        stg = b.sstga if par == 0 else b.sstgb
        stg_free = b.free_stga if par == 0 else b.free_stgb
        ev = em.dma("sync", "DF" if par == 0 else "DI", out=ldb[:, :ncol],
                    in_=w_dram[rc * 128:(rc + 1) * 128, :], waits=ld_free)
        em.wait("scalar" if par == 0 else "vector", ev)
        if par == 0:
            em.waits("scalar", stg_free)
            evh = em.add("scalar", "activation", stg[:, 0:1024], ldb[:, 0:1024],
                         AF.Identity, inc="S")
            if ncol > 1024:
                evh = em.add("scalar", "activation", stg[:, 1024:ncol],
                             ldb[:, 1024:ncol], AF.Identity, inc="S")
            em.wait("vector", evh)
            em.wait("vector", ev)
        else:
            em.waits("vector", stg_free)
            evh = em.add("vector", "tensor_copy", stg[:, 0:ncol], ldb[:, :ncol],
                         inc="V")
            em.wait("vector", evh)
        evl = em.add("vector", "tensor_tensor", stg[:, 2048:2048 + ncol],
                     ldb[:, :ncol], stg[:, 0:ncol], ALU.subtract, inc="V")
        if par == 0:
            b.free_sres = [evl]
        else:
            b.free_sjunk = [evl]
        em.dma("gpsimd", "DSG" if par == 0 else "DSB",
               out=cache_h[rc * 128:(rc + 1) * 128, :], in_=stg[:, 0:ncol],
               waits=[evh])
        em.dma("gpsimd", "DSG" if par == 0 else "DSB",
               out=cache_l[rc * 128:(rc + 1) * 128, :],
               in_=stg[:, 2048:2048 + ncol], waits=[evl])
        if par == 0:
            b.free_stga = [("DSG", em.cnt["DSG"])]
        else:
            b.free_stgb = [("DSB", em.cnt["DSB"])]


def emit_wsplit_perm(em, b, w_dram, cache_h, cache_l, nrow, ncol, blk):
    nkc = nrow // 128
    wcols = nkc * blk
    for oc in range(ncol // blk):
        par = oc % 2
        ldb = b.sres if par == 0 else b.sjunk
        ld_free = b.free_sres if par == 0 else b.free_sjunk
        stg = b.sstga if par == 0 else b.sstgb
        stg_free = b.free_stga if par == 0 else b.free_stgb
        in_ap = rar(w_dram[:, oc * blk:(oc + 1) * blk], "(c p) f -> p c f", p=128)
        out_ap = rar(ldb[:, :wcols], "p (c f) -> p c f", f=blk)
        ev = em.dma("sync", "DF" if par == 0 else "DI", out=out_ap, in_=in_ap,
                    waits=ld_free)
        em.wait("scalar" if par == 0 else "vector", ev)
        if par == 0:
            em.waits("scalar", stg_free)
            evh = em.add("scalar", "activation", stg[:, 0:1024], ldb[:, 0:1024],
                         AF.Identity, inc="S")
            if wcols > 1024:
                evh = em.add("scalar", "activation", stg[:, 1024:wcols],
                             ldb[:, 1024:wcols], AF.Identity, inc="S")
            em.wait("vector", evh)
            em.wait("vector", ev)
        else:
            em.waits("vector", stg_free)
            evh = em.add("vector", "tensor_copy", stg[:, 0:wcols], ldb[:, :wcols],
                         inc="V")
            em.wait("vector", evh)
        evl = em.add("vector", "tensor_tensor", stg[:, 2048:2048 + wcols],
                     ldb[:, :wcols], stg[:, 0:wcols], ALU.subtract, inc="V")
        if par == 0:
            b.free_sres = [evl]
        else:
            b.free_sjunk = [evl]
        em.dma("gpsimd", "DSG" if par == 0 else "DSB",
               out=cache_h[oc * 128:(oc + 1) * 128, :], in_=stg[:, 0:wcols],
               waits=[evh])
        em.dma("gpsimd", "DSG" if par == 0 else "DSB",
               out=cache_l[oc * 128:(oc + 1) * 128, :],
               in_=stg[:, 2048:2048 + wcols], waits=[evl])
        if par == 0:
            b.free_stga = [("DSG", em.cnt["DSG"])]
        else:
            b.free_stgb = [("DSB", em.cnt["DSB"])]


def p0rest_specs(b):
    specs = []
    for (w, ch, cl, nrow, ncol, perm) in (
            (b.w_uq, b.cuq_h, b.cuq_l, CD, CD, True),
            (b.w_qr, b.cqr_h, b.cqr_l, CD, CD, False),
            (b.w_uv, b.cuv_h, b.cuv_l, CD, D, False),
            (b.w_uk, b.cuk_h, b.cuk_l, CD, CD, True),
            (b.w_kr, b.ckr_h, b.ckr_l, D, CD, True),
            (b.w_out, b.cout_h, b.cout_l, D, D, False)):
        if perm:
            wcols = (nrow // 128) * 128
            for oc in range(ncol // 128):
                specs.append((w[:, oc * 128:(oc + 1) * 128], wcols, ch, cl,
                              oc, True))
        else:
            for rc in range(nrow // 128):
                specs.append((w[rc * 128:(rc + 1) * 128, :], ncol, ch, cl,
                              rc, False))
    return specs


def emit_p0rest_iter(em, b, i, spec):
    in_src, wcols, ch, cl, row, perm = spec
    par = i % 2
    ldb = b.sldP[par]
    sh = b.sphP[par]
    sl = b.splP[par]
    qn = "DP0" if par == 0 else "DP1"
    sqn = "DG0" if par == 0 else "DG1"
    if perm:
        out_ap = rar(ldb[:, :wcols], "p (c f) -> p c f", f=128)
        in_ap = rar(in_src, "(c p) f -> p c f", p=128)
    else:
        out_ap = ldb[:, :wcols]
        in_ap = in_src
    ev = em.dma("scalar", qn, out=out_ap, in_=in_ap, waits=b.p0_ldfree[par])
    heng = "scalar" if par == 0 else "vector"
    em.wait(heng, ev)
    em.waits(heng, b.p0_stfree[par])
    if par == 0:
        evh = em.add("scalar", "activation", sh[:, :wcols], ldb[:, :wcols],
                     AF.Identity, inc="S")
    else:
        evh = em.add("vector", "tensor_copy", sh[:, :wcols], ldb[:, :wcols],
                     inc="V")
    em.wait("gpsimd", ev)
    em.wait("gpsimd", evh)
    em.waits("gpsimd", b.p0_stfree[par])
    evl = em.add("gpsimd", "tensor_tensor", sl[:, :wcols], ldb[:, :wcols],
                 sh[:, :wcols], ALU.subtract, inc="G")
    b.p0_ldfree[par] = [evl]
    em.dma("gpsimd", sqn, out=ch[row * 128:(row + 1) * 128, :],
           in_=sh[:, :wcols], waits=[evh])
    em.dma("gpsimd", sqn, out=cl[row * 128:(row + 1) * 128, :],
           in_=sl[:, :wcols], waits=[evl])
    b.p0_stfree[par] = [(sqn, em.cnt[sqn])]



def emit_rope(em, b, qr_psum, out_hi, out_lo, col_base, col_stride, g, h0=0,
              nh=16):
    qr = qr_psum.rearrange("p (h j) -> p h j", j=64)
    TW = 16 * 64
    W32 = nh * 32
    sinv = bass.AP(b.stab, g * 64, [[TW, 128], [0, nh], [1, 32]])
    cosv = bass.AP(b.stab, g * 64 + 32, [[TW, 128], [0, nh], [1, 32]])
    sr = rar(b.srop, "p (h j) -> p h j", j=32)[:, :nh]
    sr2 = rar(b.srop2, "p (h j) -> p h j", j=16)
    sro = rar(b.sropo, "p (h j) -> p h j", j=32)[:, :nh]
    ev = em.add("vector", "tensor_tensor", b.srop[:, :W32], qr[:, :, 0:32], cosv,
                ALU.mult, inc="V")
    sinv_a = bass.AP(b.stab, g * 64, [[TW, 128], [0, nh], [1, 16]])
    sinv_b = bass.AP(b.stab, g * 64 + 16, [[TW, 128], [0, nh], [1, 16]])
    ev2a = em.add("vector", "tensor_tensor", sr2[:, 0:nh, :], qr[:, :, 16:32],
                  sinv_a, ALU.mult, inc="V")
    ev2b = em.add("vector", "tensor_tensor", sr2[:, 16:16 + nh, :], qr[:, :, 0:16],
                  sinv_b, ALU.mult, inc="V")
    em.wait("vector", ev)
    em.wait("vector", ev2a)
    em.add("vector", "tensor_tensor", sro[:, :, 0:16], sr[:, :, 0:16],
           sr2[:, 0:nh, :], ALU.subtract, inc="V")
    em.wait("vector", ev2b)
    ev4 = em.add("vector", "tensor_tensor", sro[:, :, 16:32], sr[:, :, 16:32],
                 sr2[:, 16:16 + nh, :], ALU.add, inc="V")
    em.wait("vector", ev4)
    oh = rar(out_hi, "p (h j) -> p h j", j=col_stride)[:, h0:h0 + nh]
    ol = rar(out_lo, "p (h j) -> p h j", j=col_stride)[:, h0:h0 + nh]
    rot_h = oh[:, :, col_base:col_base + 32]
    rot_l = ol[:, :, col_base:col_base + 32]
    ev5 = em.add("vector", "tensor_copy", rot_h, b.sropo[:, :W32], inc="V")
    em.wait("vector", ev5)
    em.add("vector", "tensor_tensor", rot_l, b.sropo[:, :W32], rot_h, ALU.subtract,
           inc="V")
    pas_h = oh[:, :, col_base + 32:col_base + 64]
    pas_l = ol[:, :, col_base + 32:col_base + 64]
    ev7 = em.add("vector", "tensor_copy", pas_h, qr[:, :, 32:64], inc="V")
    em.wait("vector", ev7)
    em.add("vector", "tensor_tensor", pas_l, qr[:, :, 32:64], pas_h, ALU.subtract,
           inc="V")
    return ("V", em.cnt["V"])



def emit_b1a(em, b):
    pfree = {0: None, 1: None}
    b.p0i = 0
    xnbuf = [(b.sxnh, b.sxnl), (b.sxnh2, b.sxnl2)]
    xnready = {0: None, 1: None}
    xnfree = {0: list(b.xn_free), 1: []}
    for tc in range(NTC):
        b.xn_free = xnfree[0]
        xnready[0] = emit_xnprep_tc(em, b, b.resid, 0, tc)
    for tile in range(NT):
        par_t = tile % 2
        xh, xl = xnbuf[par_t]
        win_free = b.winf
        first_round_mm = True
        for ri, (cache_h, cache_l, dst_h, dst_l) in enumerate((
                (b.cdkv_h, b.cdkv_l, b.ckv_h, b.ckv_l),
                (b.cdq_h, b.cdq_l, b.cq_h, b.cq_l))):
            for oc in range(NKC):
                if ri == 1 and tile + 1 < NT and oc % 2 == 0:
                    np_par = (tile + 1) % 2
                    b.xn_free = xnfree[np_par]
                    xnready[np_par] = emit_xnprep_tc(
                        em, b, b.resid, tile + 1, oc // 2,
                        xnbuf[np_par][0], xnbuf[np_par][1])
                par = oc % 2
                ps_ = b.pA if par == 0 else b.pB
                so = par * 512
                wh = b.win[par]
                wl = b.win[2 + par]
                evh_d = em.dma("sync", f"DW{par}", out=wh[:, :],
                               in_=cache_h[oc * 128:(oc + 1) * 128, :],
                               waits=win_free[par])
                evl_d = em.dma("sync", f"DW{2 + par}", out=wl[:, :],
                               in_=cache_l[oc * 128:(oc + 1) * 128, :],
                               waits=win_free[2 + par])
                em.wait("tensor", evh_d)
                em.wait("tensor", evl_d)
                em.wait("tensor", pfree[par])
                if first_round_mm:
                    em.waits("tensor", xnready[par_t])
                    first_round_mm = False
                ev_mm = None
                for (wv, xn) in ((wh, xh), (wh, xl), (wl, xh)):
                    for dc in range(NDC):
                        st = (wv is wh and xn is xh and dc == 0)
                        last = (wv is wl and dc == NDC - 1)
                        ev_mm = em.add("tensor", "matmul", ps_[:, :],
                                       wv[:, dc * 128:(dc + 1) * 128],
                                       xn[:, dc * 512:(dc + 1) * 512],
                                       start=st, stop=last,
                                       inc=("T" if last else None))
                em.wait("vector", ev_mm)
                em.waits("vector", b.free_stga if par == 0 else b.free_stgb)
                stg = b.sstga if par == 0 else b.sstgb
                evh = em.add("vector", "tensor_copy", stg[:, 0:512],
                             ps_[:, :], inc="V")
                em.wait("vector", evh)
                evl = em.add("vector", "tensor_tensor", stg[:, 512:1024],
                             ps_[:, :], stg[:, 0:512], ALU.subtract, inc="V")
                pfree[par] = evl
                win_free[par] = [ev_mm]
                win_free[2 + par] = [ev_mm]
                em.dma("gpsimd", "DSG" if par == 0 else "DSB",
                       out=dst_h[oc * 128:(oc + 1) * 128, tile * T:(tile + 1) * T],
                       in_=stg[:, 0:512], waits=[evh])
                em.dma("gpsimd", "DSG" if par == 0 else "DSB",
                       out=dst_l[oc * 128:(oc + 1) * 128, tile * T:(tile + 1) * T],
                       in_=stg[:, 512:1024], waits=[evl])
                if par == 0:
                    b.free_stga = [("DSG", em.cnt["DSG"])]
                else:
                    b.free_stgb = [("DSB", em.cnt["DSB"])]
                if b.p0q:
                    emit_p0rest_iter(em, b, b.p0i, b.p0q.pop(0))
                    b.p0i += 1
        xnfree[par_t] = [("T", em.cnt["T"])]
    while b.p0q:
        emit_p0rest_iter(em, b, b.p0i, b.p0q.pop(0))
        b.p0i += 1
    b.xn_free = ()
    b.xn_ready = []



def emit_b1b(em, b):
    first_m = True
    for tile in range(NT):
        for (dram_h, dram_l, sbh, sbl) in (
                (b.cq_h, b.cq_l, b.scqh, b.scql),
                (b.ckv_h, b.ckv_l, b.sckvh, b.sckvl)):
            em.dma("sync", "DT", out=rar(sbh, "p (c t) -> p c t", c=NKC),
                   in_=rar(dram_h[:, tile * T:(tile + 1) * T], "(c p) f -> p c f",
                           p=128))
            em.dma("sync", "DT", out=rar(sbl, "p (c t) -> p c t", c=NKC),
                   in_=rar(dram_l[:, tile * T:(tile + 1) * T], "(c p) f -> p c f",
                           p=128))
        em.fence()
        b.win_free = [(), (), (), ()]
        win_free = b.win_free
        qfree = {0: None, 1: None}
        for oc in range(NKC):
            par = oc % 2
            ps_ = b.pA if par == 0 else b.pB
            wh = b.win[par]
            wl = b.win[2 + par]
            evh_d = em.dma("sync", f"DW{par}", out=wh[:, 0:1024],
                           in_=b.cuq_h[oc * 128:(oc + 1) * 128, :],
                           waits=win_free[par])
            evl_d = em.dma("sync", f"DW{2 + par}", out=wl[:, 0:1024],
                           in_=b.cuq_l[oc * 128:(oc + 1) * 128, :],
                           waits=win_free[2 + par])
            em.wait("tensor", evh_d)
            em.wait("tensor", evl_d)
            em.wait("tensor", qfree[par])
            ev_mm = None
            for (wv, cq) in ((wh, b.scqh), (wh, b.scql), (wl, b.scqh)):
                for kc in range(NKC):
                    st = (wv is wh and cq is b.scqh and kc == 0)
                    last = (wv is wl and kc == NKC - 1)
                    ev_mm = em.add("tensor", "matmul", ps_[:, :],
                                   wv[:, kc * 128:(kc + 1) * 128],
                                   cq[:, kc * 512:(kc + 1) * 512],
                                   start=st, stop=last,
                                   inc=("T" if last else None))
            em.wait("vector", ev_mm)
            out_h = bass.AP(b.sqcfh, oc * 128, [[4096, 128], [1024, 4], [1, 128]])
            out_l = bass.AP(b.sqcfl, oc * 128, [[4096, 128], [1024, 4], [1, 128]])
            evh = em.add("vector", "tensor_copy", out_h, ps_[:, :], inc="V")
            em.wait("vector", evh)
            evl = em.add("vector", "tensor_tensor", out_l, ps_[:, :], out_h,
                         ALU.subtract, inc="V")
            qfree[par] = evl
            win_free[par] = [ev_mm]
            win_free[2 + par] = [ev_mm]
        em.fence()
        prev_mmm = None
        for tc in range(NTC):
            row0 = tile * T + tc * 128
            for half, (qcf, stg, Qd, tsem) in enumerate(
                    ((b.sqcfh, b.sQstgh, b.sQh, "DTA"),
                     (b.sqcfl, b.sQstgl, b.sQl, "DTB"))):
                in_ap = rar(qcf[:, tc * 1024:(tc + 1) * 1024],
                            "p (c t) -> p c t", t=128)
                evt = em.dma("sync", tsem,
                             out=rar(stg, "p (c d) -> p c d", d=128),
                             in_=in_ap, waits=b.qstg_free[half], transpose=True)
                em.wait("vector", evt)
                em.wait("vector", prev_mmm)
                out_ap = bass.AP(Qd, 0, [[2048, 128], [256, NKC], [128, 2], [1, 64]])
                in_ap2 = bass.AP(stg, 0, [[1024, 128], [128, NKC], [64, 2], [1, 64]])
                eve = em.add("vector", "tensor_copy", out_ap, in_ap2, inc="V")
                b.qstg_free[half] = [eve]
            win_free = b.win_free
            em.waits("tensor", b.qr_free)
            for oc in range(NKC):
                wh = b.win[oc % 2]
                wl = b.win[2 + oc % 2]
                evh_d = em.dma("sync", f"DW{oc % 2}", out=wh[:, 0:1024],
                               in_=b.cqr_h[oc * 128:(oc + 1) * 128, :],
                               waits=win_free[oc % 2])
                evl_d = em.dma("sync", f"DW{2 + oc % 2}", out=wl[:, 0:1024],
                               in_=b.cqr_l[oc * 128:(oc + 1) * 128, :],
                               waits=win_free[2 + oc % 2])
                em.wait("tensor", evh_d)
                em.wait("tensor", evl_d)
                for (wv, qcf) in ((wh, b.sqcfh), (wh, b.sqcfl), (wl, b.sqcfh)):
                    lhsT = qcf[:, tc * 1024 + oc * 128:tc * 1024 + (oc + 1) * 128]
                    st = (wv is wh and qcf is b.sqcfh and oc == 0)
                    for ng, png in ((0, b.pC), (1, b.pD)):
                        em.add("tensor", "matmul", png[:, :],
                               lhsT, wv[:, ng * 512:(ng + 1) * 512],
                               start=st, stop=(wv is wl and oc == NKC - 1),
                               inc=("T" if (wv is wl and ng == 1) else None))
                ev_qr = ("T", em.cnt["T"])
                win_free[oc % 2] = [ev_qr]
                win_free[2 + oc % 2] = [ev_qr]
            em.wait("vector", ev_qr)
            for half, png in ((0, b.pC), (1, b.pD)):
                ev_rope = emit_rope(em, b, png[:, :], b.sQh, b.sQl, 64, 128,
                                    tile * NTC + tc, h0=half * 8, nh=8)
            b.qr_free = [ev_rope]
            win_free = b.win_free
            for kc in range(NKC):
                wh = b.win[kc % 2]
                wl = b.win[2 + kc % 2]
                evh_d = em.dma("sync", f"DW{kc % 2}", out=wh[:, :],
                               in_=b.cuv_h[kc * 128:(kc + 1) * 128, :],
                               waits=win_free[kc % 2])
                evl_d = em.dma("sync", f"DW{2 + kc % 2}", out=wl[:, :],
                               in_=b.cuv_l[kc * 128:(kc + 1) * 128, :],
                               waits=win_free[2 + kc % 2])
                em.wait("tensor", evh_d)
                em.wait("tensor", evl_d)
                for (wv, ckv) in ((wh, b.sckvh), (wh, b.sckvl), (wl, b.sckvh)):
                    lhsT = ckv[:, kc * 512 + tc * 128:kc * 512 + (tc + 1) * 128]
                    st = (wv is wh and ckv is b.sckvh and kc == 0)
                    for vg in range(4):
                        em.add("tensor", "matmul",
                               b.pdown[:, vg * 512:(vg + 1) * 512],
                               lhsT, wv[:, vg * 512:(vg + 1) * 512],
                               start=st, stop=(wv is wl and kc == NKC - 1),
                               inc=("T" if (wv is wl and vg == 3) else None))
                ev_v = ("T", em.cnt["T"])
                win_free[kc % 2] = [ev_v]
                win_free[2 + kc % 2] = [ev_v]
            em.wait("vector", ev_v)
            em.wait("vector", prev_mmm)
            evh = em.add("vector", "tensor_copy", b.sVh[:], b.pdown[:, :], inc="V")
            em.wait("vector", evh)
            evl = em.add("vector", "tensor_tensor", b.sVl[:], b.pdown[:, :],
                         b.sVh[:], ALU.subtract, inc="V")
            em.wait("tensor", evl)
            em.wait("tensor", ev_rope)
            for h in range(NH):
                for (Qx, Vx) in ((b.sQh, b.sVh), (b.sQl, b.sVh), (b.sQh, b.sVl)):
                    st = (h % 4 == 0 and Qx is b.sQh and Vx is b.sVh)
                    sp = (h % 4 == 3 and Qx is b.sQh and Vx is b.sVl)
                    em.add("tensor", "matmul",
                           b.pdown[:, h * 128:(h + 1) * 128],
                           Qx[:, h * 128:(h + 1) * 128],
                           Vx[:, h * 128:(h + 1) * 128],
                           start=st, stop=sp,
                           inc=("T" if sp else None))
            ev_mmm = ("T", em.cnt["T"])
            prev_mmm = ev_mmm
            em.wait("vector", ev_mmm)
            if first_m:
                evm = em.add("vector", "tensor_copy", b.sMf[:], b.pdown[:, :],
                             inc="V")
                first_m = False
            else:
                evm = em.add("vector", "tensor_tensor", b.sMf[:], b.pdown[:, :],
                             b.sMf[:], ALU.add, inc="V")
            em.wait("tensor", evm)
        em.fence()



def emit_b2(em, b):
    prev = None
    for part, dst in ((0, b.ssinf), (1, b.scosf)):
        evh = em.dma("sync", "DC", out=b.sjunk[:],
                     in_=b.angf_hi[:, part * S:(part + 1) * S],
                     waits=([prev] if prev else []) + list(b.free_sjunk or ()))
        evl = em.dma("sync", "DC", out=b.sres[:],
                     in_=b.angf_lo[:, part * S:(part + 1) * S],
                     waits=b.free_sres)
        em.wait("vector", evl)
        eva = em.add("vector", "tensor_tensor", dst[:], b.sjunk[:], b.sres[:],
                     ALU.add, inc="V")
        b.free_sjunk = [eva]
        b.free_sres = [eva]
        em.wait("scalar", eva)
        prev = em.add("scalar", "activation", dst[:], dst[:], AF.Sin, inc="S")
    ev1 = em.add("vector", "tensor_scalar", b.sMf[:], b.sMf[:], ISQ, None,
                 op0=ALU.mult, inc="V")
    em.wait("vector", ev1)
    ev2 = em.add("vector", "tensor_copy", b.sMh[:], b.sMf[:], inc="V")
    em.wait("vector", ev2)
    ev3 = em.add("vector", "tensor_tensor", b.sMl[:], b.sMf[:], b.sMh[:],
                 ALU.subtract, inc="V")
    for (srcM, dstM, w) in ((b.sMh, b.sMsh, ev2), (b.sMl, b.sMsl, ev3)):
        em.dma("sync", "DT", out=dstM[0:64, :], in_=srcM[64:128, :], waits=[w])
        em.dma("sync", "DT", out=dstM[64:128, :], in_=srcM[0:64, :])
    em.fence()
    b.ps_free = []
    b.xn_free = ()
    b.ckv_free = ()
    for tile in range(NT):
        if tile == 0:
            emit_xnprep(em, b, b.resid, 0)
        ev_ckv = em.dma("sync", "DT", out=rar(b.sckvh, "p (c t) -> p c t", c=NKC),
                        in_=rar(b.ckv_h[:, tile * T:(tile + 1) * T],
                                "(c p) f -> p c f", p=128),
                        waits=b.ckv_free)
        ev_ckv = em.dma("sync", "DT", out=rar(b.sckvl, "p (c t) -> p c t", c=NKC),
                        in_=rar(b.ckv_l[:, tile * T:(tile + 1) * T],
                                "(c p) f -> p c f", p=128))
        cosli = b.scosf[:, tile * T:(tile + 1) * T]
        sinli = b.ssinf[:, tile * T:(tile + 1) * T]
        win_free = b.winf
        bank = {0: (b.pA[:, :], b.pB[:, :], b.pC[:, :], b.pD[:, :]),
                1: (b.pdown[:, 0:512], b.pdown[:, 512:1024],
                    b.pdown[:, 1024:1536], b.pdown[:, 1536:2048])}
        bfree = {0: {"kc": None, "kr": None, "sh": None, "o": None},
                 1: {"kc": None, "kr": None, "sh": None, "o": None}}
        sfree = {0: {"rp": (), "ro": (), "kc": ()}, 1: {"rp": (), "ro": (), "kc": ()}}
        for c in range(NKC):
            par = c % 2
            pKC, pKR, pSH, pO = bank[par]
            srph = b.srph[:, par * 512:(par + 1) * 512]
            srpl = b.srpl[:, par * 512:(par + 1) * 512]
            sroh = b.sroh[:, par * 512:(par + 1) * 512]
            srol = b.srol[:, par * 512:(par + 1) * 512]
            skch = b.skch[:, par * 512:(par + 1) * 512]
            skcl = b.skcl[:, par * 512:(par + 1) * 512]
            w0 = b.win[par]
            w1 = b.win[2 + par]
            em.dma("sync", f"DW{par}", out=w0[:, 0:1024],
                   in_=b.cuk_h[c * 128:(c + 1) * 128, :], waits=win_free[par])
            e_uk = em.dma("sync", f"DW{par}", out=w0[:, 1024:2048],
                          in_=b.cuk_l[c * 128:(c + 1) * 128, :])
            e_kh = em.dma("sync", f"DW{2 + par}", out=w1[:, :],
                          in_=b.ckr_h[c * 128:(c + 1) * 128, :],
                          waits=win_free[2 + par])
            w2 = b.win[2 + (1 - par)]
            e_kl = em.dma("sync", f"DW{2 + (1 - par)}", out=w2[:, :],
                          in_=b.ckr_l[c * 128:(c + 1) * 128, :],
                          waits=win_free[2 + (1 - par)])
            em.wait("tensor", e_kh)
            em.wait("tensor", e_kl)
            if c == 0:
                em.waits("tensor", b.xn_ready)
                em.waits("tensor", b.ps_free)
                em.wait("tensor", ev_ckv)
            em.wait("tensor", bfree[par]["kr"])
            ev_kr = None
            for (wv, xn) in ((w1, b.sxnh), (w1, b.sxnl), (w2, b.sxnh)):
                for dc in range(NDC):
                    st = (wv is w1 and xn is b.sxnh and dc == 0)
                    last = (wv is w2 and dc == NDC - 1)
                    ev_kr = em.add("tensor", "matmul", pKR,
                                   wv[:, dc * 128:(dc + 1) * 128],
                                   xn[:, dc * 512:(dc + 1) * 512],
                                   start=st, stop=last,
                                   inc=("T" if last else None))
            win_free[2 + par] = [ev_kr]
            win_free[2 + (1 - par)] = [ev_kr]
            em.wait("tensor", e_uk)
            em.wait("tensor", bfree[par]["kc"])
            ev_kc = None
            for (woff, ckv) in ((0, b.sckvh), (0, b.sckvl), (1024, b.sckvh)):
                for kc in range(NKC):
                    st = (woff == 0 and ckv is b.sckvh and kc == 0)
                    last = (woff == 1024 and kc == NKC - 1)
                    ev_kc = em.add("tensor", "matmul", pKC,
                                   w0[:, woff + kc * 128:woff + (kc + 1) * 128],
                                   ckv[:, kc * 512:(kc + 1) * 512],
                                   start=st, stop=last,
                                   inc=("T" if last else None))
            win_free[par] = [ev_kc]
            em.wait("vector", ev_kr)
            em.waits("vector", sfree[par]["rp"])
            e_h = em.add("vector", "tensor_copy", srph, pKR, inc="V")
            em.wait("vector", e_h)
            e_l = em.add("vector", "tensor_tensor", srpl, pKR, srph,
                         ALU.subtract, inc="V")
            em.wait("tensor", e_l)
            em.wait("tensor", bfree[par]["sh"])
            em.add("tensor", "matmul", pSH, b.sperm[:], srph,
                   start=True, stop=False)
            ev_sh = em.add("tensor", "matmul", pSH, b.sperm[:], srpl,
                           start=False, stop=True, inc="T")
            sfree[par]["rp"] = [ev_sh]
            em.wait("vector", ev_sh)
            r1 = em.add("vector", "tensor_tensor", b.sropa[:], pKR, cosli,
                        ALU.mult, inc="V")
            bfree[par]["kr"] = r1
            em.wait("vector", r1)
            r2 = em.add("vector", "tensor_tensor", b.sropb[:], pSH, sinli,
                        ALU.mult, inc="V")
            bfree[par]["sh"] = r2
            em.wait("vector", r2)
            r3 = em.add("vector", "tensor_tensor", b.sropa[:], b.sropa[:],
                        b.sropb[:], ALU.add, inc="V")
            em.wait("vector", r3)
            em.waits("vector", sfree[par]["ro"])
            g1 = em.add("vector", "tensor_copy", sroh, b.sropa[:], inc="V")
            em.wait("vector", g1)
            g2 = em.add("vector", "tensor_tensor", srol, b.sropa[:], sroh,
                        ALU.subtract, inc="V")
            em.wait("vector", ev_kc)
            em.waits("vector", sfree[par]["kc"])
            c1 = em.add("vector", "tensor_copy", skch, pKC, inc="V")
            em.wait("vector", c1)
            c2 = em.add("vector", "tensor_tensor", skcl, pKC, skch,
                        ALU.subtract, inc="V")
            bfree[par]["kc"] = c2
            for hp in range(2):
                h = 2 * c + hp
                base = hp * 64
                if hp == 0:
                    mtop = (b.sMh, b.sMl)
                    mbot = (b.sMsh, b.sMsl)
                else:
                    mtop = (b.sMsh, b.sMsl)
                    mbot = (b.sMh, b.sMl)
                em.wait("tensor", c2)
                em.wait("tensor", g2)
                em.wait("tensor", bfree[par]["o"])
                ev_o = None
                first = True
                for part, (Mt, rh, rl) in (
                        (0, ((mtop), skch, skcl)),
                        (1, ((mbot), sroh, srol))):
                    Mh_, Ml_ = Mt
                    for (Mx, rx) in ((Mh_, rh), (Ml_, rh), (Mh_, rl)):
                        ev_o = em.add(
                            "tensor", "matmul", pO,
                            Mx[base:base + 64, h * 128:(h + 1) * 128],
                            rx[base:base + 64, :],
                            start=first, stop=(part == 1 and Mx is Mh_ and rx is rl),
                            inc=("T" if (part == 1 and Mx is Mh_ and rx is rl)
                                 else None))
                        first = False
                em.wait("vector", ev_o)
                eh = em.add("vector", "tensor_copy",
                            b.sOh[:, h * 512:(h + 1) * 512], pO, inc="V")
                em.wait("vector", eh)
                el = em.add("vector", "tensor_tensor",
                            b.sOl[:, h * 512:(h + 1) * 512], pO,
                            b.sOh[:, h * 512:(h + 1) * 512], ALU.subtract,
                            inc="V")
                bfree[par]["o"] = el
            sfree[par]["ro"] = [("T", em.cnt["T"])]
            sfree[par]["kc"] = [("T", em.cnt["T"])]
        b.xn_free = [("T", em.cnt["T"])]
        b.ckv_free = [("T", em.cnt["T"])]
        if tile + 1 < NT:
            emit_xnprep(em, b, b.resid, tile + 1)
        o_ready = [bfree[0]["o"], bfree[1]["o"], bfree[0]["kc"], bfree[1]["kc"],
                   bfree[0]["kr"], bfree[1]["kr"], bfree[0]["sh"], bfree[1]["sh"]]
        first_op = True
        for sweep in ((0, 1), (2, 3)):
            psets = {sweep[0]: [b.pdown[:, dg * 512:(dg + 1) * 512]
                                for dg in range(4)],
                     sweep[1]: [b.pA[:, :], b.pB[:, :], b.pC[:, :], b.pD[:, :]]}
            ev_mm = None
            for cc in range(NDC):
                par = cc % 2
                woh = b.win[par]
                wol = b.win[2 + par]
                evh_d = em.dma("sync", f"DW{par}", out=woh[:, :],
                               in_=b.cout_h[cc * 128:(cc + 1) * 128, :],
                               waits=win_free[par])
                evl_d = em.dma("sync", f"DW{2 + par}", out=wol[:, :],
                               in_=b.cout_l[cc * 128:(cc + 1) * 128, :],
                               waits=win_free[2 + par])
                em.wait("tensor", evh_d)
                em.wait("tensor", evl_d)
                if first_op:
                    em.waits("tensor", o_ready)
                    em.waits("tensor", b.ps_free)
                    first_op = False
                for tc in sweep:
                    for (Ox, wv) in ((b.sOh, woh), (b.sOl, woh), (b.sOh, wol)):
                        lhsT = Ox[:, cc * 512 + tc * 128:cc * 512 + (tc + 1) * 128]
                        st = (Ox is b.sOh and wv is woh and cc == 0)
                        sp = (Ox is b.sOh and wv is wol and cc == NDC - 1)
                        for dg in range(4):
                            em.add("tensor", "matmul", psets[tc][dg], lhsT,
                                   wv[:, dg * 512:(dg + 1) * 512],
                                   start=st, stop=sp,
                                   inc=("T" if (tc == sweep[-1] and Ox is b.sOh
                                                and wv is wol and dg == 3)
                                        else None))
                ev_mm = ("T", em.cnt["T"])
                win_free[par] = [ev_mm]
                win_free[2 + par] = [ev_mm]
            for tc in sweep:
                row0 = tile * T + tc * 128
                ev_x2 = em.dma("scalar", "DX2", out=b.sxt2[:],
                               in_=b.resid[row0:row0 + 128, :], waits=b.free_sxt2)
                em.wait("vector", ev_mm)
                em.wait("vector", ev_x2)
                em.waits("vector", b.free_sres)
                if tc == sweep[0]:
                    evr = em.add("vector", "tensor_tensor", b.sres[:],
                                 b.pdown[:, :], b.sxt2[:], ALU.add, inc="V")
                else:
                    for dg in range(4):
                        evr = em.add("vector", "tensor_tensor",
                                     b.sres[:, dg * 512:(dg + 1) * 512],
                                     psets[tc][dg],
                                     b.sxt2[:, dg * 512:(dg + 1) * 512],
                                     ALU.add, inc="V")
                b.free_sxt2 = [evr]
                em.wait("tensor", evr)
                b.ps_free = [evr]
                em.dma("gpsimd", "DR", out=b.resid[row0:row0 + 128, :],
                       in_=b.sres[:], waits=[evr])
                b.free_sres = [("DR", em.cnt["DR"])]
    em.fence()



ALL_PHASES = ("P0", "A", "B1a", "B1b", "B2", "C")


def build_program(debug=False, phases=ALL_PHASES):
    nc = bass.Bass("TRN2", target_bir_lowering=False)
    b = NS()

    def din(name, shape, dt=F32):
        return nc.dram_tensor(name, shape, dt, kind="ExternalInput").ap()

    b.x_in = din("x", [S, D])
    b.w_dkv = din("w_dkv", [D, CD]); b.w_dq = din("w_dq", [D, CD])
    b.w_uk = din("w_uk", [CD, CD]); b.w_uv = din("w_uv", [CD, D])
    b.w_uq = din("w_uq", [CD, CD]); b.w_kr = din("w_kr", [D, CD])
    b.w_qr = din("w_qr", [CD, CD]); b.w_out = din("w_out", [D, D])
    b.wq_gate = din("wq_gate", [HID, D], I32)
    b.wq_up = din("wq_up", [HID, D], I32)
    b.wq_down = din("wq_down", [D, HID], I32)
    b.s_gate = din("s_gate", [HID]); b.s_up = din("s_up", [HID])
    b.sdb_in = din("sdb", [128, D])
    b.ang_hi = din("ang_hi", [S, 64]); b.ang_lo = din("ang_lo", [S, 64])
    b.angf_hi = din("angf_hi", [128, 2 * S]); b.angf_lo = din("angf_lo", [128, 2 * S])
    b.perm_in = din("perm", [128, 128], BF16)
    b.ident_in = din("ident", [128, 128], BF16)
    b.y = nc.dram_tensor("y", [S, D], F32, kind="ExternalOutput").ap()

    def scr(name, shape, dt, dbgk=False):
        kind = "ExternalOutput" if (dbgk and debug) else "Internal"
        return nc.dram_tensor(name, shape, dt, kind=kind).ap()

    b.resid = scr("resid", [S, D], F32, dbgk=True)
    b.wgT_c = scr("wgT_c", [NHC * 128, D], BF16)
    b.wuT_c = scr("wuT_c", [NHC * 128, D], BF16)
    b.wdT_c = scr("wdT_c", [NHC * 128, D], BF16)
    b.cdkv_h = scr("cdkv_h", [CD, D], BF16); b.cdkv_l = scr("cdkv_l", [CD, D], BF16)
    b.cdq_h = scr("cdq_h", [CD, D], BF16); b.cdq_l = scr("cdq_l", [CD, D], BF16)
    b.cuq_h = scr("cuq_h", [CD, CD], BF16); b.cuq_l = scr("cuq_l", [CD, CD], BF16)
    b.cuk_h = scr("cuk_h", [CD, CD], BF16)
    b.cuk_l = scr("cuk_l", [CD, CD], BF16)
    b.ckr_h = scr("ckr_h", [CD, D], BF16); b.ckr_l = scr("ckr_l", [CD, D], BF16)
    b.cqr_h = scr("cqr_h", [CD, CD], BF16); b.cqr_l = scr("cqr_l", [CD, CD], BF16)
    b.cuv_h = scr("cuv_h", [CD, D], BF16); b.cuv_l = scr("cuv_l", [CD, D], BF16)
    b.cout_h = scr("cout_h", [D, D], BF16); b.cout_l = scr("cout_l", [D, D], BF16)
    b.ckv_h = scr("ckv_h", [CD, S], BF16, dbgk=True)
    b.ckv_l = scr("ckv_l", [CD, S], BF16)
    b.cq_h = scr("cq_h", [CD, S], BF16); b.cq_l = scr("cq_l", [CD, S], BF16)

    em = Em(nc)
    with contextlib.ExitStack() as stack:
        def sb(name, shape, dt):
            return stack.enter_context(nc.sbuf_tensor(name, shape, dt))

        def ps(name, shape, dt):
            return stack.enter_context(nc.psum_tensor(name, shape, dt))

        for sname in ("T", "V", "S", "G", "Y", "TF", "VF", "SF", "GF", "YF") + DMA_SEMS:
            em.reg_sem(sname, stack.enter_context(nc.semaphore(sname)))
        b.sxt = sb("sxt", [128, D], F32)
        b.sxt2 = sb("sxt2", [128, D], F32)
        b.sxnh = sb("sxnh", [128, NDC * 512], BF16)
        b.sxnl = sb("sxnl", [128, NDC * 512], BF16)
        b.sxtmh = sb("sxtmh", [128, D], BF16)
        b.sxtml = sb("sxtml", [128, D], BF16)
        b.sjunk = sb("sjunk", [128, D], F32)
        b.sres = sb("sres", [128, D], F32)
        b.sstat = sb("sstat", [128, 16], F32)
        b.sscg = sb("sscg", [128, NHC], F32)
        b.sscu = sb("sscu", [128, NHC], F32)
        b.sident = sb("sident", [128, 128], BF16)
        b.ssil = sb("ssil", [128, 512], F32)
        b.shh32 = sb("shh32", [128, 512], F32)
        b.win = [sb(f"win{i}", [128, 2048], BF16) for i in range(4)]
        b.stab = sb("stab", [128, 16 * 64], F32)
        b.pA = ps("pA", [128, 512], F32)
        b.pB = ps("pB", [128, 512], F32)
        b.pC = ps("pC", [128, 512], F32)
        b.pD = ps("pD", [128, 512], F32)
        b.pdown = ps("pdown", [128, 2048], F32)
        b.sperm = sb("sperm", [128, 128], BF16)
        for attr in ("free_sxt", "free_sxt2", "free_sjunk", "free_sres",
                     "free_sxtm", "free_sint", "free_sconv", "free_stga",
                     "free_stgb"):
            setattr(b, attr, ())


        b.winf = [(), (), (), ()]
        b.ps_free = []
        b.xn_free = ()
        b.xn_ready = []
        b.ckv_free = ()
        b.pref = []
        em.dma("sync", "DT", out=b.sident[:], in_=b.ident_in[:, :])
        em.dma("sync", "DT", out=b.sperm[:], in_=b.perm_in[:, :])
        em.dma("sync", "DT", out=b.sscg[:],
               in_=rar(b.s_gate, "(c p) -> p c", p=128),
               allow_slow_non_contiguous=True)
        em.dma("sync", "DT", out=b.sscu[:],
               in_=rar(b.s_up, "(c p) -> p c", p=128),
               allow_slow_non_contiguous=True)
        prev_sin = None
        for g in range(16):
            evh = em.dma("sync", "DC", out=b.sjunk[:, 0:64],
                         in_=b.ang_hi[g * 128:(g + 1) * 128, :],
                         waits=([prev_sin] if prev_sin else ()))
            evl = em.dma("sync", "DC", out=b.sjunk[:, 64:128],
                         in_=b.ang_lo[g * 128:(g + 1) * 128, :])
            em.wait("vector", evl)
            if prev_sin is not None:
                em.wait("vector", prev_sin)
            eva = em.add("vector", "tensor_tensor", b.sjunk[:, 128:192],
                         b.sjunk[:, 0:64], b.sjunk[:, 64:128], ALU.add, inc="V")
            em.wait("scalar", eva)
            prev_sin = em.add("scalar", "activation",
                              b.stab[:, g * 64:(g + 1) * 64],
                              b.sjunk[:, 128:192], AF.Sin, inc="S")
        b.free_sjunk = [prev_sin]
        b.p0q = []
        if "P0" in phases:
            with nc.sbuf_tensor("sstga", [128, 4096], BF16) as sstga, \
                 nc.sbuf_tensor("sstgb", [128, 4096], BF16) as sstgb:
                b.sstga, b.sstgb = sstga, sstgb
                emit_wsplit_perm(em, b, b.w_dkv, b.cdkv_h, b.cdkv_l, D, CD, 128)
                emit_wsplit_perm(em, b, b.w_dq, b.cdq_h, b.cdq_l, D, CD, 128)
                if "B1a" in phases:
                    b.p0q = p0rest_specs(b)
                else:
                    emit_wsplit_perm(em, b, b.w_uq, b.cuq_h, b.cuq_l, CD, CD, 128)
                    emit_wsplit_perm(em, b, b.w_uk, b.cuk_h, b.cuk_l, CD, CD, 128)
                    emit_wsplit_perm(em, b, b.w_kr, b.ckr_h, b.ckr_l, D, CD, 128)
                    emit_wsplit(em, b, b.w_qr, b.cqr_h, b.cqr_l, CD, CD)
                    emit_wsplit(em, b, b.w_uv, b.cuv_h, b.cuv_l, CD, D)
                    emit_wsplit(em, b, b.w_out, b.cout_h, b.cout_l, D, D)
                em.fence()
                em.flush()
        if "A" in phases:
            with nc.sbuf_tensor("shhh", [128, NHC * 512], BF16) as shhh, \
                 nc.sbuf_tensor("shhl", [128, NHC * 512], BF16) as shhl, \
                 nc.sbuf_tensor("sint", [128, 2048], I32) as sint, \
                 nc.sbuf_tensor("ssdbA", [128, D], F32) as ssdb, \
                 nc.sbuf_tensor("sconvA", [128, 2048], BF16) as sconv:
                b.shhh, b.shhl, b.sint, b.sconv = shhh, shhl, sint, sconv
                b.ssdb = ssdb
                emit_ffn(em, b, b.x_in, b.resid, convert=True)
                em.fence()
                em.flush()
        if "B1a" in phases:
            with nc.sbuf_tensor("sstga2", [128, 2048], BF16) as sstga, \
                 nc.sbuf_tensor("sstgb2", [128, 2048], BF16) as sstgb, \
                 nc.sbuf_tensor("sxnh2", [128, NDC * 512], BF16) as sxnh2, \
                 nc.sbuf_tensor("sxnl2", [128, NDC * 512], BF16) as sxnl2, \
                 nc.sbuf_tensor("sldP0", [128, 2048], F32) as sldP0, \
                 nc.sbuf_tensor("sldP1", [128, 2048], F32) as sldP1, \
                 nc.sbuf_tensor("sphP0", [128, 2048], BF16) as sphP0, \
                 nc.sbuf_tensor("sphP1", [128, 2048], BF16) as sphP1, \
                 nc.sbuf_tensor("splP0", [128, 2048], BF16) as splP0, \
                 nc.sbuf_tensor("splP1", [128, 2048], BF16) as splP1, \
                 nc.sbuf_tensor("skrs_h", [128, 1024], BF16) as skrs_h, \
                 nc.sbuf_tensor("skrs_l", [128, 1024], BF16) as skrs_l, \
                 nc.sbuf_tensor("scosA", [128, 512], F32) as scos, \
                 nc.sbuf_tensor("ssinA", [128, 512], F32) as ssin, \
                 nc.sbuf_tensor("sropA", [128, 512], F32) as srop, \
                 nc.sbuf_tensor("srop2A", [128, 512], F32) as srop2, \
                 nc.sbuf_tensor("sropoA", [128, 512], F32) as sropo:
                b.sstga, b.sstgb = sstga, sstgb
                b.sxnh2, b.sxnl2 = sxnh2, sxnl2
                b.sldP = [sldP0, sldP1]
                b.sphP = [sphP0, sphP1]
                b.splP = [splP0, splP1]
                b.p0_ldfree = {0: (), 1: ()}
                b.p0_stfree = {0: (), 1: ()}
                b.skrs_h, b.skrs_l = skrs_h, skrs_l
                b.scos, b.ssin, b.srop, b.srop2, b.sropo = scos, ssin, srop, srop2, sropo
                b.free_stga = ()
                b.free_stgb = ()
                emit_b1a(em, b)
                em.fence()
                em.flush()
        if "B1b" not in phases:
            return nc
        with nc.sbuf_tensor("sckvh", [128, NKC * 512], BF16) as sckvh, \
             nc.sbuf_tensor("sckvl", [128, NKC * 512], BF16) as sckvl, \
             nc.sbuf_tensor("sMf", [128, NH * 128], F32) as sMf:
            b.sckvh, b.sckvl, b.sMf = sckvh, sckvl, sMf
            with nc.sbuf_tensor("scqh", [128, NKC * 512], BF16) as scqh, \
                 nc.sbuf_tensor("scql", [128, NKC * 512], BF16) as scql, \
                 nc.sbuf_tensor("sqcfh", [128, NKC * 512], BF16) as sqcfh, \
                 nc.sbuf_tensor("sqcfl", [128, NKC * 512], BF16) as sqcfl, \
                 nc.sbuf_tensor("sQh", [128, 2048], BF16) as sQh, \
                 nc.sbuf_tensor("sQl", [128, 2048], BF16) as sQl, \
                 nc.sbuf_tensor("sVh", [128, 2048], BF16) as sVh, \
                 nc.sbuf_tensor("sVl", [128, 2048], BF16) as sVl, \
                 nc.sbuf_tensor("sQstgh", [128, 1024], BF16) as sQstgh, \
                 nc.sbuf_tensor("sQstgl", [128, 1024], BF16) as sQstgl, \
                 nc.sbuf_tensor("sropB", [128, 512], F32) as srop, \
                 nc.sbuf_tensor("srop2B", [128, 512], F32) as srop2, \
                 nc.sbuf_tensor("sropoB", [128, 512], F32) as sropo:
                b.scqh, b.scql, b.sqcfh, b.sqcfl = scqh, scql, sqcfh, sqcfl
                b.sQh, b.sQl, b.sVh, b.sVl = sQh, sQl, sVh, sVl
                b.sQstgh, b.sQstgl = sQstgh, sQstgl
                b.qstg_free = {0: (), 1: ()}
                b.qr_free = ()
                b.srop, b.srop2, b.sropo = srop, srop2, sropo
                emit_b1b(em, b)
                em.fence()
                em.flush()
            if "B2" not in phases:
                return nc
            with nc.sbuf_tensor("sMh", [128, NH * 128], BF16) as sMh, \
                 nc.sbuf_tensor("sMl", [128, NH * 128], BF16) as sMl, \
                 nc.sbuf_tensor("sMsh", [128, NH * 128], BF16) as sMsh, \
                 nc.sbuf_tensor("sMsl", [128, NH * 128], BF16) as sMsl, \
                 nc.sbuf_tensor("scosf", [128, S], F32) as scosf, \
                 nc.sbuf_tensor("ssinf", [128, S], F32) as ssinf, \
                 nc.sbuf_tensor("sropa", [128, 512], F32) as sropa, \
                 nc.sbuf_tensor("sropb", [128, 512], F32) as sropb, \
                 nc.sbuf_tensor("srph", [128, 1024], BF16) as srph, \
                 nc.sbuf_tensor("srpl", [128, 1024], BF16) as srpl, \
                 nc.sbuf_tensor("sroh", [128, 1024], BF16) as sroh, \
                 nc.sbuf_tensor("srol", [128, 1024], BF16) as srol, \
                 nc.sbuf_tensor("skch", [128, 1024], BF16) as skch, \
                 nc.sbuf_tensor("skcl", [128, 1024], BF16) as skcl, \
                 nc.sbuf_tensor("sOh", [128, NH * 512], BF16) as sOh, \
                 nc.sbuf_tensor("sOl", [128, NH * 512], BF16) as sOl:
                b.sMh, b.sMl, b.sMsh, b.sMsl = sMh, sMl, sMsh, sMsl
                b.scosf, b.ssinf = scosf, ssinf
                b.sropa, b.sropb = sropa, sropb
                b.srph, b.srpl, b.sroh, b.srol = srph, srpl, sroh, srol
                b.skch, b.skcl, b.sOh, b.sOl = skch, skcl, sOh, sOl
                b.free_srp = ()
                b.free_sro = ()
                b.free_skc = ()
                emit_b2(em, b)
                em.fence()
                em.flush()
        if "C" in phases:
            with nc.sbuf_tensor("shhh2", [128, NHC * 512], BF16) as shhh, \
                 nc.sbuf_tensor("shhl2", [128, NHC * 512], BF16) as shhl, \
                 nc.sbuf_tensor("ssdbC", [128, D], F32) as ssdb:
                b.shhh, b.shhl = shhh, shhl
                b.ssdb = ssdb
                emit_ffn(em, b, b.resid, b.y, convert=False)
                em.fence()
                em.flush()
    return nc



def make_consts():
    ident = np.eye(128, dtype=ml_dtypes.bfloat16)
    half = 32
    inv_freq = 1.0 / 10000.0 ** (np.arange(0, half, 2, dtype=np.float64) / half)
    t = np.arange(S, dtype=np.float64) / 40.0
    rot = np.concatenate([np.outer(t, inv_freq)] * 2, axis=1)
    twopi = 2.0 * np.pi
    angS = rot - twopi * np.round(rot / twopi)
    angC = (rot + np.pi / 2)
    angC = angC - twopi * np.round(angC / twopi)
    ang = np.concatenate([angS, angC], axis=1)
    ang_hi = ang.astype(np.float32)
    ang_lo = (ang - ang_hi.astype(np.float64)).astype(np.float32)
    angf = np.zeros((2, 64, S), np.float64)
    angf[0, :32, :] = angS.T
    angf[1, :32, :] = angC.T
    angf[1, 32:, :] = angC_pass = np.pi / 2
    angfS = np.concatenate([angf[0]] * 2, axis=0)
    angfC = np.concatenate([angf[1]] * 2, axis=0)
    angf_all = np.concatenate([angfS, angfC], axis=1)
    angf_hi = angf_all.astype(np.float32)
    angf_lo = (angf_all - angf_hi.astype(np.float64)).astype(np.float32)
    perm = np.zeros((128, 128), np.float32)
    for blk in range(2):
        o = blk * 64
        for p in range(16):
            perm[o + p + 16, o + p] = -1.0
        for p in range(16, 32):
            perm[o + p - 16, o + p] = 1.0
        for p in range(32, 64):
            perm[o + p, o + p] = 1.0
    perm = perm.astype(ml_dtypes.bfloat16)
    return ident, ang_hi, ang_lo, angf_hi, angf_lo, perm


_nc_cache = {}


def get_nc(debug=False):
    if debug not in _nc_cache:
        _nc_cache[debug] = build_program(debug=debug)
    return _nc_cache[debug]


def make_in_maps(inputs):
    ident, ang_hi, ang_lo, angf_hi, angf_lo, perm = make_consts()
    sdb = np.ascontiguousarray(np.broadcast_to(
        np.asarray(inputs["L0_ffn_s_down"], np.float32)[None, :], (128, D)))
    common = dict(
        w_dkv=np.ascontiguousarray(inputs["L0_attn_W_dkv"], np.float32),
        w_dq=np.ascontiguousarray(inputs["L0_attn_W_dq"], np.float32),
        w_uk=np.ascontiguousarray(inputs["L0_attn_W_uk"], np.float32),
        w_uv=np.ascontiguousarray(inputs["L0_attn_W_uv"], np.float32),
        w_uq=np.ascontiguousarray(inputs["L0_attn_W_uq"], np.float32),
        w_kr=np.ascontiguousarray(inputs["L0_attn_W_kr"], np.float32),
        w_qr=np.ascontiguousarray(inputs["L0_attn_W_qr"], np.float32),
        w_out=np.ascontiguousarray(inputs["L0_attn_W_out"], np.float32),
        wq_gate=np.ascontiguousarray(inputs["L0_ffn_wq_gate"], np.int32),
        wq_up=np.ascontiguousarray(inputs["L0_ffn_wq_up"], np.int32),
        wq_down=np.ascontiguousarray(inputs["L0_ffn_wq_down"], np.int32),
        s_gate=np.ascontiguousarray(inputs["L0_ffn_s_gate"], np.float32),
        s_up=np.ascontiguousarray(inputs["L0_ffn_s_up"], np.float32),
        sdb=sdb, ang_hi=ang_hi, ang_lo=ang_lo, angf_hi=angf_hi,
        angf_lo=angf_lo, perm=perm, ident=ident,
    )
    x = np.asarray(inputs["L0_attn_x"], np.float32)
    return [dict(common, x=np.ascontiguousarray(x[i])) for i in range(x.shape[0])]


def kernel(**inputs):
    nc = get_nc(debug=False)
    in_maps = make_in_maps(inputs)
    res = run_bass_kernel_spmd(nc, in_maps, core_ids=list(range(8)))
    return np.stack([res.results[i]["y"] for i in range(8)], axis=0)
